# version 13
# baseline (speedup 1.0000x reference)
import numpy as np
import ml_dtypes
import concourse.bass as bass
import concourse.mybir as mybir
from concourse.bass_utils import run_bass_kernel_spmd

F32 = mybir.dt.float32
BF16 = mybir.dt.bfloat16
AF = mybir.ActivationFunctionType
ALU = mybir.AluOpType

DM = 1024
NCH = DM // 128
FF = 2048
NEXP = 8
H, HS = 16, 64
L = 128
LN_EPS = 1e-5
GN_EPS = 0.00064

CV_TMX, CV_TD, CV_FAAAA, CV_AAAAA, CV_MISCA, CV_MISCK, CV_LNW, CV_LNB = range(8)
CV_TMAA0 = 8
NVC = 12
SC_LNEPS, SC_GNEPS, SC_KNEPS, SC_EPS30, SC_ZERO = range(5)
NSC = 5


def ts(i, n=128):
    return slice(i * n, (i + 1) * n)


class PsumRot:

    def __init__(self, n):
        self.n = n
        self.uses = 0
        self.frees = []

    def acquire(self):
        slot = self.uses % self.n
        guard = (self.frees[self.uses - self.n]
                 if self.uses >= self.n else None)
        self.uses += 1
        return slot, guard

    def release(self, guards):
        self.frees.append(dict(guards))


class B:

    def __init__(self, nc):
        self.nc = nc
        self.q = {k: [] for k in ("tensor", "vector", "scalar", "gpsimd",
                                  "sync")}
        self._sems = {}

    def sem(self, name):
        if name not in self._sems:
            self._sems[name] = [self.nc.semaphore("sem_" + name).__enter__(),
                                0]
        return self._sems[name]

    def n(self, name):
        return self.sem(name)[1]

    def r(self, name, k=1):
        s = self.sem(name)
        s[1] += k
        return s[1]

    def h(self, name):
        return self.sem(name)[0]

    def t(self, fn):
        self.q["tensor"].append(fn)

    def v(self, fn):
        self.q["vector"].append(fn)

    def c(self, fn):
        self.q["scalar"].append(fn)

    def g(self, fn):
        self.q["gpsimd"].append(fn)

    def y(self, fn):
        self.q["sync"].append(fn)

    def barrier(self):
        nc = self.nc
        if not any(self.q.values()):
            return
        with nc.Block() as blk:
            for eng in ("tensor", "vector", "scalar", "gpsimd", "sync"):
                fns = self.q[eng]
                if fns:
                    getattr(blk, eng)(lambda e, fns=fns: [f(e) for f in fns])
        self.q = {k: [] for k in self.q}


def build_program(T=2048, dbg=(), stop_after=None, pers=None, lastskip=None, sph=9):
    NTC = T // 128
    NS = T // 512
    nc = bass.Bass("TRN2", target_bir_lowering=False)
    b = B(nc)
    dbg = set(dbg)
    dbg_outs = {}

    d = {}

    def din(name, shape, dt=F32):
        d[name] = nc.dram_tensor(name, list(shape), dt,
                                 kind="ExternalInput").ap()
        return d[name]

    din("x_fm", [DM, T])
    out_d = nc.dram_tensor("out_fm", [DM, T], F32, kind="ExternalOutput").ap()

    for nm, sh, dt in [
        ("ident_bf", [128, 128], BF16), ("zid_bf", [128, 256], BF16),
        ("ones1_bf", [1, 128], BF16), ("ones1_f32", [1, 128], F32),
        ("onescol_bf", [128, 1], BF16), ("mask_sl", [128, 128], F32),
        ("mask_t", [128, 256], F32), ("mask_sl2", [128, 256], BF16),
        ("mask_t2", [128, 512], BF16), ("ident2x", [128, 64], BF16),
        ("hsel", [16, NCH * 128], F32),
        ("hselt_bf", [128, NCH * 16], BF16), ("chanvec", [DM, NVC], F32),
        ("consts", [128, NSC], F32), ("lnw_tm", [128, DM], F32),
        ("lnb_tm", [128, DM], F32),
        ("wg", [DM, 8], F32), ("ws1", [DM, FF], BF16),
        ("ws3", [DM, FF], BF16),
        ("ws2", [FF, DM], BF16), ("we1", [NEXP, DM, FF], BF16),
        ("we3", [NEXP, DM, FF], BF16), ("we2", [NEXP, FF, DM], BF16),
        ("wrT", [DM, DM], BF16), ("wkT", [DM, DM], BF16),
        ("wvT", [DM, DM], BF16), ("woT", [DM, DM], BF16),
        ("maa_w1T", [DM, 128], F32), ("maa_w2", [4, 32, DM], F32),
        ("decay_w1T", [DM, 64], F32), ("decay_w2", [64, DM], F32),
        ("kkk_w1T", [DM, 16], F32), ("kkk_w2", [16, DM], F32),
        ("aaa_w1T", [DM, 16], F32), ("aaa_w2", [16, DM], F32),
        ("ma_w1T", [DM, 16], F32), ("ma_w2", [16, DM], F32),
        ("mk_w1T", [DM, 16], F32), ("mk_w2", [16, DM], F32),
        ("gate_w1T", [DM, 128], F32), ("gate_w2", [128, DM], F32),
    ]:
        din(nm, sh, dt)

    wdram = nc.dram_tensor("wdram", [NTC * 8, 128], BF16,
                           kind="Internal").ap()

    def sbuf(name, shape, dt=F32):
        return nc.alloc_sbuf_tensor(name, list(shape), dt).ap()

    R = [sbuf(f"R{m}", [128, T]) for m in range(NCH)]
    Xn = [None] * NCH
    ident_bf = sbuf("s_identbf", [128, 128], BF16)
    zid_bf = sbuf("s_zidbf", [128, 256], BF16)
    ones1_bf = sbuf("s_ones1bf", [1, 128], BF16)
    ones1_f32 = sbuf("s_ones1f", [1, 128])
    onescol_bf = sbuf("s_onescolbf", [128, 1], BF16)
    mask_sl = sbuf("s_masksl", [128, 128])
    mask_t = sbuf("s_maskt", [128, 256])
    mask_sl2 = sbuf("s_masksl2", [128, 256], BF16)
    mask_t2 = sbuf("s_maskt2", [128, 512], BF16)
    ident2x = sbuf("s_ident2x", [128, 64], BF16)
    hsel = sbuf("s_hsel", [16, NCH * 128])
    hselt_bf = sbuf("s_hselt", [128, NCH * 16], BF16)
    chanvec = sbuf("s_chanvec", [128, NCH * NVC])
    consts = sbuf("s_consts", [128, NSC])
    lnw_tm = sbuf("s_lnwtm", [128, DM])
    lnb_tm = sbuf("s_lnbtm", [128, DM])

    def cvap(m, col):
        return chanvec[:, m * NVC + col: m * NVC + col + 1]

    def dump(name, ap_list, dt=None):
        if name not in dbg:
            return
        if not isinstance(ap_list, (list, tuple)):
            ap_list = [ap_list]
        rows = sum(a.shape[0] for a in ap_list)
        w = ap_list[0].shape[1]
        dt = dt or ap_list[0].dtype
        od = nc.dram_tensor("dbg_" + name, [rows, w], dt,
                            kind="ExternalOutput").ap()
        dbg_outs[name] = od
        tgt = b.r("dY", 16 * len(ap_list))

        def f(s, ap_list=list(ap_list), tgt=tgt):
            r0 = 0
            for a in ap_list:
                s.dma_start(od[r0:r0 + a.shape[0], :], a[:]
                            ).then_inc(b.h("dY"), 16)
                r0 += a.shape[0]
            s.wait_ge(b.h("dY"), tgt)
        b.y(f)
        b.barrier()

    def f_init(s):
        for sb_ap, nm in [
            (ident_bf, "ident_bf"), (zid_bf, "zid_bf"),
            (ones1_bf, "ones1_bf"), (ones1_f32, "ones1_f32"),
            (onescol_bf, "onescol_bf"), (mask_sl, "mask_sl"),
            (mask_t, "mask_t"), (mask_sl2, "mask_sl2"),
            (mask_t2, "mask_t2"), (ident2x, "ident2x"),
            (hsel, "hsel"), (hselt_bf, "hselt_bf"),
            (consts, "consts"), (lnw_tm, "lnw_tm"), (lnb_tm, "lnb_tm"),
        ]:
            s.dma_start(sb_ap[:], d[nm][:]).then_inc(b.h("dY"), 16)
        s.dma_start(chanvec.rearrange("p (m v) -> p m v", m=NCH),
                    d["chanvec"].rearrange("(m p) v -> p m v", p=128)
                    ).then_inc(b.h("dY"), 16)
        for m in range(NCH):
            s.dma_start(R[m][:], d["x_fm"][ts(m), :]).then_inc(b.h("dY"), 16)
        s.wait_ge(b.h("dY"), 16 * (16 + NCH))
    b.r("dY", 16 * (16 + NCH))

    b.y(f_init)
    b.barrier()

    _ln_ctr = [0]

    def emit_ln(out_bf_list, sq_bf, rb_bf, apl_f32, lnrow):
        _ln_ctr[0] += 1
        _lt = _ln_ctr[0]
        with (
            nc.psum_tensor(f"ln_mean{_lt}", [1, T], F32) as mean_ps,
            nc.psum_tensor(f"ln_sq{_lt}", [1, T], F32) as sq_ps,
        ):
            dY0 = b.n("dY")
            pe_prev = 0
            for m in range(NCH):
                cn = b.r("sc")

                def fc(cc, m=m, dY0=dY0, pe_prev=pe_prev):
                    if m == 0:
                        cc.wait_ge(b.h("dY"), dY0)
                    if pe_prev:
                        cc.wait_ge(b.h("pe"), pe_prev)
                    cc.activation(sq_bf[:, :T], R[m][:, :T],
                                  AF.Square).then_inc(b.h("sc"), 1)
                b.c(fc)
                vn = b.r("v")

                def fv(vv, m=m, dY0=dY0, pe_prev=pe_prev):
                    if m == 0:
                        vv.wait_ge(b.h("dY"), dY0)
                    if pe_prev:
                        vv.wait_ge(b.h("pe"), pe_prev)
                    vv.tensor_copy(rb_bf[:, :T],
                                   R[m][:, :T]).then_inc(b.h("v"), 1)
                b.v(fv)
                pe_m = b.r("pe")

                def ft(tt, m=m, vn=vn, cn=cn):
                    tt.wait_ge(b.h("v"), vn)
                    tt.wait_ge(b.h("sc"), cn)
                    for nsI in range(NS):
                        tt.matmul(mean_ps[:, ts(nsI, 512)], onescol_bf[:],
                                  rb_bf[:, ts(nsI, 512)], start=(m == 0),
                                  stop=(m == NCH - 1))
                        ins = tt.matmul(sq_ps[:, ts(nsI, 512)], onescol_bf[:],
                                        sq_bf[:, ts(nsI, 512)],
                                        start=(m == 0), stop=(m == NCH - 1))
                    ins.then_inc(b.h("pe"), 1)
                b.t(ft)
                pe_prev = pe_m
            pe_stats = b.n("pe")

            mu = lnrow[:, 0 * T:1 * T]
            var = lnrow[:, 1 * T:2 * T]
            srow = var
            trow = mu
            v1 = b.r("v"); v2 = b.r("v"); v3 = b.r("v")

            def frow_v(vv, pe_stats=pe_stats, v1=v1, v2=v2):
                vv.wait_ge(b.h("pe"), pe_stats)
                vv.tensor_scalar_mul(mu, mean_ps[:1, :T],
                                     1.0 / DM).then_inc(b.h("v"), 1)
                vv.wait_ge(b.h("v"), v1)
                vv.tensor_mul(var, mu, mu).then_inc(b.h("v"), 1)
                vv.wait_ge(b.h("v"), v2)
                vv.scalar_tensor_tensor(var, sq_ps[:1, :T], 1.0 / DM, var,
                                        ALU.mult,
                                        ALU.subtract).then_inc(b.h("v"), 1)
            b.v(frow_v)
            c1 = b.r("sc", 2)

            def frow_c(cc, v3=v3, c1=c1):
                cc.wait_ge(b.h("v"), v3)
                cc.activation(srow, var, AF.Ln,
                              bias=consts[:1, SC_LNEPS:SC_LNEPS + 1]
                              ).then_inc(b.h("sc"), 1)
                cc.wait_ge(b.h("sc"), c1 - 1)
                cc.activation(srow, srow, AF.Exp,
                              scale=-0.5).then_inc(b.h("sc"), 1)
            b.c(frow_c)
            v5 = b.r("v")

            def frow_v2(vv, c1=c1):
                vv.wait_ge(b.h("sc"), c1)
                vv.scalar_tensor_tensor(trow, mu, -1.0, srow, ALU.mult,
                                        ALU.mult).then_inc(b.h("v"), 1)
            b.v(frow_v2)
            b.barrier()

        with (
            nc.psum_tensor(f"ln_sb{_lt}", [128, T], F32) as sb_ps,
            nc.psum_tensor(f"ln_tb{_lt}", [128, T], F32) as tb_ps,
        ):
            pe_bc = b.r("pe")

            def fbc(tt, pe_bc=pe_bc):
                for nsI in range(NS):
                    tt.matmul(sb_ps[:, ts(nsI, 512)], ones1_f32[:],
                              srow[:1, ts(nsI, 512)], start=True, stop=True)
                    ins = tt.matmul(tb_ps[:, ts(nsI, 512)], ones1_f32[:],
                                    trow[:1, ts(nsI, 512)], start=True,
                                    stop=True)
                ins.then_inc(b.h("pe"), 1)
            b.t(fbc)

            for m in range(NCH):
                vprev = b.n("v")
                va = b.r("v")
                vb = b.r("v")

                def fap(vv, m=m, va=va, pe_bc=pe_bc, vprev=vprev):
                    if m == 0:
                        vv.wait_ge(b.h("pe"), pe_bc)
                    else:
                        vv.wait_ge(b.h("v"), vprev)
                    vv.tensor_tensor(apl_f32[:, :T], R[m][:, :T],
                                     sb_ps[:, :T],
                                     ALU.mult).then_inc(b.h("v"), 1)
                    vv.wait_ge(b.h("v"), va)
                    vv.tensor_tensor(out_bf_list[m][:, :T], apl_f32[:, :T],
                                     tb_ps[:, :T],
                                     ALU.add).then_inc(b.h("v"), 1)
                b.v(fap)
            b.barrier()

    def emit_gating(callno, gtile, gscr, gm12, wtile_bf, wT_bf, wg_bf):
        with (
            nc.psum_tensor(f"g_ps{callno}", [128, 8], F32) as gps,
            nc.psum_tensor(f"g_tp{callno}", [128, 128], BF16) as gtp,
        ):
            dgw = b.r("dG", 16)

            def fwg(g, dgw=dgw):
                g.dma_start(wg_bf.rearrange("p (kc e) -> p kc e", kc=NCH),
                            d["wg"].rearrange("(kc p) e -> p kc e", p=128)
                            ).then_inc(b.h("dG"), 16)
                g.wait_ge(b.h("dG"), dgw)
            b.g(fwg)
            for tc in range(NTC):
                pen = b.r("pe")

                def ft(tt, tc=tc, pen=pen, dgw=dgw):
                    if tc == 0:
                        tt.wait_ge(b.h("dG"), dgw)
                    for kc in range(NCH):
                        ins = tt.matmul(gps[:, :8], Xn[kc][:, ts(tc)],
                                        wg_bf[:, kc * 8:(kc + 1) * 8],
                                        start=(kc == 0), stop=(kc == NCH - 1))
                    ins.then_inc(b.h("pe"), 1)
                b.t(ft)
                cn = b.r("sc")

                def fc(cc, tc=tc, pen=pen):
                    cc.wait_ge(b.h("pe"), pen)
                    cc.activation(gtile[:, tc * 8:(tc + 1) * 8], gps[:, :8],
                                  AF.Sigmoid).then_inc(b.h("sc"), 1)
                b.c(fc)

                def ftw(tt, cn=cn):
                    tt.wait_ge(b.h("sc"), cn)
                b.t(ftw)

            g3v = gtile[:, :NTC * 8].rearrange("p (a e) -> p a e", a=NTC)
            m1 = gm12[:, 0 * NTC:1 * NTC]
            m2 = gm12[:, 1 * NTC:2 * NTC]
            winv = gm12[:, 2 * NTC:3 * NTC]
            mask1 = gscr[:, 0 * NTC * 8:1 * NTC * 8]
            gmsk = gscr[:, 1 * NTC * 8:2 * NTC * 8]
            wraw = gscr[:, 2 * NTC * 8:3 * NTC * 8]
            sc_all = b.n("sc")
            vv0 = b.n("v")
            vtop = [b.r("v") for _ in range(9)]

            def ftop(vv, sc_all=sc_all, vv0=vv0):
                vv.wait_ge(b.h("sc"), sc_all)

                def r3(ap_):
                    return ap_.rearrange("p (a e) -> p a e", a=NTC)

                def r1(ap_):
                    return ap_.rearrange("p (a e) -> p a e", a=NTC, e=1)
                m1b = r1(m1).broadcast_to((128, NTC, 8))
                m2b = r1(m2).broadcast_to((128, NTC, 8))
                wib = r1(winv).broadcast_to((128, NTC, 8))
                vv.reduce_max(r1(m1), g3v,
                              axis=mybir.AxisListType.X).then_inc(b.h("v"), 1)
                vv.wait_ge(b.h("v"), vv0 + 1)
                vv.tensor_tensor(r3(mask1), g3v, m1b,
                                 ALU.is_ge).then_inc(b.h("v"), 1)
                vv.wait_ge(b.h("v"), vv0 + 2)
                vv.scalar_tensor_tensor(r3(gmsk), r3(mask1), -1e30, g3v,
                                        ALU.mult,
                                        ALU.add).then_inc(b.h("v"), 1)
                vv.wait_ge(b.h("v"), vv0 + 3)
                vv.reduce_max(r1(m2), r3(gmsk),
                              axis=mybir.AxisListType.X).then_inc(b.h("v"), 1)
                vv.wait_ge(b.h("v"), vv0 + 4)
                vv.tensor_tensor(winv, m1, m2, ALU.add).then_inc(b.h("v"), 1)
                vv.wait_ge(b.h("v"), vv0 + 5)
                vv.reciprocal(winv, winv).then_inc(b.h("v"), 1)
                vv.wait_ge(b.h("v"), vv0 + 6)
                vv.tensor_tensor(r3(wraw), g3v, m2b,
                                 ALU.is_ge).then_inc(b.h("v"), 1)
                vv.wait_ge(b.h("v"), vv0 + 7)
                vv.tensor_tensor(r3(wraw), r3(wraw), g3v,
                                 ALU.mult).then_inc(b.h("v"), 1)
                vv.wait_ge(b.h("v"), vv0 + 8)
                vv.tensor_tensor(r3(wtile_bf), r3(wraw), wib,
                                 ALU.mult).then_inc(b.h("v"), 1)
            b.v(ftop)
            v_top = vtop[-1]
            pe_tr = b.r("pe")

            def ftr(tt, v_top=v_top, pe_tr=pe_tr):
                tt.wait_ge(b.h("v"), v_top)
                tt.transpose(gtp[:NTC * 8, :], wtile_bf[:, :NTC * 8],
                             ident_bf[:]).then_inc(b.h("pe"), 1)
            b.t(ftr)
            v_wT = b.r("v")

            def fev(vv, pe_tr=pe_tr):
                vv.wait_ge(b.h("pe"), pe_tr)
                vv.tensor_copy(wT_bf[:NTC * 8, :],
                               gtp[:NTC * 8, :]).then_inc(b.h("v"), 1)
            b.v(fev)
            dg1 = b.r("dGt", 16)

            def fdma(g, v_wT=v_wT, dg1=dg1):
                g.wait_ge(b.h("v"), v_wT)
                g.dma_start(wdram[:, :],
                            wT_bf[:NTC * 8, :]).then_inc(b.h("dGt"), 16)
                g.wait_ge(b.h("dGt"), dg1)
            b.g(fdma)
            b.barrier()

    def emit_ffn(callno):
        NTQ = T // 512
        from contextlib import ExitStack
        with ExitStack() as _est:
            w_row = _est.enter_context(nc.sbuf_tensor(f"f{callno}_wrow", [1, T], BF16))
            w_bc = _est.enter_context(nc.sbuf_tensor(f"f{callno}_wbc", [128, T], BF16))
            w1a = _est.enter_context(nc.sbuf_tensor(f"f{callno}_w1a", [128, 8 * 512], BF16))
            w3a = _est.enter_context(nc.sbuf_tensor(f"f{callno}_w3a", [128, 8 * 512], BF16))
            w1b = _est.enter_context(nc.sbuf_tensor(f"f{callno}_w1b", [128, 8 * 512], BF16))
            w3b = _est.enter_context(nc.sbuf_tensor(f"f{callno}_w3b", [128, 8 * 512], BF16))
            w2s = _est.enter_context(nc.sbuf_tensor(f"f{callno}_w2", [128, 8 * DM], BF16))
            hbuf = _est.enter_context(nc.sbuf_tensor(f"f{callno}_h", [128, 8 * T], BF16))
            sibuf = _est.enter_context(nc.sbuf_tensor(f"f{callno}_si", [128, 2 * 512], BF16))
            g3buf = _est.enter_context(nc.sbuf_tensor(f"f{callno}_g3", [128, 2 * 512], BF16))
            tmbuf = _est.enter_context(nc.sbuf_tensor(f"f{callno}_tm", [128, 2 * 512], BF16))
            for m in range(NCH):
                Xn[m] = _est.enter_context(
                    nc.sbuf_tensor(f"f{callno}_Xn{m}", [128, T], BF16)).ap() \
                    if False else _est.enter_context(
                    nc.sbuf_tensor(f"f{callno}_Xn{m}x", [128, T], BF16))
            gtile = hbuf[:, 0:2 * NTC * 8].bitcast(F32)
            gscr = hbuf[:, 2 * NTC * 8:8 * NTC * 8].bitcast(F32)
            gm12 = hbuf[:, 8 * NTC * 8:8 * NTC * 8 + 6 * NTC].bitcast(F32)
            _go = 8 * NTC * 8 + 6 * NTC
            wtile_bf = hbuf[:, _go:_go + NTC * 8]
            wT_bf = hbuf[:, _go + NTC * 8:_go + NTC * 8 + 128]
            wg_bf = hbuf[:, _go + NTC * 8 + 128:_go + NTC * 8 + 128 + 64]
            sq_bf = hbuf[:, 0:T]
            rb_bf = hbuf[:, T:2 * T]
            apl_f32 = hbuf[:, 2 * T:4 * T].bitcast(F32)
            lnrow = hbuf[0:1, 4 * T:8 * T].bitcast(F32)
            emit_ln(Xn, sq_bf, rb_bf, apl_f32, lnrow)
            emit_gating(callno, gtile, gscr, gm12, wtile_bf, wT_bf, wg_bf)
            pg = _est.enter_context(nc.psum_tensor(f"f{callno}_pg",
                                                   [128, 4 * 512], F32))
            po = _est.enter_context(nc.psum_tensor(f"f{callno}_po",
                                                   [128, 2 * 512], F32))
            dgt_n = b.n("dGt")
            wrow_free_pe = 0
            w13 = [(w1a, w3a), (w1b, w3b)]
            experts = [-1] + list(range(NEXP))
            units = [(e, hf) for e in experts for hf in (0, 1)]
            wsem = ["dWa", "dWb"]

            w13_loaded = {}
            w2_loaded = {}
            grot = PsumRot(2)
            orot = PsumRot(2)
            q_done_pe = {}
            u_done_pe = {}
            wbc_evac = {}
            hready = {}
            pln_hist = []
            gq = 0

            def dma_w13(g, e, qf, slot):
                src1 = d["ws1"] if e < 0 else d["we1"][e]
                src3 = d["ws3"] if e < 0 else d["we3"][e]
                for wsb, src in ((w13[slot][0], src1), (w13[slot][1], src3)):
                    g.dma_start(
                        wsb.rearrange("p (kc f) -> p kc f", kc=8),
                        src[:, ts(qf, 512)].rearrange("(kc p) f -> p kc f",
                                                      p=128)
                    ).then_inc(b.h(wsem[slot]), 16)

            def dma_w2(g, e, hf):
                src2 = d["ws2"] if e < 0 else d["we2"][e]
                g.dma_start(
                    w2s.rearrange("p (kc dd) -> p kc dd", kc=8),
                    src2[ts(hf, 1024), :].rearrange("(kc p) dd -> p kc dd",
                                                    p=128)
                ).then_inc(b.h("dW2"), 16)

            w13_loaded[0] = b.r("dWa", 32)
            w13_loaded[1] = b.r("dWb", 32)
            w2_loaded[0] = b.r("dW2", 16)

            def fpro(g):
                dma_w13(g, units[0][0], 0, 0)
                dma_w13(g, units[0][0], 1, 1)
                dma_w2(g, units[0][0], 0)
            b.g(fpro)

            def wait_guard(tt, guard):
                if guard:
                    for snm, cnt in guard.items():
                        tt.wait_ge(b.h(snm), cnt)

            for ui, (e, hf) in enumerate(units):
                if e >= 0 and hf == 0:
                    drow = b.r("dGt", 16)

                    def frow(g, e=e, drow=drow, dgt_n=dgt_n,
                             wrow_free_pe=wrow_free_pe):
                        g.wait_ge(b.h("dGt"), dgt_n)
                        if wrow_free_pe:
                            g.wait_ge(b.h("pe"), wrow_free_pe)
                        srcw = bass.AP(wdram.tensor, e * 128,
                                       [[1, 1], [8 * 128, NTC], [1, 128]])
                        g.dma_start(w_row[:1, :].rearrange(
                            "p (a n) -> p a n", a=NTC),
                            srcw).then_inc(b.h("dGt"), 16)
                    b.g(frow)
                    for nsI in range(NS):
                        oslot, guard = orot.acquire()
                        pen = b.r("pe")

                        def fbc(tt, nsI=nsI, oslot=oslot, guard=guard,
                                drow=drow):
                            wait_guard(tt, guard)
                            tt.wait_ge(b.h("dGt"), drow)
                            tt.matmul(po[:, ts(oslot, 512)], ones1_bf[:],
                                      w_row[:1, nsI * 512:(nsI + 1) * 512],
                                      start=True,
                                      stop=True).then_inc(b.h("pe"), 1)
                        b.t(fbc)
                        vn = b.r("v")

                        def fbe(vv, nsI=nsI, oslot=oslot, pen=pen):
                            vv.wait_ge(b.h("pe"), pen)
                            vv.tensor_copy(w_bc[:, ts(nsI, 512)],
                                           po[:, ts(oslot, 512)]
                                           ).then_inc(b.h("v"), 1)
                        b.v(fbe)
                        orot.release({"v": vn})
                    wrow_free_pe = b.n("pe")
                    wbc_evac[e] = b.n("v")

                for q in (0, 1):
                    qf = hf * 2 + q
                    slot = gq % 2
                    myload = w13_loaded[gq]
                    w1sb, w3sb = w13[slot]
                    for fc in range(4):
                        for tq in range(NTQ):
                            gslot, guard = grot.acquire()
                            s1 = gslot * 2
                            s3 = s1 + 1
                            pen = b.r("pe")

                            def fg(tt, fc=fc, tq=tq, s1=s1, s3=s3,
                                   guard=guard, myload=myload, w1sb=w1sb,
                                   w3sb=w3sb, slot=slot):
                                tt.wait_ge(b.h(wsem[slot]), myload)
                                wait_guard(tt, guard)
                                for kc in range(NCH):
                                    tt.matmul(
                                        pg[:, ts(s1, 512)],
                                        w1sb[:, kc * 512 + fc * 128:
                                             kc * 512 + (fc + 1) * 128],
                                        Xn[kc][:, ts(tq, 512)],
                                        start=(kc == 0), stop=(kc == NCH - 1))
                                for kc in range(NCH):
                                    ins = tt.matmul(
                                        pg[:, ts(s3, 512)],
                                        w3sb[:, kc * 512 + fc * 128:
                                             kc * 512 + (fc + 1) * 128],
                                        Xn[kc][:, ts(tq, 512)],
                                        start=(kc == 0), stop=(kc == NCH - 1))
                                ins.then_inc(b.h("pe"), 1)
                            b.t(fg)

                            sslot = gslot
                            scr_guard = (pln_hist[-2] if len(pln_hist) >= 2
                                         else 0)
                            cn = b.r("sc")

                            def fsi(cc, s1=s1, sslot=sslot, pen=pen,
                                    scr_guard=scr_guard):
                                cc.wait_ge(b.h("pe"), pen)
                                if scr_guard:
                                    cc.wait_ge(b.h("pl"), scr_guard)
                                cc.activation(sibuf[:, ts(sslot, 512)],
                                              pg[:, ts(s1, 512)],
                                              AF.Silu).then_inc(b.h("sc"), 1)
                            b.c(fsi)
                            vg3 = b.r("v")

                            def fg3(vv, s3=s3, sslot=sslot, pen=pen,
                                    scr_guard=scr_guard):
                                vv.wait_ge(b.h("pe"), pen)
                                if scr_guard:
                                    vv.wait_ge(b.h("pl"), scr_guard)
                                vv.tensor_scalar_mul(
                                    g3buf[:, ts(sslot, 512)],
                                    pg[:, ts(s3, 512)],
                                    1.0 / 6.0).then_inc(b.h("v"), 1)
                            b.v(fg3)
                            grot.release({"v": vg3, "sc": cn})

                            hfc = q * 4 + fc
                            hslice = hbuf[:, hfc * T + tq * 512:
                                          hfc * T + (tq + 1) * 512]
                            prev_u = u_done_pe.get(ui - 1, 0)
                            pln = b.r("pl")
                            if e < 0:
                                def fmul(pp, hslice=hslice, sslot=sslot,
                                         vg3=vg3, cn=cn, prev_u=prev_u):
                                    pp.wait_ge(b.h("v"), vg3)
                                    pp.wait_ge(b.h("sc"), cn)
                                    if prev_u:
                                        pp.wait_ge(b.h("pe"), prev_u)
                                    pp.tensor_tensor(
                                        hslice, sibuf[:, ts(sslot, 512)],
                                        g3buf[:, ts(sslot, 512)],
                                        ALU.mult).then_inc(b.h("pl"), 1)
                                b.g(fmul)
                                hready[(qf, fc, tq)] = ("pl", pln)
                            else:
                                def fmul(pp, sslot=sslot, vg3=vg3, cn=cn):
                                    pp.wait_ge(b.h("v"), vg3)
                                    pp.wait_ge(b.h("sc"), cn)
                                    pp.tensor_tensor(
                                        tmbuf[:, ts(sslot, 512)],
                                        sibuf[:, ts(sslot, 512)],
                                        g3buf[:, ts(sslot, 512)],
                                        ALU.mult).then_inc(b.h("pl"), 1)
                                b.g(fmul)
                                wbcn = wbc_evac[e]
                                vw = b.r("v")

                                def fwm(vv, hslice=hslice, sslot=sslot,
                                        tq=tq, pln=pln, wbcn=wbcn,
                                        prev_u=prev_u):
                                    vv.wait_ge(b.h("pl"), pln)
                                    vv.wait_ge(b.h("v"), wbcn)
                                    if prev_u:
                                        vv.wait_ge(b.h("pe"), prev_u)
                                    vv.tensor_tensor(
                                        hslice, tmbuf[:, ts(sslot, 512)],
                                        w_bc[:, ts(tq, 512)],
                                        ALU.mult).then_inc(b.h("v"), 1)
                                b.v(fwm)
                                hready[(qf, fc, tq)] = ("v", vw)
                            pln_hist.append(pln)
                    q_done_pe[gq] = b.n("pe")
                    nxt = gq + 2
                    if nxt < len(units) * 2:
                        ne, nhf = units[nxt // 2]
                        nqf = nhf * 2 + (nxt % 2)
                        mydone = q_done_pe[gq]
                        w13_loaded[nxt] = b.r(wsem[slot], 32)

                        def fpre(g, ne=ne, nqf=nqf, slot=slot,
                                 mydone=mydone):
                            g.wait_ge(b.h("pe"), mydone)
                            dma_w13(g, ne, nqf, slot)
                        b.g(fpre)
                    gq += 1

                myw2 = w2_loaded[ui]
                for dc in range(NCH):
                    for tq in range(NTQ):
                        oslot, guard = orot.acquire()
                        hw = [hready[(hf * 2 + qq, fc, tq)]
                              for qq in (0, 1) for fc in range(4)]
                        vmax = max([n for k, n in hw if k == "v"], default=0)
                        plmax = max([n for k, n in hw if k == "pl"],
                                    default=0)
                        pen = b.r("pe")

                        def fo(tt, dc=dc, tq=tq, oslot=oslot, guard=guard,
                               myw2=myw2, vmax=vmax, plmax=plmax):
                            tt.wait_ge(b.h("dW2"), myw2)
                            if vmax:
                                tt.wait_ge(b.h("v"), vmax)
                            if plmax:
                                tt.wait_ge(b.h("pl"), plmax)
                            wait_guard(tt, guard)
                            for kc in range(8):
                                ins = tt.matmul(
                                    po[:, ts(oslot, 512)],
                                    w2s[:, kc * DM + dc * 128:
                                        kc * DM + (dc + 1) * 128],
                                    hbuf[:, kc * T + tq * 512:
                                         kc * T + (tq + 1) * 512],
                                    start=(kc == 0), stop=(kc == 7))
                            ins.then_inc(b.h("pe"), 1)
                        b.t(fo)
                        vn = b.r("v")

                        def fadd(vv, dc=dc, tq=tq, oslot=oslot, pen=pen):
                            vv.wait_ge(b.h("pe"), pen)
                            vv.tensor_tensor(R[dc][:, ts(tq, 512)],
                                             R[dc][:, ts(tq, 512)],
                                             po[:, ts(oslot, 512)],
                                             ALU.add).then_inc(b.h("v"), 1)
                        b.v(fadd)
                        orot.release({"v": vn})
                u_done_pe[ui] = b.n("pe")
                if ui + 1 < len(units):
                    ne, nhf = units[ui + 1]
                    mydone = u_done_pe[ui]
                    w2_loaded[ui + 1] = b.r("dW2", 16)

                    def fw2(g, ne=ne, nhf=nhf, mydone=mydone):
                        g.wait_ge(b.h("pe"), mydone)
                        dma_w2(g, ne, nhf)
                    b.g(fw2)
            _wa, _wb, _w2, _dgt = (b.n("dWa"), b.n("dWb"), b.n("dW2"),
                                   b.n("dGt"))

            def fdrain2(g, _wa=_wa, _wb=_wb, _w2=_w2, _dgt=_dgt):
                g.wait_ge(b.h("dWa"), _wa)
                g.wait_ge(b.h("dWb"), _wb)
                g.wait_ge(b.h("dW2"), _w2)
                g.wait_ge(b.h("dGt"), _dgt)
            b.g(fdrain2)
            b.barrier()

    def emit_attn():
        W = 128
        NBK = T // W

        def vop(waits, fn):
            n = b.r("v")

            def f(vv, waits=dict(waits), fn=fn):
                for s, c in waits.items():
                    if c:
                        vv.wait_ge(b.h(s), c)
                fn(vv).then_inc(b.h("v"), 1)
            b.v(f)
            return n

        def cop(waits, fn):
            n = b.r("sc")

            def f(cc, waits=dict(waits), fn=fn):
                for s, c in waits.items():
                    if c:
                        cc.wait_ge(b.h(s), c)
                fn(cc).then_inc(b.h("sc"), 1)
            b.c(f)
            return n

        def pop(waits, fn):
            n = b.r("pl")

            def f(pp, waits=dict(waits), fn=fn):
                for s, c in waits.items():
                    if c:
                        pp.wait_ge(b.h(s), c)
                fn(pp).then_inc(b.h("pl"), 1)
            b.g(f)
            return n

        def top(waits, fn):
            n = b.r("pe")

            def f(tt, waits=dict(waits), fn=fn):
                for s, c in waits.items():
                    if c:
                        tt.wait_ge(b.h(s), c)
                fn(tt).then_inc(b.h("pe"), 1)
            b.t(f)
            return n

        from contextlib import ExitStack
        with ExitStack() as _est:
            wsA = _est.enter_context(nc.sbuf_tensor("a_wsA", [128, 8 * 512], BF16))
            wsB = _est.enter_context(nc.sbuf_tensor("a_wsB", [128, 8 * 512], BF16))
            maa1 = _est.enter_context(nc.sbuf_tensor("a_maa1", [128, 8 * 128], BF16))
            maa2a = _est.enter_context(nc.sbuf_tensor("a_maa2a", [64, DM], BF16))
            maa2b = _est.enter_context(nc.sbuf_tensor("a_maa2b", [64, DM], BF16))
            dw1 = _est.enter_context(nc.sbuf_tensor("a_dw1", [128, 8 * 64], BF16))
            dw2 = _est.enter_context(nc.sbuf_tensor("a_dw2", [64, DM], BF16))
            bw1 = _est.enter_context(nc.sbuf_tensor("a_bw1", [128, 8 * 64], BF16))
            bw2ka = _est.enter_context(nc.sbuf_tensor("a_bw2ka", [64, DM], BF16))
            bw2mm = _est.enter_context(nc.sbuf_tensor("a_bw2mm", [64, DM], BF16))
            gw1 = _est.enter_context(nc.sbuf_tensor("a_gw1", [128, 8 * 128], BF16))
            gw2 = _est.enter_context(nc.sbuf_tensor("a_gw2", [128, DM], BF16))
            xnb = _est.enter_context(nc.sbuf_tensor("a_xnb", [128, 8 * W], BF16))
            rT = _est.enter_context(nc.sbuf_tensor("a_rT", [128, 8 * W], BF16))
            kraw = _est.enter_context(nc.sbuf_tensor("a_kraw", [128, 8 * W], BF16))
            gTt = _est.enter_context(nc.sbuf_tensor("a_gT", [128, 8 * W], BF16))
            alpha = _est.enter_context(nc.sbuf_tensor("a_alpha", [128, 8 * W], BF16))
            kkn = _est.enter_context(nc.sbuf_tensor("a_kkn", [128, 8 * W], BF16))
            kf = _est.enter_context(nc.sbuf_tensor("a_kf", [128, 8 * W], BF16))
            wdec = _est.enter_context(nc.sbuf_tensor("a_wdec", [128, 8 * W], BF16))
            dinc = _est.enter_context(nc.sbuf_tensor("a_dinc", [128, 8 * W], BF16))
            sqb2 = dinc
            dinv = _est.enter_context(nc.sbuf_tensor("a_dinv", [128, 8 * W], BF16))
            Qbt = _est.enter_context(nc.sbuf_tensor("a_Qb", [128, 8 * W], BF16))
            Abt = _est.enter_context(nc.sbuf_tensor("a_Ab", [128, 8 * W], BF16))
            Btt = _est.enter_context(nc.sbuf_tensor("a_Bt", [128, 8 * W], BF16))
            Ktt = _est.enter_context(nc.sbuf_tensor("a_Kt", [128, 8 * W], BF16))
            sbA = _est.enter_context(nc.sbuf_tensor("a_sbA", [128, 8 * W], BF16))
            losb = sbA[0:64, 0:256]
            gt1 = sbA[:, 256:384]
            dt1 = sbA[0:64, 384:512]
            kbt = sbA[0:16, 512:640]
            abt = sbA[0:48, 640:768]
            mabt = sbA[0:16, 768:896]
            mkbt = sbA[0:48, 896:1024]
            sbB = _est.enter_context(nc.sbuf_tensor("a_sbB", [128, 8 * W], BF16))
            xx = _est.enter_context(nc.sbuf_tensor("a_xx", [128, 8 * W], F32))
            lwt = _est.enter_context(nc.sbuf_tensor("a_lw", [128, 8 * W], F32))
            Lct = _est.enter_context(nc.sbuf_tensor("a_Lc", [128, 8 * W], F32))
            Bhtm = Lct.bitcast(BF16)[:, 0:DM]
            Khtm = Lct.bitcast(BF16)[:, DM:2 * DM]
            scrA = _est.enter_context(nc.sbuf_tensor("a_scrA", [128, 8 * W], F32))
            ln2r = scrA[0:1, 0:512]
            scrB = _est.enter_context(nc.sbuf_tensor("a_scrB", [128, 8 * W], F32))
            rnsb = scrB[0:16, 0:128]
            rksb = scrB.bitcast(BF16)[0:16, 256:384]
            rktm = scrB.bitcast(BF16)[:, 384:400]
            m16 = scrB[:, 256:352]
            vtm = _est.enter_context(nc.sbuf_tensor("a_vtm", [128, DM], BF16))
            Atm = _est.enter_context(nc.sbuf_tensor("a_Atm", [128, DM], BF16))
            xmr = Atm
            ytm = _est.enter_context(nc.sbuf_tensor("a_ytm", [128, DM], BF16))
            mag = ytm
            xmw = ytm

            gtm = _est.enter_context(nc.sbuf_tensor("a_gtm", [128, DM], BF16))
            mkg = gtm
            xmk = gtm

            ygfm = _est.enter_context(nc.sbuf_tensor("a_ygfm", [128, 8 * W], BF16))
            rcp2 = ygfm
            xmv = ygfm
            STf = _est.enter_context(nc.sbuf_tensor("a_STf", [128, H * HS // 2], F32))
            STb = _est.enter_context(nc.sbuf_tensor("a_STb", [128, H * HS // 2], BF16))
            NILV = 8
            NsbS = [_est.enter_context(nc.sbuf_tensor(f"a_Nsb{i}",
                    [128, 128], BF16)) for i in range(NILV)]
            nt3tS = [_est.enter_context(nc.sbuf_tensor(f"a_nt3t{i}",
                     [128, 256], BF16)) for i in range(NILV)]
            t2t4S = [_est.enter_context(nc.sbuf_tensor(f"a_t2t4{i}",
                     [128, 256], BF16)) for i in range(NILV)]
            Pb0S = [_est.enter_context(nc.sbuf_tensor(f"a_Pb0{i}",
                    [128, 128], BF16)) for i in range(NILV)]
            Pb1S = [_est.enter_context(nc.sbuf_tensor(f"a_Pb1{i}",
                    [128, 128], BF16)) for i in range(NILV)]
            PTtS = [_est.enter_context(nc.sbuf_tensor(f"a_PTt{i}",
                    [128, 6 * 128], BF16)) for i in range(NILV)]
            Xc0S = [_est.enter_context(nc.sbuf_tensor(f"a_Xc0{i}",
                    [128, 64], BF16)) for i in range(NILV)]
            Xc1S = [_est.enter_context(nc.sbuf_tensor(f"a_Xc1{i}",
                    [128, 64], BF16)) for i in range(NILV)]
            ddlS = [_est.enter_context(nc.sbuf_tensor(f"a_ddl{i}",
                    [128, 64], BF16)) for i in range(NILV)]
            slot_free_pe = {}
            carry = _est.enter_context(nc.sbuf_tensor("a_carry", [128, 8], F32))
            basec = _est.enter_context(nc.sbuf_tensor("a_base", [128, 8], F32))
            dlastf = _est.enter_context(nc.sbuf_tensor("a_dlastf", [128, 8], F32))
            bnd = _est.enter_context(nc.sbuf_tensor("a_bnd", [128, 8], BF16))
            P0 = _est.enter_context(nc.psum_tensor("a_P0", [128, 1024], F32))
            P1 = _est.enter_context(nc.psum_tensor("a_P1", [128, 1024], F32))
            P2 = _est.enter_context(nc.psum_tensor("a_P2", [128, 1024], F32))
            PTps = _est.enter_context(nc.psum_tensor("a_PT", [128, 2048], BF16))
            Pt = [P0, P1, P2]
            prot = PsumRot(3)
            ptrot = PsumRot(2)

            def acq():
                s, g = prot.acquire()
                return Pt[s], dict(g or {})

            def acqT():
                s, g = ptrot.acquire()
                return PTps[:, s * 1024:(s + 1) * 1024], dict(g or {})

            def v3(ap_):
                return ap_.rearrange("p (m w) -> p m w", m=8)

            def cv3(col):
                apx = chanvec[:, col::NVC]
                return apx.rearrange("p (m o) -> p m o", o=1).broadcast_to(
                    (128, 8, W))

            z128 = consts[:, SC_ZERO:SC_ZERO + 1].broadcast_to((128, 128))
            vz = vop({}, lambda vv: vv.memset(STf[:], 0.0))
            vz = vop({"v": vz}, lambda vv: vv.memset(STb[:], 0.0))
            vz = vop({"v": vz}, lambda vv: vv.memset(carry[:], 0.0))
            v_bn_prev = vop({"v": vz}, lambda vv: vv.memset(bnd[:], 0.0))

            dS_done = b.r("dS", 16 * 17)

            def fsm(g):
                def ld(dst, nm):
                    g.dma_start(dst, d[nm]).then_inc(b.h("dS"), 16)
                g.dma_start(maa1.rearrange("p (kc x) -> p kc x", kc=8),
                            d["maa_w1T"].rearrange("(kc p) x -> p kc x",
                                                   p=128)
                            ).then_inc(b.h("dS"), 16)
                g.dma_start(dw1.rearrange("p (kc x) -> p kc x", kc=8),
                            d["decay_w1T"].rearrange("(kc p) x -> p kc x",
                                                     p=128)
                            ).then_inc(b.h("dS"), 16)
                g.dma_start(gw1.rearrange("p (kc x) -> p kc x", kc=8),
                            d["gate_w1T"].rearrange("(kc p) x -> p kc x",
                                                    p=128)
                            ).then_inc(b.h("dS"), 16)
                for i, nm in enumerate(("kkk_w1T", "aaa_w1T", "ma_w1T",
                                        "mk_w1T")):
                    g.dma_start(
                        bw1.rearrange("p (kc x) -> p kc x", kc=8
                                      )[:, :, i * 16:(i + 1) * 16],
                        d[nm].rearrange("(kc p) x -> p kc x", p=128)
                    ).then_inc(b.h("dS"), 16)
                ld(gw2[:], "gate_w2")
                ld(dw2[:], "decay_w2")
                ld(bw2ka[0:16, :], "kkk_w2")
                ld(bw2ka[32:48, :], "aaa_w2")
                ld(bw2mm[0:16, :], "ma_w2")
                ld(bw2mm[32:48, :], "mk_w2")
                g.dma_start(maa2a[0:32, :], d["maa_w2"][0]
                            ).then_inc(b.h("dS"), 16)
                g.dma_start(maa2a[32:64, :], d["maa_w2"][1]
                            ).then_inc(b.h("dS"), 16)
                g.dma_start(maa2b[0:32, :], d["maa_w2"][2]
                            ).then_inc(b.h("dS"), 16)
                g.dma_start(maa2b[32:64, :], d["maa_w2"][3]
                            ).then_inc(b.h("dS"), 16)
            b.g(fsm)

            dA_cur = b.r("dA", 16)
            dB_cur = b.r("dB", 16)

            def fw0(g):
                for slot_sb, nm in ((wsA, "wrT"), (wsB, "wkT")):
                    sem = "dA" if slot_sb is wsA else "dB"
                    g.dma_start(
                        slot_sb.rearrange("p (kc x) -> p kc x", kc=8),
                        d[nm][:, 0:512].rearrange("(kc p) x -> p kc x",
                                                  p=128)
                    ).then_inc(b.h(sem), 16)
            b.g(fw0)

            ln2_pe_prev = 0
            stv = {h: {} for h in range(H)}

            for blk in range(NBK):
                t0 = blk * W
                tsl = slice(t0, t0 + W)

                ps_st, g_st = acq()
                pe_prev = ln2_pe_prev
                for m in range(NCH):
                    cn = cop({"dS": dS_done, "pe": pe_prev},
                             lambda cc, m=m: cc.activation(
                                 sqb2[:, ts(m)], R[m][:, tsl], AF.Square))
                    vn = vop({"pe": pe_prev},
                             lambda vv, m=m: vv.tensor_copy(
                                 rcp2[:, ts(m)], R[m][:, tsl]))
                    waits = {"sc": cn, "v": vn}
                    if m == 0:
                        for kk_, vv_ in g_st.items():
                            waits[kk_] = max(waits.get(kk_, 0), vv_)

                    def fst(tt, m=m):
                        tt.matmul(ps_st[:1, 0:W], onescol_bf[:],
                                  rcp2[:, ts(m)], start=(m == 0),
                                  stop=(m == NCH - 1))
                        return tt.matmul(ps_st[:1, 512:512 + W],
                                         onescol_bf[:], sqb2[:, ts(m)],
                                         start=(m == 0), stop=(m == NCH - 1))
                    pe_prev = top(waits, fst)
                pe_stats = pe_prev
                ln2_pe_prev = pe_stats
                mu2 = ln2r[:, 0 * W:1 * W]
                var2 = ln2r[:, 1 * W:2 * W]
                s2 = ln2r[:, 2 * W:3 * W]
                t2r = ln2r[:, 3 * W:4 * W]
                vn = vop({"pe": pe_stats}, lambda vv: vv.tensor_scalar_mul(
                    mu2, ps_st[:1, 0:W], 1.0 / DM))
                vn = vop({"v": vn}, lambda vv: vv.tensor_mul(var2, mu2, mu2))
                vn = vop({"v": vn}, lambda vv: vv.scalar_tensor_tensor(
                    var2, ps_st[:1, 512:512 + W], 1.0 / DM, var2, ALU.mult,
                    ALU.subtract))
                cn = cop({"v": vn}, lambda cc: cc.activation(
                    s2, var2, AF.Sqrt,
                    bias=consts[:1, SC_LNEPS:SC_LNEPS + 1]))
                vn = vop({"sc": cn}, lambda vv: vv.reciprocal(s2, s2))
                vrows = vop({"v": vn}, lambda vv: vv.scalar_tensor_tensor(
                    t2r, mu2, -1.0, s2, ALU.mult, ALU.mult))
                prot.release({"v": vrows})

                ps_bc, g_bc = acq()
                g_bc["v"] = max(g_bc.get("v", 0), vrows)

                def fbc(tt):
                    tt.matmul(ps_bc[:, 0:W], ones1_f32[:], s2[:1, :],
                              start=True, stop=True)
                    return tt.matmul(ps_bc[:, 512:512 + W], ones1_f32[:],
                                     t2r[:1, :], start=True, stop=True)
                pe_bc = top(g_bc, fbc)

                vap = 0
                for m in range(NCH):
                    vap = vop({"pe": pe_bc, "v": v_bn_prev},
                              lambda vv, m=m: vv.tensor_tensor(
                                  scrA[:, ts(m)], R[m][:, tsl],
                                  ps_bc[:, 0:W], ALU.mult))
                    vap = vop({"v": vap},
                              lambda vv, m=m: vv.tensor_tensor(
                                  xnb[:, ts(m)], scrA[:, ts(m)],
                                  ps_bc[:, 512:512 + W], ALU.add))
                v_ln2 = vap
                prot.release({"v": v_ln2})
                vxx = vop({"v": v_ln2}, lambda vv: vv.tensor_tensor(
                    v3(xx)[:, :, 1:W], v3(xnb)[:, :, 0:W - 1],
                    v3(xnb)[:, :, 1:W], ALU.subtract))
                vxx = vop({"v": vxx}, lambda vv: vv.tensor_tensor(
                    v3(xx)[:, :, 0:1],
                    bnd.rearrange("p (m o) -> p m o", o=1),
                    v3(xnb)[:, :, 0:1], ALU.subtract))
                v_bn_prev = vop({"v": vxx}, lambda vv: vv.tensor_copy(
                    bnd[:], xnb[:, W - 1::W]))

                vmx = vop({"v": vxx}, lambda vv: vv.tensor_tensor(
                    v3(scrB), v3(xx), cv3(CV_TMX), ALU.mult))
                pmx = pop({"v": vmx}, lambda pp: pp.tensor_tensor(
                    xmv[:, :], scrB[:, :], xnb[:, :], ALU.add))

                ps_lo, g_lo = acq()
                g_lo["pl"] = max(g_lo.get("pl", 0), pmx)
                g_lo["dS"] = dS_done

                def flo(tt):
                    for half in (0, 1):
                        for kc in range(NCH):
                            ins = tt.matmul(
                                ps_lo[0:64, half * 512:half * 512 + W],
                                maa1[:, kc * 128 + half * 64:
                                     kc * 128 + half * 64 + 64],
                                xmv[:, ts(kc)], start=(kc == 0),
                                stop=(kc == NCH - 1))
                    return ins
                pe_lo = top(g_lo, flo)
                cn1 = cop({"pe": pe_lo}, lambda cc: cc.activation(
                    losb[:, 0:128], ps_lo[0:64, 0:W], AF.Tanh))
                cn2 = cop({"sc": cn1}, lambda cc: cc.activation(
                    losb[:, 128:256], ps_lo[0:64, 512:512 + W], AF.Tanh))
                prot.release({"sc": cn2})

                xm_outs = [xmr, xmw, xmk, xmv]
                xm_ready = {}
                v_lo2_prev = vmx
                prev_pool = pmx
                for gi in range(4):
                    ps_g, g_g = acq()
                    g_g["sc"] = max(g_g.get("sc", 0), cn2)
                    if gi == 3:
                        g_g["pe"] = max(g_g.get("pe", 0), pe_lo)
                    m2ap = maa2a if gi < 2 else maa2b
                    pbase = 32 * (gi % 2)
                    lslice = losb[pbase:pbase + 32,
                                  (gi // 2) * 128:(gi // 2) * 128 + W]

                    def flo2(tt, m2ap=m2ap, pbase=pbase, lslice=lslice,
                             ps_g=ps_g):
                        for m in range(NCH):
                            ins = tt.matmul(ps_g[:, m * W:(m + 1) * W],
                                            m2ap[pbase:pbase + 32, ts(m)],
                                            lslice, start=True, stop=True)
                        return ins
                    pe_g = top(g_g, flo2)
                    vs = vop({"pe": pe_g, "v": v_lo2_prev,
                              "pl": prev_pool},
                             lambda vv, ps_g=ps_g, gi=gi: vv.tensor_tensor(
                                 v3(scrA), v3(ps_g), cv3(CV_TMAA0 + gi),
                                 ALU.add))
                    vs = vop({"v": vs}, lambda vv: vv.tensor_tensor(
                        scrA[:, :], scrA[:, :], xx[:, :], ALU.mult))
                    pw = {"v": vs}
                    if gi == 3:
                        pw["pl"] = pmx
                        pw["pe"] = pe_g
                    pn = pop(pw, lambda pp, gi=gi:
                             pp.tensor_tensor(xm_outs[gi][:, :], scrA[:, :],
                                              xnb[:, :], ALU.add))
                    xm_ready[gi] = pn
                    prev_pool = pn
                    prot.release({"v": vs})
                    v_lo2_prev = vs

                def load_after(pe_n, slot_sb, nm, half):
                    def f(g, pe_n=pe_n, slot_sb=slot_sb, nm=nm, half=half):
                        if pe_n:
                            g.wait_ge(b.h("pe"), pe_n)
                        sem = "dA" if slot_sb is wsA else "dB"
                        g.dma_start(
                            slot_sb.rearrange("p (kc x) -> p kc x", kc=8),
                            d[nm][:, half * 512:(half + 1) * 512].rearrange(
                                "(kc p) x -> p kc x", p=128)
                        ).then_inc(b.h(sem), 16)
                    b.g(f)

                def proj_half(slot_sb, src_xm, dst, half, wsem, wn, xm_n):
                    ps_p, g_p = acq()
                    g_p[wsem] = max(g_p.get(wsem, 0), wn)
                    g_p["pl"] = max(g_p.get("pl", 0), xm_n)

                    def fp(tt, slot_sb=slot_sb, src_xm=src_xm, ps_p=ps_p,
                           half=half):
                        for mo in range(4):
                            for kc in range(NCH):
                                ins = tt.matmul(
                                    ps_p[:, mo * W:(mo + 1) * W],
                                    slot_sb[:, kc * 512 + mo * 128:
                                            kc * 512 + (mo + 1) * 128],
                                    src_xm[:, ts(kc)], start=(kc == 0),
                                    stop=(kc == NCH - 1))
                        return ins
                    pe_p = top(g_p, fp)
                    vn = vop({"pe": pe_p}, lambda vv, dst=dst, ps_p=ps_p,
                             half=half: vv.tensor_copy(
                                 dst[:, half * 512:(half + 1) * 512],
                                 ps_p[:, 0:512]))
                    prot.release({"v": vn})
                    return pe_p, vn

                pe_r0, v_r0 = proj_half(wsA, xmr, rT, 0, "dA", dA_cur,
                                        xm_ready[0])
                dA_cur = b.r("dA", 16)
                load_after(pe_r0, wsA, "wrT", 1)
                pe_r1, v_r1 = proj_half(wsA, xmr, rT, 1, "dA", dA_cur,
                                        xm_ready[0])
                dA_cur = b.r("dA", 16)
                load_after(pe_r1, wsA, "wvT", 0)
                pe_r, v_r = pe_r1, max(v_r0, v_r1)

                pe_k0, v_k0 = proj_half(wsB, xmk, kraw, 0, "dB", dB_cur,
                                        xm_ready[2])
                dB_cur = b.r("dB", 16)
                load_after(pe_k0, wsB, "wkT", 1)
                pe_k1, v_k1 = proj_half(wsB, xmk, kraw, 1, "dB", dB_cur,
                                        xm_ready[2])
                dB_cur = b.r("dB", 16)
                load_after(pe_k1, wsB, "woT", 0)
                pe_k, v_k = pe_k1, max(v_k0, v_k1)
                dB_wo0 = dB_cur

                v_v = 0
                pe_v = 0
                for nsI in (0, 1):
                    ps_v, g_v = acq()
                    g_v["dA"] = max(g_v.get("dA", 0), dA_cur)
                    g_v["pl"] = max(g_v.get("pl", 0), xm_ready[3])

                    def fv_(tt, ps_v=ps_v, nsI=nsI):
                        for kc in range(NCH):
                            ins = tt.matmul(
                                ps_v[:, 0:512], xmv[:, ts(kc)],
                                wsA[:, kc * 512:(kc + 1) * 512],
                                start=(kc == 0), stop=(kc == NCH - 1))
                        return ins
                    pe_v = top(g_v, fv_)
                    v_v = vop({"pe": pe_v, "v": v_v},
                              lambda vv, ps_v=ps_v, nsI=nsI:
                              vv.tensor_copy(vtm[:, nsI * 512:
                                                 (nsI + 1) * 512],
                                             ps_v[:, 0:512]))
                    prot.release({"v": v_v})
                    dA_cur = b.r("dA", 16)
                    if nsI == 0:
                        load_after(pe_v, wsA, "wvT", 1)
                    else:
                        load_after(pe_v, wsA, "wrT", 0)

                ps_g1, g_g1 = acq()
                g_g1["dS"] = dS_done
                g_g1["pl"] = max(g_g1.get("pl", 0), xm_ready[0])

                def fg1(tt, ps_g1=ps_g1):
                    for kc in range(NCH):
                        ins = tt.matmul(ps_g1[:, 0:W],
                                        gw1[:, kc * 128:(kc + 1) * 128],
                                        xmr[:, ts(kc)], start=(kc == 0),
                                        stop=(kc == NCH - 1))
                    return ins
                pe_g1 = top(g_g1, fg1)
                c_g1 = cop({"pe": pe_g1}, lambda cc, ps_g1=ps_g1:
                           cc.activation(gt1[:, :], ps_g1[:, 0:W], AF.Tanh))
                prot.release({"sc": c_g1})
                ps_g2, g_g2 = acq()
                g_g2["sc"] = max(g_g2.get("sc", 0), c_g1)

                def fg2(tt, ps_g2=ps_g2):
                    for m in range(NCH):
                        ins = tt.matmul(ps_g2[:, m * W:(m + 1) * W],
                                        gw2[:, ts(m)], gt1[:, :],
                                        start=True, stop=True)
                    return ins
                pe_g2 = top(g_g2, fg2)
                v_g2 = vop({"pe": pe_g2}, lambda vv, ps_g2=ps_g2:
                           vv.tensor_copy(gTt[:, :], ps_g2[:, :]))
                prot.release({"v": v_g2})

                ps_d1, g_d1 = acq()
                g_d1["pl"] = max(g_d1.get("pl", 0), xm_ready[1])
                g_d1["dS"] = dS_done

                def fd1(tt, ps_d1=ps_d1):
                    for kc in range(NCH):
                        ins = tt.matmul(ps_d1[0:64, 0:W],
                                        dw1[:, kc * 64:(kc + 1) * 64],
                                        xmw[:, ts(kc)], start=(kc == 0),
                                        stop=(kc == NCH - 1))
                    return ins
                pe_d1 = top(g_d1, fd1)
                c_d1 = cop({"pe": pe_d1}, lambda cc, ps_d1=ps_d1:
                           cc.activation(dt1[:, :], ps_d1[0:64, 0:W],
                                         AF.Tanh))
                prot.release({"sc": c_d1})
                ps_d2, g_d2 = acq()
                g_d2["sc"] = max(g_d2.get("sc", 0), c_d1)

                def fd2(tt, ps_d2=ps_d2):
                    for m in range(NCH):
                        ins = tt.matmul(ps_d2[:, m * W:(m + 1) * W],
                                        dw2[:, ts(m)], dt1[:, :],
                                        start=True, stop=True)
                    return ins
                pe_d2 = top(g_d2, fd2)
                vn_u = vop({"pe": pe_d2, "pl": xm_ready[3]},
                           lambda vv, ps_d2=ps_d2: vv.tensor_tensor(
                               v3(scrA), v3(ps_d2), cv3(CV_TD), ALU.add))
                prot.release({"v": vn_u})
                v_m1 = vop({"v": vn_u, "pl": pmx}, lambda vv:
                           vv.tensor_scalar(scrB[:, :], scrA[:, :], -1.0,
                                            0.0, ALU.mult, ALU.max))
                v_au = vop({"v": v_m1}, lambda vv:
                           vv.scalar_tensor_tensor(scrA[:, :], scrB[:, :],
                                                   2.0, scrA[:, :],
                                                   ALU.mult, ALU.add))
                c_ex = cop({"v": v_au}, lambda cc: cc.activation(
                    scrA[:, :], scrA[:, :], AF.Exp, scale=-1.0))
                c_ln = cop({"sc": c_ex}, lambda cc: cc.activation(
                    scrA[:, :], scrA[:, :], AF.Ln, bias=1.0))
                v_sp = vop({"sc": c_ln}, lambda vv:
                           vv.tensor_tensor(scrA[:, :], scrA[:, :],
                                            scrB[:, :], ALU.add))
                v_wd = vop({"v": v_sp}, lambda vv: vv.tensor_scalar(
                    wdec[:, :], scrA[:, :], -1.0, -0.5, ALU.mult, ALU.add))
                c_lw = cop({"v": v_wd}, lambda cc: cc.activation(
                    scrB[:, :], wdec[:, :], AF.Exp))
                v_lw = vop({"sc": c_lw}, lambda vv: vv.tensor_scalar_mul(
                    lwt[:, :], scrB[:, :], -1.0))

                ps_kb, g_kb = acq()
                g_kb["pl"] = max(g_kb.get("pl", 0), xm_ready[3])
                g_kb["dS"] = dS_done

                def fkb(tt, ps_kb=ps_kb):
                    for kc in range(NCH):
                        tt.matmul(ps_kb[0:16, 0:W],
                                  bw1[:, kc * 64:kc * 64 + 16],
                                  xmk[:, ts(kc)], start=(kc == 0),
                                  stop=(kc == NCH - 1))
                    for kc in range(NCH):
                        tt.matmul(ps_kb[32:48, 0:W],
                                  bw1[:, kc * 64 + 16:kc * 64 + 32],
                                  xmw[:, ts(kc)], start=(kc == 0),
                                  stop=(kc == NCH - 1))
                    for kc in range(NCH):
                        tt.matmul(ps_kb[0:16, 512:512 + W],
                                  bw1[:, kc * 64 + 32:kc * 64 + 48],
                                  xmw[:, ts(kc)], start=(kc == 0),
                                  stop=(kc == NCH - 1))
                    for kc in range(NCH):
                        ins = tt.matmul(ps_kb[32:48, 512:512 + W],
                                        bw1[:, kc * 64 + 48:kc * 64 + 64],
                                        xmk[:, ts(kc)], start=(kc == 0),
                                        stop=(kc == NCH - 1))
                    return ins
                pe_kb = top(g_kb, fkb)
                c_kb = cop({"pe": pe_kb}, lambda cc, ps_kb=ps_kb:
                           cc.activation(kbt[:, :], ps_kb[0:16, 0:W],
                                         AF.Tanh))
                v_ab2 = vop({"pe": pe_kb}, lambda vv, ps_kb=ps_kb:
                            vv.tensor_copy(abt[32:48, :], ps_kb[32:48, 0:W]))
                v_ma = vop({"v": v_ab2}, lambda vv, ps_kb=ps_kb:
                           vv.tensor_copy(mabt[:, :],
                                          ps_kb[0:16, 512:512 + W]))
                v_mk = vop({"v": v_ma}, lambda vv, ps_kb=ps_kb:
                           vv.tensor_copy(mkbt[32:48, :],
                                          ps_kb[32:48, 512:512 + W]))
                c_mk = c_kb
                prot.release({"v": v_mk, "sc": c_kb})

                ps_kk, g_kk = acq()
                g_kk["sc"] = max(g_kk.get("sc", 0), c_kb)
                g_kk["v"] = max(g_kk.get("v", 0), v_mk)

                def fkk(tt, ps_kk=ps_kk):
                    for m in range(NCH):
                        ins = tt.matmul(ps_kk[:, m * W:(m + 1) * W],
                                        bw2ka[0:16, ts(m)], kbt[:, :],
                                        start=True, stop=True)
                    return ins
                pe_kk = top(g_kk, fkk)
                v_kk = vop({"pe": pe_kk, "v": v_k}, lambda vv, ps_kk=ps_kk:
                           vv.tensor_tensor(kkn[:, :], ps_kk[:, :],
                                            kraw[:, :], ALU.add))
                prot.release({"v": v_kk})
                c_sq = cop({"v": v_kk, "pe": pe_stats},
                           lambda cc: cc.activation(
                               sqb2[:, :], kkn[:, :], AF.Square))
                ps_nr, g_nr = acq()
                g_nr["sc"] = max(g_nr.get("sc", 0), c_sq)

                def fnr(tt, ps_nr=ps_nr):
                    for m in range(NCH):
                        ins = tt.matmul(ps_nr[0:16, 0:W],
                                        hselt_bf[:, m * 16:(m + 1) * 16],
                                        sqb2[:, ts(m)], start=(m == 0),
                                        stop=(m == NCH - 1))
                    return ins
                pe_nr = top(g_nr, fnr)
                c_nr = cop({"pe": pe_nr}, lambda cc, ps_nr=ps_nr:
                           cc.activation(rnsb[:, :], ps_nr[0:16, 0:W],
                                         AF.Ln,
                                         bias=consts[0:16,
                                                     SC_EPS30:SC_EPS30 + 1]))
                c_nr = cop({"sc": c_nr}, lambda cc: cc.activation(
                    rnsb[:, :], rnsb[:, :], AF.Exp, scale=-0.5))
                prot.release({"sc": c_nr})
                v_nr = vop({"sc": c_nr}, lambda vv: vv.tensor_scalar_min(
                    rnsb[:, :], rnsb[:, :], 1e12))
                ps_nb, g_nb = acq()
                g_nb["v"] = max(g_nb.get("v", 0), v_nr)

                def fnb(tt, ps_nb=ps_nb):
                    for m in range(NCH):
                        ins = tt.matmul(ps_nb[:, m * W:(m + 1) * W],
                                        hsel[:, m * 128:(m + 1) * 128],
                                        rnsb[:, :], start=True, stop=True)
                    return ins
                pe_nb = top(g_nb, fnb)
                v_kkn = vop({"pe": pe_nb}, lambda vv, ps_nb=ps_nb:
                            vv.tensor_tensor(kkn[:, :], kkn[:, :],
                                             ps_nb[:, :], ALU.mult))
                prot.release({"v": v_kkn})

                def expand_gate(lhs_ap, rhs_sb, cvcol, dst, wait_sc,
                                wait_v=0):
                    ps_e, g_e = acq()
                    g_e["sc"] = max(g_e.get("sc", 0), wait_sc)
                    if wait_v:
                        g_e["v"] = max(g_e.get("v", 0), wait_v)

                    def fe(tt, lhs_ap=lhs_ap, rhs_sb=rhs_sb, ps_e=ps_e):
                        for m in range(NCH):
                            ins = tt.matmul(ps_e[:, m * W:(m + 1) * W],
                                            lhs_ap[ts(m)] if False else
                                            lhs_ap[:, ts(m)], rhs_sb[:, :],
                                            start=True, stop=True)
                        return ins
                    pe_e = top(g_e, fe)
                    vn = vop({"pe": pe_e, "sc": c_lw},
                             lambda vv, ps_e=ps_e, cvcol=cvcol:
                             vv.tensor_tensor(v3(scrB), v3(ps_e),
                                              cv3(cvcol), ALU.add))
                    cs = cop({"v": vn}, lambda cc, dst=dst: cc.activation(
                        dst[:, :], scrB[:, :], AF.Sigmoid))
                    prot.release({"v": vn})
                    return cs

                c_al = expand_gate(bw2ka[32:48, :], abt[32:48, :],
                                   CV_AAAAA, alpha, c_kb, v_mk)
                c_mag = expand_gate(bw2mm[0:16, :], mabt, CV_MISCA, mag,
                                    c_al, v_mk)
                c_mkg = expand_gate(bw2mm[32:48, :], mkbt[32:48, :],
                                     CV_MISCK, mkg, c_mag, v_mk)

                v_f1 = vop({"sc": c_mag}, lambda vv: vv.tensor_scalar(
                    sbA[:, :], mag[:, :], -1.0, 1.0, ALU.mult, ALU.add))
                v_f2 = vop({"v": v_f1, "sc": c_al},
                           lambda vv: vv.tensor_tensor(
                               sbB[:, :], alpha[:, :], sbA[:, :], ALU.mult))
                v_f3 = vop({"v": v_f2}, lambda vv: vv.tensor_tensor(
                    sbA[:, :], mag[:, :], sbB[:, :], ALU.add))
                v_ks = vop({"v": v_f3}, lambda vv: vv.tensor_tensor(
                    sbB[:, :], kraw[:, :], sbA[:, :], ALU.mult))
                v_e1 = vop({"v": v_ks, "sc": c_mkg},
                           lambda vv: vv.tensor_tensor(
                               sbA[:, :], wdec[:, :], mkg[:, :], ALU.mult))
                c_e2 = cop({"v": v_e1}, lambda cc: cc.activation(
                    sbA[:, :], sbA[:, :], AF.Exp))
                v_kf = vop({"sc": c_e2}, lambda vv: vv.tensor_tensor(
                    kf[:, :], sbB[:, :], sbA[:, :], ALU.mult))

                v_base = vop({"v": v_lw}, lambda vv: vv.tensor_copy(
                    basec[:, :], carry[:, :]))
                scans = v_base
                for m in range(NCH):
                    scans = vop({"v": scans}, lambda vv, m=m:
                                vv.tensor_tensor_scan(
                                    Lct[:, ts(m)], lwt[:, ts(m)],
                                    z128, carry[:, m:m + 1],
                                    ALU.add, ALU.add))
                v_cu = vop({"v": scans}, lambda vv: vv.tensor_copy(
                    carry[:, :], Lct[:, W - 1::W]))
                v_rel = vop({"v": v_cu}, lambda vv: vv.tensor_tensor(
                    v3(Lct), v3(Lct),
                    basec.rearrange("p (m o) -> p m o", o=1).broadcast_to(
                        (128, 8, W)), ALU.subtract))
                c_di = cop({"v": v_rel, "sc": c_e2},
                           lambda cc: cc.activation(
                               dinc[:, :], Lct[:, :], AF.Exp))
                c_dl = cop({"sc": c_di}, lambda cc: cc.activation(
                    dlastf[:, :], Lct[:, W - 1::W], AF.Exp))
                c_dv = cop({"sc": c_di}, lambda cc: cc.activation(
                    dinv[:, :], Lct[:, :], AF.Exp, scale=-1.0))
                v_de = vop({"v": v_rel}, lambda vv: vv.tensor_tensor(
                    scrA[:, :], Lct[:, :], lwt[:, :], ALU.subtract))
                c_de = cop({"v": v_de, "sc": c_dv, "v2": 0}, lambda cc:
                           cc.activation(sbA[:, :], scrA[:, :], AF.Exp))

                v_Qb = vop({"sc": c_di, "v": v_r}, lambda vv:
                           vv.tensor_tensor(Qbt[:, :], rT[:, :], dinc[:, :],
                                            ALU.mult))
                v_Ab = vop({"sc": c_de, "v": v_kkn}, lambda vv:
                           vv.tensor_tensor(Abt[:, :], kkn[:, :], sbA[:, :],
                                            ALU.mult))
                v_b1 = vop({"sc": c_dv, "v": v_Ab}, lambda vv:
                           vv.scalar_tensor_tensor(sbB[:, :], alpha[:, :],
                                                   -1.0, dinv[:, :],
                                                   ALU.mult, ALU.mult))
                v_Bt = vop({"v": v_b1}, lambda vv: vv.tensor_tensor(
                    Btt[:, :], kkn[:, :], sbB[:, :], ALU.mult))
                v_Kt = vop({"v": v_kf, "sc": c_dv}, lambda vv:
                           vv.tensor_tensor(Ktt[:, :], kf[:, :], dinv[:, :],
                                            ALU.mult))
                dl3 = v3(dinc)[:, :, W - 1:W].broadcast_to((128, 8, W))
                v_Bh = vop({"v": v_Bt}, lambda vv: vv.tensor_tensor(
                    v3(sbA), v3(Btt), dl3, ALU.mult))
                v_Kh = vop({"v": v_Kt}, lambda vv: vv.tensor_tensor(
                    v3(sbB), v3(Ktt), dl3, ALU.mult))

                tmv = {}
                prev_solve_pe = 0
                for nmsrc, src, dst, vwait in (("Ab", Abt, Atm, v_Ab),
                                               ("Bh", sbA, Bhtm, v_Bh),
                                               ("Kh", sbB, Khtm, v_Kh)):
                    ps_t, g_t = acqT()

                    def ftr(tt, src=src, ps_t=ps_t):
                        for m in range(NCH):
                            ins = tt.transpose(ps_t[:, m * W:(m + 1) * W],
                                               src[:, ts(m)], ident_bf[:])
                        return ins
                    wt = dict(g_t)
                    wt["v"] = max(wt.get("v", 0), vwait)
                    pe_t = top(wt, ftr)
                    vt = vop({"pe": pe_t}, lambda vv, dst=dst, ps_t=ps_t:
                             vv.tensor_copy(dst[:, :], ps_t[:, :1024]))
                    ptrot.release({"v": vt})
                    tmv[nmsrc] = vt

                vmats = max(v_Qb, v_Ab, v_Bt, v_Kt)
                vtms = max(tmv["Ab"], tmv["Bh"], tmv["Kh"], v_v)
                y_evs = []
                last_state_pe = 0

                def gmax(*ds):
                    out = {}
                    for dd in ds:
                        if not dd:
                            continue
                        for k2, v2 in dd.items():
                            out[k2] = max(out.get(k2, 0), v2)
                    return out

                def ecopy(h, waits, dst, src):
                    if h % 2 == 0:
                        return {"v": vop(waits, lambda vv, dst=dst, src=src:
                                         vv.tensor_copy(dst, src))}
                    return {"sc": cop(waits, lambda cc, dst=dst, src=src:
                                      cc.copy(dst, src))}

                SH = {}
                for h in range(H):
                    mm_ = h // 2
                    pb = 64 * (h % 2)
                    csl = slice(mm_ * W, (mm_ + 1) * W)
                    SH[h] = dict(
                        i=h % NILV, pb=pb, mm_=mm_,
                        Af=Abt[pb:pb + 64, csl], Bf=Btt[pb:pb + 64, csl],
                        Kf=Ktt[pb:pb + 64, csl], Qf=Qbt[pb:pb + 64, csl],
                        Vts=vtm[:, h * 64:h * 64 + 64],
                        Bhs=Bhtm[:, h * 64:h * 64 + 64],
                        Khs=Khtm[:, h * 64:h * 64 + 64],
                        STbs=STb[pb:pb + 64, mm_ * 64:mm_ * 64 + 64],
                        STfs=STf[pb:pb + 64, mm_ * 64:mm_ * 64 + 64],
                    )

                for hg in range(0, H, NILV):
                    hh = list(range(hg, min(hg + NILV, H)))
                    for h in hh:
                        s = SH[h]
                        i = s["i"]
                        sg = {"pe": slot_free_pe.get(i, 0)}
                        ps1, gu1 = acq()
                        pe1 = top(gmax(gu1, {"v": vmats}, sg),
                                  lambda tt, ps1=ps1, s=s: tt.matmul(
                                      ps1[:, 0:128], s["Af"], s["Bf"],
                                      start=True, stop=True))
                        vN = vop(gmax({"pe": pe1}, sg),
                                 lambda vv, ps1=ps1, i=i: vv.tensor_tensor(
                                     NsbS[i][:, :], ps1[:, 0:128],
                                     mask_sl[:, :], ALU.mult))
                        prot.release({"v": vN})
                        s["gN"] = {"v": vN}
                    for h in hh:
                        s = SH[h]
                        i = s["i"]
                        sg = {"pe": slot_free_pe.get(i, 0)}
                        ps2_, gu2 = acq()

                        def f13(tt, ps2_=ps2_, s=s):
                            tt.matmul(ps2_[:, 0:128], s["Bf"], s["Af"],
                                      start=True, stop=True)
                            return tt.matmul(ps2_[:, 128:256], s["Bf"],
                                             s["Qf"], start=True, stop=True)
                        pe2 = top(gmax(gu2, {"v": vmats}, sg), f13)
                        v13 = vop(gmax({"pe": pe2}, sg),
                                  lambda vv, ps2_=ps2_, i=i:
                                  vv.tensor_tensor(nt3tS[i][:, :],
                                                   ps2_[:, 0:256],
                                                   mask_t[:, :], ALU.mult))
                        prot.release({"v": v13})
                        s["g13"] = {"v": v13}
                    for h in hh:
                        s = SH[h]
                        i = s["i"]
                        sg = {"pe": slot_free_pe.get(i, 0)}
                        ps3, gu3 = acq()

                        def f24(tt, ps3=ps3, s=s):
                            tt.matmul(ps3[:, 0:128], s["Kf"], s["Af"],
                                      start=True, stop=True)
                            return tt.matmul(ps3[:, 128:256], s["Kf"],
                                             s["Qf"], start=True, stop=True)
                        pe3 = top(gmax(gu3, {"v": vmats}, sg), f24)
                        v24 = vop(gmax({"pe": pe3}, sg),
                                  lambda vv, ps3=ps3, i=i:
                                  vv.tensor_tensor(t2t4S[i][:, :],
                                                   ps3[:, 0:256],
                                                   mask_t[:, :], ALU.mult))
                        prot.release({"v": v24})
                        s["g24"] = {"v": v24}
                        s["P"] = NsbS[i][:, :]
                        s["PT"] = nt3tS[i][:, 0:128]
                        s["gP"] = s["gN"]
                        s["gPT"] = s["g13"]
                    for lev in range(1, 7):
                        for h in hh:
                            s = SH[h]
                            i = s["i"]
                            psq, gsq = acq()

                            def fsq(tt, psq=psq, PTc=s["PT"], Pc=s["P"]):
                                tt.matmul(psq[:, 0:128], PTc, Pc,
                                          start=True, stop=True)
                                return tt.matmul(psq[:, 512:640], Pc, PTc,
                                                 start=True, stop=True)
                            pesq = top(gmax(gsq, s["gP"], s["gPT"]), fsq)
                            Pnew = (Pb0S[i] if lev % 2 == 1 else Pb1S[i])
                            gPn = ecopy(h, {"pe": pesq}, Pnew[:, :],
                                        psq[:, 0:128])
                            PTnew = PTtS[i][:, (lev - 1) * 128:lev * 128]
                            gPTn = ecopy(h, gmax({"pe": pesq}, gPn),
                                         PTnew, psq[:, 512:640])
                            prot.release(gmax(gPn, gPTn))
                            s["P"] = Pnew[:, :]
                            s["PT"] = PTnew
                            s["gP"] = gPn
                            s["gPT"] = gPTn
                    for h in hh:
                        s = SH[h]
                        i = s["i"]
                        ps4, gu4 = acq()

                        def fw0(tt, s=s, ps4=ps4, i=i):
                            tt.matmul(ps4[:, 0:64], s["Af"], s["STbs"],
                                      start=True, stop=False)
                            return tt.matmul(ps4[:, 0:64],
                                             t2t4S[i][:, 0:128], s["Vts"],
                                             start=False, stop=True)
                        pew0 = top(gmax(gu4, s["g24"], {"v": vtms},
                                        stv[h]), fw0)
                        gX = ecopy(h, {"pe": pew0}, Xc0S[i][:, :],
                                   ps4[:, 0:64])
                        prot.release(gX)
                        s["Xcur"], s["Xnxt"] = Xc0S[i], Xc1S[i]
                        s["gX"] = gX
                    for k in range(6, -1, -1):
                        for h in hh:
                            s = SH[h]
                            i = s["i"]
                            PTk = (PTtS[i][:, (k - 1) * 128:k * 128]
                                   if k >= 1 else nt3tS[i][:, 0:128])
                            psc, gc = acq()

                            def fch(tt, psc=psc, PTk=PTk, Xc=s["Xcur"]):
                                tt.matmul(psc[:, 0:64], ident_bf[:, :],
                                          Xc[:, :], start=True, stop=False)
                                return tt.matmul(psc[:, 0:64], PTk,
                                                 Xc[:, :],
                                                 start=False, stop=True)
                            pech = top(gmax(gc, s["gX"], s["gPT"],
                                            s["g13"]), fch)
                            gX = ecopy(h, {"pe": pech}, s["Xnxt"][:, :],
                                       psc[:, 0:64])
                            prot.release(gX)
                            s["Xcur"], s["Xnxt"] = s["Xnxt"], s["Xcur"]
                            s["gX"] = gX
                    for h in hh:
                        s = SH[h]
                        i = s["i"]
                        ps5, gu5 = acq()

                        def fy(tt, ps5=ps5, s=s, i=i, Xc=s["Xcur"]):
                            tt.matmul(ps5[:, 0:64], s["Qf"], s["STbs"],
                                      start=True, stop=False)
                            tt.matmul(ps5[:, 0:64], nt3tS[i][:, 128:256],
                                      Xc[:, :], start=False, stop=False)
                            return tt.matmul(ps5[:, 0:64],
                                             t2t4S[i][:, 128:256], s["Vts"],
                                             start=False, stop=True)
                        pey = top(gmax(gu5, s["gX"], s["g24"], s["g13"],
                                       stv[h]), fy)
                        gY = ecopy(h, {"pe": pey},
                                   ytm[:, h * 64:h * 64 + 64],
                                   ps5[:, 0:64])
                        prot.release(gY)
                        y_evs.append(gY)
                        s["gY"] = gY
                        s["peY"] = pey
                    for h in hh:
                        s = SH[h]
                        i = s["i"]
                        pb = s["pb"]
                        mm_ = s["mm_"]
                        vdd = vop(gmax({"sc": c_dl}, s["gY"]),
                                  lambda vv, pb=pb, mm_=mm_, i=i:
                                  vv.tensor_scalar(
                                      ddlS[i][pb:pb + 64, :],
                                      ident_bf[pb:pb + 64, pb:pb + 64],
                                      dlastf[pb:pb + 64, mm_:mm_ + 1],
                                      None, ALU.mult))
                        ps6, gu6 = acq()

                        def fs6(tt, ps6=ps6, s=s, pb=pb, i=i,
                                Xc=s["Xcur"]):
                            tt.matmul(ps6[pb:pb + 64, 0:64],
                                      ddlS[i][pb:pb + 64, :], s["STbs"],
                                      start=True, stop=False)
                            tt.matmul(ps6[pb:pb + 64, 0:64], s["Bhs"],
                                      Xc[:, :], start=False, stop=False)
                            return tt.matmul(ps6[pb:pb + 64, 0:64],
                                             s["Khs"], s["Vts"],
                                             start=False, stop=True)
                        pes6 = top(gmax(gu6, {"v": vdd}, s["gX"],
                                        {"v": vtms}, {"pe": s["peY"]}), fs6)
                        g1_ = ecopy(h, {"pe": pes6}, s["STfs"],
                                    ps6[pb:pb + 64, 0:64])
                        g2_ = ecopy(h, gmax({"pe": pes6}, g1_), s["STbs"],
                                    ps6[pb:pb + 64, 0:64])
                        prot.release(gmax(g1_, g2_))
                        stv[h] = gmax(g1_, g2_)
                        slot_free_pe[i] = pes6
                        last_state_pe = pes6

                gy_all = gmax(*y_evs)
                y3 = ytm.rearrange("p (g i) -> p g i", g=16)
                c_ys = cop(gmax(gy_all, {"pe": last_state_pe}),
                           lambda cc: cc.activation(sbA[:, :DM], ytm[:, :],
                                                    AF.Square))
                sbA3 = sbA[:, :DM].rearrange("p (g i) -> p g i", g=16)
                musum = m16[:, 0:16]
                sqsum = m16[:, 16:32]
                muv = m16[:, 32:48]
                mus2 = m16[:, 48:64]
                varv = m16[:, 64:80]
                rstd = m16[:, 80:96]
                vgs = vop(gmax({"sc": c_ys}, gy_all),
                          lambda vv: vv.reduce_sum(
                              musum.rearrange("p (g o) -> p g o", o=1), y3,
                              axis=mybir.AxisListType.X))
                vgs = vop({"v": vgs}, lambda vv: vv.reduce_sum(
                    sqsum.rearrange("p (g o) -> p g o", o=1), sbA3,
                    axis=mybir.AxisListType.X))
                vgs = vop({"v": vgs}, lambda vv: vv.tensor_scalar_mul(
                    muv, musum, 1.0 / HS))
                vgs = vop({"v": vgs}, lambda vv: vv.tensor_mul(
                    mus2, muv, muv))
                vgs = vop({"v": vgs}, lambda vv: vv.scalar_tensor_tensor(
                    varv, sqsum, 1.0 / HS, mus2, ALU.mult, ALU.subtract))
                c_rs = cop({"v": vgs}, lambda cc: cc.activation(
                    rstd, varv, AF.Ln,
                    bias=consts[:, SC_GNEPS:SC_GNEPS + 1]))
                c_rs = cop({"sc": c_rs}, lambda cc: cc.activation(
                    rstd, rstd, AF.Exp, scale=-0.5))
                v_rs = c_rs
                v_rs = vop({"sc": c_rs}, lambda vv: vv.tensor_copy(
                    rstd, rstd)) if False else 0
                mu3 = muv.rearrange("p (g o) -> p g o", o=1).broadcast_to(
                    (128, 16, HS))
                rs3 = rstd.rearrange("p (g o) -> p g o", o=1).broadcast_to(
                    (128, 16, HS))
                vyn = vop({"sc": c_rs, "v": vgs},
                          lambda vv: vv.tensor_tensor(
                    sbB[:, :DM].rearrange("p (g i) -> p g i", g=16), y3,
                    mu3, ALU.subtract))
                vyn = vop({"v": vyn}, lambda vv: vv.tensor_tensor(
                    sbB[:, :DM].rearrange("p (g i) -> p g i", g=16),
                    sbB[:, :DM].rearrange("p (g i) -> p g i", g=16), rs3,
                    ALU.mult))
                vyn = vop({"v": vyn}, lambda vv: vv.tensor_tensor(
                    sbB[:, :DM], sbB[:, :DM], lnw_tm[:, :], ALU.mult))
                vyn = vop({"v": vyn}, lambda vv: vv.tensor_tensor(
                    sbB[:, :DM], sbB[:, :DM], lnb_tm[:, :], ALU.add))
                vrk = vop({"v": max(v_r, v_kf, vyn)},
                          lambda vv: vv.tensor_tensor(
                              sbA[:, :], rT[:, :], kf[:, :], ALU.mult))
                vrk = vop({"v": vrk}, lambda vv: vv.tensor_tensor(
                    v3(sbA), v3(sbA), cv3(CV_FAAAA), ALU.mult))
                ps_rk, g_rk = acq()
                g_rk["v"] = max(g_rk.get("v", 0), vrk)

                def frk(tt, ps_rk=ps_rk):
                    for m in range(NCH):
                        ins = tt.matmul(ps_rk[0:16, 0:W],
                                        hselt_bf[:, m * 16:(m + 1) * 16],
                                        sbA[:, ts(m)], start=(m == 0),
                                        stop=(m == NCH - 1))
                    return ins
                pe_rk = top(g_rk, frk)
                v_rk2 = vop({"pe": pe_rk}, lambda vv, ps_rk=ps_rk:
                            vv.tensor_copy(rksb[:, :], ps_rk[0:16, 0:W]))
                prot.release({"v": v_rk2})
                ps_rt, g_rt = acqT()
                g_rt["v"] = max(g_rt.get("v", 0), v_rk2)
                pe_rt = top(g_rt, lambda tt, ps_rt=ps_rt: tt.transpose(
                    ps_rt[:, 0:16], rksb[:, :], ident_bf[0:16, 0:16]))
                v_rt = vop({"pe": pe_rt}, lambda vv, ps_rt=ps_rt:
                           vv.tensor_copy(rktm[:, :], ps_rt[:, 0:16]))
                ptrot.release({"v": v_rt})
                rk3 = rktm.rearrange("p (g o) -> p g o", o=1).broadcast_to(
                    (128, 16, HS))
                v3tm = vtm.rearrange("p (g i) -> p g i", g=16)
                vrkv = vop({"v": max(v_rt, v_v)}, lambda vv:
                           vv.tensor_tensor(
                               sbA[:, :DM].rearrange("p (g i) -> p g i",
                                                     g=16), v3tm, rk3,
                               ALU.mult))
                vfin = vop({"v": vrkv}, lambda vv: vv.tensor_tensor(
                    sbB[:, :DM], sbB[:, :DM], sbA[:, :DM], ALU.add))
                ps_gt, g_gt = acqT()
                g_gt["v"] = max(g_gt.get("v", 0), v_g2)

                def fgt(tt, ps_gt=ps_gt):
                    for m in range(NCH):
                        ins = tt.transpose(ps_gt[:, m * W:(m + 1) * W],
                                           gTt[:, ts(m)], ident_bf[:])
                    return ins
                pe_gt = top(g_gt, fgt)
                v_gt = vop({"pe": pe_gt}, lambda vv, ps_gt=ps_gt:
                           vv.tensor_copy(gtm[:, :], ps_gt[:, :1024]))
                ptrot.release({"v": v_gt})
                vyg = vop({"v": max(vfin, v_gt)}, lambda vv:
                          vv.tensor_tensor(sbB[:, :DM], sbB[:, :DM],
                                           gtm[:, :], ALU.mult))
                ps_yf, g_yf = acqT()
                g_yf["v"] = max(g_yf.get("v", 0), vyg)

                def fyf(tt, ps_yf=ps_yf):
                    for m in range(NCH):
                        ins = tt.transpose(ps_yf[:, m * W:(m + 1) * W],
                                           sbB[:, ts(m)], ident_bf[:])
                    return ins
                pe_yf = top(g_yf, fyf)
                v_yf = vop({"pe": pe_yf}, lambda vv, ps_yf=ps_yf:
                           vv.tensor_copy(ygfm[:, :], ps_yf[:, :1024]))
                ptrot.release({"v": v_yf})

                wo_halves = [dB_wo0, None]
                for half in (0, 1):
                    if half == 1:
                        dB_cur = b.r("dB", 16)
                        load_after(pe_wo, wsB, "woT", 1)
                        wo_halves[1] = dB_cur
                    ps_wo, g_wo = acq()
                    g_wo["dB"] = max(g_wo.get("dB", 0),
                                     wo_halves[half] if half == 0
                                     else dB_cur)
                    g_wo["v"] = max(g_wo.get("v", 0), v_yf)

                    def fwo2(tt, ps_wo=ps_wo, half=half):
                        for mo in range(4):
                            for kc in range(NCH):
                                ins = tt.matmul(
                                    ps_wo[:, mo * W:(mo + 1) * W],
                                    wsB[:, kc * 512 + mo * 128:
                                        kc * 512 + (mo + 1) * 128],
                                    ygfm[:, ts(kc)], start=(kc == 0),
                                    stop=(kc == NCH - 1))
                        return ins
                    pe_wo = top(g_wo, fwo2)
                    v_ra = 0
                    for mo in range(4):
                        m = half * 4 + mo
                        v_ra = vop({"pe": pe_wo, "v": v_ra, "sc": c_ys},
                                   lambda vv, m=m, mo=mo, ps_wo=ps_wo:
                                   vv.tensor_tensor(
                                       R[m][:, tsl], R[m][:, tsl],
                                       ps_wo[:, mo * W:(mo + 1) * W],
                                       ALU.add))
                    prot.release({"v": v_ra})
                dB_cur = b.r("dB", 16)
                load_after(pe_wo, wsB, "wkT", 0)
                b.barrier()
            _da, _db, _ds = b.n("dA"), b.n("dB"), b.n("dS")

            def fdrain(g, _da=_da, _db=_db, _ds=_ds):
                g.wait_ge(b.h("dA"), _da)
                g.wait_ge(b.h("dB"), _db)
                g.wait_ge(b.h("dS"), _ds)
            b.g(fdrain)
            b.barrier()


    def emit_attn_pipe():
        W = 128
        NBK = T // W
        NPAIR = 8
        NSL = 3

        def gmax(*ds):
            out = {}
            for dd in ds:
                if not dd:
                    continue
                for k2, v2 in dd.items():
                    out[k2] = max(out.get(k2, 0), v2)
            return out

        def vop(waits, fn):
            n = b.r("v")

            def f(vv, waits=dict(waits), fn=fn):
                for s, c in waits.items():
                    if c:
                        vv.wait_ge(b.h(s), c)
                fn(vv).then_inc(b.h("v"), 1)
            b.v(f)
            return n

        def cop(waits, fn):
            n = b.r("sc")

            def f(cc, waits=dict(waits), fn=fn):
                for s, c in waits.items():
                    if c:
                        cc.wait_ge(b.h(s), c)
                fn(cc).then_inc(b.h("sc"), 1)
            b.c(f)
            return n

        def pop(waits, fn):
            n = b.r("pl")

            def f(pp, waits=dict(waits), fn=fn):
                for s, c in waits.items():
                    if c:
                        pp.wait_ge(b.h(s), c)
                fn(pp).then_inc(b.h("pl"), 1)
            b.g(f)
            return n

        def top(waits, fn):
            n = b.r("pe")

            def f(tt, waits=dict(waits), fn=fn):
                for s, c in waits.items():
                    if c:
                        tt.wait_ge(b.h(s), c)
                fn(tt).then_inc(b.h("pe"), 1)
            b.t(f)
            return n

        def snap():
            return {"pe": b.n("pe"), "v": b.n("v"), "sc": b.n("sc"),
                    "pl": b.n("pl")}

        from contextlib import ExitStack
        with ExitStack() as _est:
            def sb(name, shape, dt=BF16):
                return _est.enter_context(nc.sbuf_tensor(name, list(shape),
                                                         dt))

            wsA = sb("q_wsA", [128, 8 * 512])
            wsB = sb("q_wsB", [128, 8 * 512])
            maa1 = sb("q_maa1", [128, 8 * 128])
            maa2a = sb("q_maa2a", [64, DM])
            maa2b = sb("q_maa2b", [64, DM])
            dw1 = sb("q_dw1", [128, 8 * 64])
            dw2 = sb("q_dw2", [64, DM])
            bw1 = sb("q_bw1", [128, 8 * 64])
            bw2ka = sb("q_bw2ka", [64, DM])
            bw2mm = sb("q_bw2mm", [64, DM])
            gw1 = sb("q_gw1", [128, 8 * 128])
            gw2 = sb("q_gw2", [128, DM])

            xnb = sb("q_xnb", [128, DM])
            xmr = sb("q_xmr", [128, DM])
            xmw = sb("q_xmw", [128, DM])
            xmk = sb("q_xmk", [128, DM])
            xmv = sb("q_xmv", [128, DM])
            kraw = sb("q_kraw", [128, DM])
            kkn = sb("q_kkn", [128, DM])
            alpha = sb("q_alpha", [128, DM])
            wdec = sb("q_wdec", [128, DM])
            dinc = sb("q_dinc", [128, DM])
            dinv = sb("q_dinv", [128, DM])
            mag = sb("q_mag", [128, DM])
            sbAp = sb("q_sbA", [128, DM])
            sbBp = sb("q_sbB", [128, DM])
            smb = sb("q_smb", [128, DM])
            xxb = _est.enter_context(nc.sbuf_tensor("q_xx", [128, DM], F32))
            scrA = _est.enter_context(nc.sbuf_tensor("q_scrA", [128, DM],
                                                     F32))
            scrB = _est.enter_context(nc.sbuf_tensor("q_scrB", [128, DM],
                                                     F32))
            xx = xxb
            lwt = xxb
            Lct = scrB
            losb = smb[0:64, 0:256]
            gt1 = smb[:, 256:384]
            dt1 = smb[0:64, 384:512]
            kbt = smb[0:16, 512:640]
            abt = smb[32:48, 640:768]
            mabt = smb[0:16, 768:896]
            mkbt = smb[32:48, 896:1024]
            gTt = sbAp
            mkg = dinc

            rTs = [sb(f"q_rT{s}", [128, DM]) for s in (0, 1)]
            kfs = [sb(f"q_kf{s}", [128, DM]) for s in (0, 1)]
            gtms = [sb(f"q_gt{s}", [128, DM]) for s in (0, 1)]
            QbtI = sb("q_Qb", [128, DM])
            vtmI = sb("q_vt", [128, DM])
            BKhI = sb("q_bk", [128, 2 * DM])
            dlfI = _est.enter_context(nc.sbuf_tensor("q_dl", [128, 8], F32))
            Abt = sb("q_Ab", [128, DM])
            Btt = sb("q_Bt", [128, DM])
            Ktt = sb("q_Kt", [128, DM])

            ytm = sb("q_ytm", [128, DM])
            sbAs = sb("q_sbAs", [128, DM])
            sbBs = sb("q_sbBs", [128, DM])
            ygfm = sbAs
            STb = sb("q_STb", [128, H * HS // 2])
            scrS = _est.enter_context(nc.sbuf_tensor("q_scrS", [128, 176],
                                                     F32))
            slb = [sb(f"q_slb{s}", [128, 2816]) for s in range(NSL)]
            carry = _est.enter_context(nc.sbuf_tensor("q_carry", [128, 8],
                                                      F32))
            basec = _est.enter_context(nc.sbuf_tensor("q_base", [128, 8],
                                                      F32))
            bnd = sb("q_bnd", [128, 8])

            ppT = [_est.enter_context(nc.psum_tensor(f"q_pp{j}", [128, 1024],
                                                     F32)) for j in (0, 1)]
            spT = [_est.enter_context(nc.psum_tensor(f"q_sp{j}", [128, 1024],
                                                     F32)) for j in range(2)]
            pprot = PsumRot(2)
            sprot = PsumRot(2)

            def acqP():
                s, g = pprot.acquire()
                return ppT[s], dict(g or {})

            def acqS():
                s, g = sprot.acquire()
                return spT[s], dict(g or {})

            def acqT():
                s, g = pprot.acquire()
                return ppT[s][:, 0:512].bitcast(BF16), dict(g or {})

            ptrot = pprot

            def b2(ap_, w):
                return ap_.rearrange("p (b c) -> p b c", b=2)[:, :, 0:w]

            def v3(ap_):
                return ap_.rearrange("p (m w) -> p m w", m=8)

            def cv3(col):
                apx = chanvec[:, col::NVC]
                return apx.rearrange("p (m o) -> p m o", o=1).broadcast_to(
                    (128, 8, W))

            z128 = consts[:, SC_ZERO:SC_ZERO + 1].broadcast_to((128, 128))

            C_NT, C_T24, C_PL0, C_SQA, C_SQB, C_RA, C_RB = \
                0, 512, 1024, 1280, 1792, 2304, 2560
            C_W0, C_XC, C_DD = C_PL0, C_PL0 + 128, C_SQA

            vz = vop({}, lambda vv: vv.memset(STb[:], 0.0))
            vz = vop({"v": vz}, lambda vv: vv.memset(carry[:], 0.0))
            v_init = vop({"v": vz}, lambda vv: vv.memset(bnd[:], 0.0))

            dS_done = b.r("dS", 16 * 17)

            def fsm(s):
                def ld(dst, nm):
                    s.dma_start(dst, d[nm]).then_inc(b.h("dS"), 16)
                s.dma_start(maa1.rearrange("p (kc x) -> p kc x", kc=8),
                            d["maa_w1T"].rearrange("(kc p) x -> p kc x",
                                                   p=128)
                            ).then_inc(b.h("dS"), 16)
                s.dma_start(dw1.rearrange("p (kc x) -> p kc x", kc=8),
                            d["decay_w1T"].rearrange("(kc p) x -> p kc x",
                                                     p=128)
                            ).then_inc(b.h("dS"), 16)
                s.dma_start(gw1.rearrange("p (kc x) -> p kc x", kc=8),
                            d["gate_w1T"].rearrange("(kc p) x -> p kc x",
                                                    p=128)
                            ).then_inc(b.h("dS"), 16)
                for i2_, nm in enumerate(("kkk_w1T", "aaa_w1T", "ma_w1T",
                                          "mk_w1T")):
                    s.dma_start(
                        bw1.rearrange("p (kc x) -> p kc x", kc=8
                                      )[:, :, i2_ * 16:(i2_ + 1) * 16],
                        d[nm].rearrange("(kc p) x -> p kc x", p=128)
                    ).then_inc(b.h("dS"), 16)
                ld(gw2[:], "gate_w2")
                ld(dw2[:], "decay_w2")
                ld(bw2ka[0:16, :], "kkk_w2")
                ld(bw2ka[32:48, :], "aaa_w2")
                ld(bw2mm[0:16, :], "ma_w2")
                ld(bw2mm[32:48, :], "mk_w2")
                s.dma_start(maa2a[0:32, :], d["maa_w2"][0]
                            ).then_inc(b.h("dS"), 16)
                s.dma_start(maa2a[32:64, :], d["maa_w2"][1]
                            ).then_inc(b.h("dS"), 16)
                s.dma_start(maa2b[0:32, :], d["maa_w2"][2]
                            ).then_inc(b.h("dS"), 16)
                s.dma_start(maa2b[32:64, :], d["maa_w2"][3]
                            ).then_inc(b.h("dS"), 16)
            b.g(fsm)

            stA = {"bufs": [wsA], "sem": "dA", "q": [], "cons": [],
                   "nl": 0, "eng": b.y}
            stB = {"bufs": [wsB], "sem": "dB", "q": [], "cons": [],
                   "nl": 0, "eng": b.g}

            def load_ws(st, nm, half):
                j = st["nl"]
                st["nl"] += 1
                nb = len(st["bufs"])
                buf = st["bufs"][j % nb]
                guard = st["cons"][j - nb] if j >= nb else 0
                cnt = b.r(st["sem"], 16)

                def f(s, nm=nm, half=half, buf=buf, guard=guard,
                      sem=st["sem"]):
                    if guard:
                        s.wait_ge(b.h("pe"), guard)
                    s.dma_start(
                        buf.rearrange("p (kc x) -> p kc x", kc=8),
                        d[nm][:, half * 512:(half + 1) * 512].rearrange(
                            "(kc p) x -> p kc x", p=128)
                    ).then_inc(b.h(sem), 16)
                st["eng"](f)
                st["q"].append((buf, cnt))

            def take_ws(st):
                return st["q"].pop(0)

            def done_ws(st, pe):
                st["cons"].append(pe)

            load_ws(stA, "wrT", 0)
            load_ws(stB, "wkT", 0)

            CC = {}
            Pend = {}
            Send = {}
            Sr3pe = {}
            Sr3x = {}
            Sr4end = {}
            Swo1pe = {}
            stv = {p: {"v": v_init} for p in range(NPAIR)}
            slot_busy = {}
            gy_all_blk = {}
            last_state_blk = {}

            def P_c1(i):
                C = {}
                CC[i] = C
                t0 = i * W
                tsl = slice(t0, t0 + W)
                C["tsl"] = tsl
                par = i % 2
                C["par"] = par
                pprev = Pend.get(i - 1, {})
                C["pprev"] = pprev
                if i > 0:
                    load_ws(stB, "wkT", 0)

                ps_st, g_st = acqP()
                pe_prev = 0
                for m in range(NCH):
                    w0_ = gmax({"dS": dS_done}, pprev if m == 0 else None,
                               g_st if m == 0 else None)
                    cn = cop(w0_, lambda cc, m=m: cc.activation(
                        sbAp[:, ts(m)], R[m][:, tsl], AF.Square))
                    vn = vop(w0_, lambda vv, m=m: vv.tensor_copy(
                        sbBp[:, ts(m)], R[m][:, tsl]))

                    def fst(tt, m=m, ps_st=ps_st):
                        tt.matmul(ps_st[:1, 0:W], onescol_bf[:],
                                  sbBp[:, ts(m)], start=(m == 0),
                                  stop=(m == NCH - 1))
                        return tt.matmul(ps_st[:1, 512:512 + W],
                                         onescol_bf[:], sbAp[:, ts(m)],
                                         start=(m == 0), stop=(m == NCH - 1))
                    pe_prev = top({"sc": cn, "v": vn}, fst)
                pe_stats = pe_prev
                C["pe_stats"] = pe_stats
                mu2 = scrA[0:1, 0:W]
                var2 = scrA[0:1, W:2 * W]
                s2 = scrA[0:1, 2 * W:3 * W]
                t2r = scrA[0:1, 3 * W:4 * W]
                vn = vop({"pe": pe_stats}, lambda vv: vv.tensor_scalar_mul(
                    mu2, ps_st[:1, 0:W], 1.0 / DM))
                vn = vop({"v": vn}, lambda vv: vv.tensor_mul(var2, mu2, mu2))
                vn = vop({"v": vn}, lambda vv: vv.scalar_tensor_tensor(
                    var2, ps_st[:1, 512:512 + W], 1.0 / DM, var2, ALU.mult,
                    ALU.subtract))
                cn = cop({"v": vn}, lambda cc: cc.activation(
                    s2, var2, AF.Sqrt,
                    bias=consts[:1, SC_LNEPS:SC_LNEPS + 1]))
                vn = vop({"sc": cn}, lambda vv: vv.reciprocal(s2, s2))
                vrows = vop({"v": vn}, lambda vv: vv.scalar_tensor_tensor(
                    t2r, mu2, -1.0, s2, ALU.mult, ALU.mult))
                pprot.release({"v": vrows})

                ps_bc, g_bc = acqP()
                g_bc["v"] = max(g_bc.get("v", 0), vrows)

                def fbc(tt, ps_bc=ps_bc):
                    tt.matmul(ps_bc[:, 0:W], ones1_f32[:], s2[:1, :],
                              start=True, stop=True)
                    return tt.matmul(ps_bc[:, 512:512 + W], ones1_f32[:],
                                     t2r[:1, :], start=True, stop=True)
                pe_bc = top(g_bc, fbc)

                vap = 0
                for m in range(NCH):
                    w1_ = {"pe": pe_bc, "v": vap}
                    vap = vop(w1_, lambda vv, m=m, ps_bc=ps_bc:
                              vv.tensor_tensor(scrB[:, ts(m)], R[m][:, tsl],
                                               ps_bc[:, 0:W], ALU.mult))
                    vap = vop({"v": vap}, lambda vv, m=m, ps_bc=ps_bc:
                              vv.tensor_tensor(xnb[:, ts(m)], scrB[:, ts(m)],
                                               ps_bc[:, 512:512 + W],
                                               ALU.add))
                v_ln2 = vap
                pprot.release({"v": v_ln2})
                vxx = vop({"v": v_ln2}, lambda vv: vv.tensor_tensor(
                    v3(xx)[:, :, 1:W], v3(xnb)[:, :, 0:W - 1],
                    v3(xnb)[:, :, 1:W], ALU.subtract))
                vxx = vop({"v": vxx}, lambda vv: vv.tensor_tensor(
                    v3(xx)[:, :, 0:1],
                    bnd.rearrange("p (m o) -> p m o", o=1),
                    v3(xnb)[:, :, 0:1], ALU.subtract))
                C["v_bn"] = vop({"v": vxx}, lambda vv: vv.tensor_copy(
                    bnd[:], xnb[:, W - 1::W]))
                C["vxx"] = vxx
                C["v_ln2"] = v_ln2

            def P_c2(i):
                C = CC[i]
                par = C["par"]
                vxx = C["vxx"]
                vmx = vop({"v": vxx}, lambda vv: vv.tensor_tensor(
                    v3(scrB), v3(xx), cv3(CV_TMX), ALU.mult))
                pmx = pop(gmax({"v": vmx}, C["pprev"]),
                          lambda pp: pp.tensor_tensor(
                              xmv[:, :], scrB[:, :], xnb[:, :], ALU.add))
                ps_lo, g_lo = acqP()
                g_lo["pl"] = max(g_lo.get("pl", 0), pmx)
                g_lo["dS"] = dS_done

                def flo(tt, ps_lo=ps_lo):
                    for half in (0, 1):
                        for kc in range(NCH):
                            ins = tt.matmul(
                                ps_lo[0:64, half * 512:half * 512 + W],
                                maa1[:, kc * 128 + half * 64:
                                     kc * 128 + half * 64 + 64],
                                xmv[:, ts(kc)], start=(kc == 0),
                                stop=(kc == NCH - 1))
                    return ins
                pe_lo = top(g_lo, flo)
                cn1 = cop({"pe": pe_lo}, lambda cc, ps_lo=ps_lo:
                          cc.activation(losb[:, 0:128], ps_lo[0:64, 0:W],
                                        AF.Tanh))
                cn2 = cop({"sc": cn1}, lambda cc, ps_lo=ps_lo:
                          cc.activation(losb[:, 128:256],
                                        ps_lo[0:64, 512:512 + W], AF.Tanh))
                pprot.release({"sc": cn2})

                xm_outs = [xmr, xmw, xmk, xmv]
                xm_ready = {}
                v_lo2_prev = vmx
                prev_pool = pmx
                for gi in range(4):
                    ps_g, g_g = acqP()
                    g_g["sc"] = max(g_g.get("sc", 0), cn2)
                    if gi == 3:
                        g_g["pe"] = max(g_g.get("pe", 0), pe_lo)
                    m2ap = maa2a if gi < 2 else maa2b
                    pbase = 32 * (gi % 2)
                    lslice = losb[pbase:pbase + 32,
                                  (gi // 2) * 128:(gi // 2) * 128 + W]

                    def flo2(tt, m2ap=m2ap, pbase=pbase, lslice=lslice,
                             ps_g=ps_g):
                        for m in range(NCH):
                            ins = tt.matmul(ps_g[:, m * W:(m + 1) * W],
                                            m2ap[pbase:pbase + 32, ts(m)],
                                            lslice, start=True, stop=True)
                        return ins
                    pe_g = top(g_g, flo2)
                    vs = vop({"pe": pe_g, "v": v_lo2_prev, "pl": prev_pool},
                             lambda vv, ps_g=ps_g, gi=gi: vv.tensor_tensor(
                                 v3(scrA), v3(ps_g), cv3(CV_TMAA0 + gi),
                                 ALU.add))
                    vs = vop({"v": vs}, lambda vv: vv.tensor_tensor(
                        scrA[:, :], scrA[:, :], xx[:, :], ALU.mult))
                    pw = {"v": vs}
                    if gi == 3:
                        pw["pl"] = pmx
                        pw["pe"] = pe_g
                    pn = pop(pw, lambda pp, gi=gi:
                             pp.tensor_tensor(xm_outs[gi][:, :], scrA[:, :],
                                              xnb[:, :], ALU.add))
                    xm_ready[gi] = pn
                    prev_pool = pn
                    pprot.release({"v": vs})
                    v_lo2_prev = vs
                C["xm_ready"] = xm_ready
                C["v_lo2"] = v_lo2_prev

                ps_g1, g_g1 = acqP()
                g_g1["dS"] = dS_done
                g_g1["pl"] = max(g_g1.get("pl", 0), xm_ready[0])

                def fg1(tt, ps_g1=ps_g1):
                    for kc in range(NCH):
                        ins = tt.matmul(ps_g1[:, 0:W],
                                        gw1[:, kc * 128:(kc + 1) * 128],
                                        xmr[:, ts(kc)], start=(kc == 0),
                                        stop=(kc == NCH - 1))
                    return ins
                pe_g1 = top(g_g1, fg1)
                c_g1 = cop({"pe": pe_g1}, lambda cc, ps_g1=ps_g1:
                           cc.activation(gt1[:, :], ps_g1[:, 0:W], AF.Tanh))
                pprot.release({"sc": c_g1})
                ps_g2, g_g2 = acqP()
                g_g2["sc"] = max(g_g2.get("sc", 0), c_g1)

                def fg2(tt, ps_g2=ps_g2):
                    for m in range(NCH):
                        ins = tt.matmul(ps_g2[:, m * W:(m + 1) * W],
                                        gw2[:, ts(m)], gt1[:, :],
                                        start=True, stop=True)
                    return ins
                pe_g2 = top(g_g2, fg2)
                v_g2 = vop({"pe": pe_g2}, lambda vv, ps_g2=ps_g2:
                           vv.tensor_copy(gTt[:, :], ps_g2[:, :]))
                pprot.release({"v": v_g2})
                ps_gt, g_gt = acqT()
                g_gt["v"] = max(g_gt.get("v", 0), v_g2)

                def fgt(tt, ps_gt=ps_gt):
                    for m in range(NCH):
                        ins = tt.transpose(ps_gt[:, m * W:(m + 1) * W],
                                           gTt[:, ts(m)], ident_bf[:])
                    return ins
                pe_gt = top(g_gt, fgt)
                C["pe_gt"] = pe_gt
                v_gt = vop(gmax({"pe": pe_gt}, Send.get(i - 2)),
                           lambda vv, ps_gt=ps_gt, par=par:
                           vv.tensor_copy(gtms[par][:, :], ps_gt[:, :1024]))
                ptrot.release({"v": v_gt})
                C["v_gt"] = v_gt

            def P_c3(i):
                C = CC[i]
                par = C["par"]
                xm_ready = C["xm_ready"]
                send2 = Send.get(i - 2)

                def proj_half(st, src_xm, dst, half, xm_n):
                    buf, cnt = take_ws(st)
                    ps_p, g_p = acqP()
                    g_p[st["sem"]] = max(g_p.get(st["sem"], 0), cnt)
                    g_p["pl"] = max(g_p.get("pl", 0), xm_n)

                    def fp(tt, buf=buf, src_xm=src_xm, ps_p=ps_p):
                        for mo in range(4):
                            for kc in range(NCH):
                                ins = tt.matmul(
                                    ps_p[:, mo * W:(mo + 1) * W],
                                    buf[:, kc * 512 + mo * 128:
                                        kc * 512 + (mo + 1) * 128],
                                    src_xm[:, ts(kc)], start=(kc == 0),
                                    stop=(kc == NCH - 1))
                        return ins
                    pe_p = top(g_p, fp)
                    done_ws(st, pe_p)
                    vn = vop(gmax({"pe": pe_p}, send2),
                             lambda vv, dst=dst, ps_p=ps_p, half=half:
                             vv.tensor_copy(
                                 dst[:, half * 512:(half + 1) * 512],
                                 ps_p[:, 0:512]))
                    pprot.release({"v": vn})
                    return pe_p, vn

                pe_r0, v_r0 = proj_half(stA, xmr, rTs[par], 0, xm_ready[0])
                load_ws(stA, "wrT", 1)
                pe_r1, v_r1 = proj_half(stA, xmr, rTs[par], 1, xm_ready[0])
                load_ws(stA, "wvT", 0)
                C["v_r"] = max(v_r0, v_r1)
                pe_k0, v_k0 = proj_half(stB, xmk, kraw, 0, xm_ready[2])
                load_ws(stB, "wkT", 1)
                pe_k1, v_k1 = proj_half(stB, xmk, kraw, 1, xm_ready[2])
                C["pe_k1"] = pe_k1
                C["v_k"] = max(v_k0, v_k1)
                v_v = 0
                for nsI in (0, 1):
                    vbuf, vcnt = take_ws(stA)
                    ps_v, g_v = acqP()
                    g_v["dA"] = max(g_v.get("dA", 0), vcnt)
                    g_v["pl"] = max(g_v.get("pl", 0), xm_ready[3])

                    def fv_(tt, ps_v=ps_v, vbuf=vbuf):
                        for kc in range(NCH):
                            ins = tt.matmul(
                                ps_v[:, 0:512], xmv[:, ts(kc)],
                                vbuf[:, kc * 512:(kc + 1) * 512],
                                start=(kc == 0), stop=(kc == NCH - 1))
                        return ins
                    pe_v = top(g_v, fv_)
                    done_ws(stA, pe_v)
                    v_v = vop(gmax({"pe": pe_v, "v": v_v},
                                   {"pe": Sr3pe.get(i - 1, 0)}),
                              lambda vv, ps_v=ps_v, nsI=nsI, par=par:
                              vv.tensor_copy(vtmI[:, nsI * 512:
                                                  (nsI + 1) * 512],
                                             ps_v[:, 0:512]))
                    pprot.release({"v": v_v})
                    if nsI == 0:
                        load_ws(stA, "wvT", 1)
                    else:
                        load_ws(stA, "wrT", 0)
                C["v_v"] = v_v

                ps_d1, g_d1 = acqP()
                g_d1["pl"] = max(g_d1.get("pl", 0), xm_ready[1])
                g_d1["dS"] = dS_done

                def fd1(tt, ps_d1=ps_d1):
                    for kc in range(NCH):
                        ins = tt.matmul(ps_d1[0:64, 0:W],
                                        dw1[:, kc * 64:(kc + 1) * 64],
                                        xmw[:, ts(kc)], start=(kc == 0),
                                        stop=(kc == NCH - 1))
                    return ins
                pe_d1 = top(g_d1, fd1)
                c_d1 = cop({"pe": pe_d1}, lambda cc, ps_d1=ps_d1:
                           cc.activation(dt1[:, :], ps_d1[0:64, 0:W],
                                         AF.Tanh))
                pprot.release({"sc": c_d1})
                ps_d2, g_d2 = acqP()
                g_d2["sc"] = max(g_d2.get("sc", 0), c_d1)

                def fd2(tt, ps_d2=ps_d2):
                    for m in range(NCH):
                        ins = tt.matmul(ps_d2[:, m * W:(m + 1) * W],
                                        dw2[:, ts(m)], dt1[:, :],
                                        start=True, stop=True)
                    return ins
                pe_d2 = top(g_d2, fd2)
                vn_u = vop({"pe": pe_d2, "pl": xm_ready[3]},
                           lambda vv, ps_d2=ps_d2: vv.tensor_tensor(
                               v3(scrA), v3(ps_d2), cv3(CV_TD), ALU.add))
                pprot.release({"v": vn_u})
                v_m1 = vop({"v": vn_u}, lambda vv:
                           vv.tensor_scalar(scrB[:, :], scrA[:, :], -1.0,
                                            0.0, ALU.mult, ALU.max))
                v_au = vop({"v": v_m1}, lambda vv:
                           vv.scalar_tensor_tensor(scrA[:, :], scrB[:, :],
                                                   2.0, scrA[:, :],
                                                   ALU.mult, ALU.add))
                c_ex = cop({"v": v_au}, lambda cc: cc.activation(
                    scrA[:, :], scrA[:, :], AF.Exp, scale=-1.0))
                c_ln = cop({"sc": c_ex}, lambda cc: cc.activation(
                    scrA[:, :], scrA[:, :], AF.Ln, bias=1.0))
                v_sp = vop({"sc": c_ln}, lambda vv:
                           vv.tensor_tensor(scrA[:, :], scrA[:, :],
                                            scrB[:, :], ALU.add))
                v_wd = vop({"v": v_sp}, lambda vv: vv.tensor_scalar(
                    wdec[:, :], scrA[:, :], -1.0, -0.5, ALU.mult, ALU.add))
                c_lw = cop({"v": v_wd}, lambda cc: cc.activation(
                    scrB[:, :], wdec[:, :], AF.Exp))
                v_lw = vop({"sc": c_lw, "v": C["v_lo2"],
                            "pl": xm_ready[3]},
                           lambda vv: vv.tensor_scalar_mul(
                               lwt[:, :], scrB[:, :], -1.0))
                C["v_lw"] = v_lw
                C["c_lw"] = c_lw

            def P_c4(i):
                C = CC[i]
                par = C["par"]
                xm_ready = C["xm_ready"]
                send2 = Send.get(i - 2)
                sr3 = {"pe": Sr3pe.get(i - 1, 0)}
                sr3x = Sr3x.get(i - 1)
                v_lw = C["v_lw"]

                ps_kb, g_kb = acqP()
                g_kb["pl"] = max(g_kb.get("pl", 0), xm_ready[3])
                g_kb["dS"] = dS_done

                def fkb(tt, ps_kb=ps_kb):
                    for kc in range(NCH):
                        tt.matmul(ps_kb[0:16, 0:W],
                                  bw1[:, kc * 64:kc * 64 + 16],
                                  xmk[:, ts(kc)], start=(kc == 0),
                                  stop=(kc == NCH - 1))
                    for kc in range(NCH):
                        tt.matmul(ps_kb[32:48, 0:W],
                                  bw1[:, kc * 64 + 16:kc * 64 + 32],
                                  xmw[:, ts(kc)], start=(kc == 0),
                                  stop=(kc == NCH - 1))
                    for kc in range(NCH):
                        tt.matmul(ps_kb[0:16, 512:512 + W],
                                  bw1[:, kc * 64 + 32:kc * 64 + 48],
                                  xmw[:, ts(kc)], start=(kc == 0),
                                  stop=(kc == NCH - 1))
                    for kc in range(NCH):
                        ins = tt.matmul(ps_kb[32:48, 512:512 + W],
                                        bw1[:, kc * 64 + 48:kc * 64 + 64],
                                        xmk[:, ts(kc)], start=(kc == 0),
                                        stop=(kc == NCH - 1))
                    return ins
                pe_kb = top(g_kb, fkb)
                c_kb = cop({"pe": pe_kb}, lambda cc, ps_kb=ps_kb:
                           cc.activation(kbt[:, :], ps_kb[0:16, 0:W],
                                         AF.Tanh))
                v_ab2 = vop({"pe": pe_kb}, lambda vv, ps_kb=ps_kb:
                            vv.tensor_copy(abt[:, :], ps_kb[32:48, 0:W]))
                v_ma = vop({"v": v_ab2}, lambda vv, ps_kb=ps_kb:
                           vv.tensor_copy(mabt[:, :],
                                          ps_kb[0:16, 512:512 + W]))
                v_mk = vop({"v": v_ma}, lambda vv, ps_kb=ps_kb:
                           vv.tensor_copy(mkbt[:, :],
                                          ps_kb[32:48, 512:512 + W]))
                pprot.release({"v": v_mk, "sc": c_kb})

                ps_kk, g_kk = acqP()
                g_kk["sc"] = max(g_kk.get("sc", 0), c_kb)
                g_kk["v"] = max(g_kk.get("v", 0), v_mk)

                def fkk(tt, ps_kk=ps_kk):
                    for m in range(NCH):
                        ins = tt.matmul(ps_kk[:, m * W:(m + 1) * W],
                                        bw2ka[0:16, ts(m)], kbt[:, :],
                                        start=True, stop=True)
                    return ins
                pe_kk = top(g_kk, fkk)
                v_kk = vop({"pe": pe_kk, "v": C["v_k"]},
                           lambda vv, ps_kk=ps_kk:
                           vv.tensor_tensor(kkn[:, :], ps_kk[:, :],
                                            kraw[:, :], ALU.add))
                pprot.release({"v": v_kk})
                c_sq = cop({"v": v_kk, "pe": C["pe_stats"]},
                           lambda cc: cc.activation(
                               sbBp[:, :], kkn[:, :], AF.Square))
                ps_nr, g_nr = acqP()
                g_nr["sc"] = max(g_nr.get("sc", 0), c_sq)

                def fnr(tt, ps_nr=ps_nr):
                    for m in range(NCH):
                        ins = tt.matmul(ps_nr[0:16, 0:W],
                                        hselt_bf[:, m * 16:(m + 1) * 16],
                                        sbBp[:, ts(m)], start=(m == 0),
                                        stop=(m == NCH - 1))
                    return ins
                pe_nr = top(g_nr, fnr)
                rnsb = scrB[0:16, 0:128]
                c_nr = cop({"pe": pe_nr, "v": v_lw},
                           lambda cc, ps_nr=ps_nr:
                           cc.activation(rnsb[:, :], ps_nr[0:16, 0:W],
                                         AF.Ln,
                                         bias=consts[0:16,
                                                     SC_EPS30:SC_EPS30 + 1]))
                c_nr = cop({"sc": c_nr}, lambda cc: cc.activation(
                    rnsb[:, :], rnsb[:, :], AF.Exp, scale=-0.5))
                pprot.release({"sc": c_nr})
                v_nr = vop({"sc": c_nr}, lambda vv: vv.tensor_scalar_min(
                    rnsb[:, :], rnsb[:, :], 1e12))
                ps_nb, g_nb = acqP()
                g_nb["v"] = max(g_nb.get("v", 0), v_nr)

                def fnb(tt, ps_nb=ps_nb):
                    for m in range(NCH):
                        ins = tt.matmul(ps_nb[:, m * W:(m + 1) * W],
                                        hsel[:, m * 128:(m + 1) * 128],
                                        rnsb[:, :], start=True, stop=True)
                    return ins
                pe_nb = top(g_nb, fnb)
                v_kkn = vop({"pe": pe_nb}, lambda vv, ps_nb=ps_nb:
                            vv.tensor_tensor(kkn[:, :], kkn[:, :],
                                             ps_nb[:, :], ALU.mult))
                pprot.release({"v": v_kkn})

                def expand_gate(lhs_ap, rhs_sb, cvcol, dst, wait_sc,
                                wait_v=0):
                    ps_e, g_e = acqP()
                    g_e["sc"] = max(g_e.get("sc", 0), wait_sc)
                    if wait_v:
                        g_e["v"] = max(g_e.get("v", 0), wait_v)

                    def fe(tt, lhs_ap=lhs_ap, rhs_sb=rhs_sb, ps_e=ps_e):
                        for m in range(NCH):
                            ins = tt.matmul(ps_e[:, m * W:(m + 1) * W],
                                            lhs_ap[:, ts(m)], rhs_sb[:, :],
                                            start=True, stop=True)
                        return ins
                    pe_e = top(g_e, fe)
                    vn = vop({"pe": pe_e, "sc": C["c_lw"]},
                             lambda vv, ps_e=ps_e, cvcol=cvcol:
                             vv.tensor_tensor(v3(scrB), v3(ps_e),
                                              cv3(cvcol), ALU.add))
                    cs = cop({"v": vn}, lambda cc, dst=dst: cc.activation(
                        dst[:, :], scrB[:, :], AF.Sigmoid))
                    pprot.release({"v": vn})
                    return cs

                c_al = expand_gate(bw2ka[32:48, :], abt[:, :],
                                   CV_AAAAA, alpha, c_kb, v_mk)
                c_mag = expand_gate(bw2mm[0:16, :], mabt[:, :], CV_MISCA,
                                    mag, c_al, v_mk)
                c_mkg = expand_gate(bw2mm[32:48, :], mkbt[:, :], CV_MISCK,
                                    mkg, c_mag, v_mk)

                v_f1 = vop({"sc": c_mag, "pe": C["pe_gt"]},
                           lambda vv: vv.tensor_scalar(
                               sbAp[:, :], mag[:, :], -1.0, 1.0, ALU.mult,
                               ALU.add))
                v_f2 = vop({"v": v_f1, "sc": c_al, "pe": pe_nr},
                           lambda vv: vv.tensor_tensor(
                               sbBp[:, :], alpha[:, :], sbAp[:, :],
                               ALU.mult))
                v_f3 = vop({"v": v_f2}, lambda vv: vv.tensor_tensor(
                    sbAp[:, :], mag[:, :], sbBp[:, :], ALU.add))
                v_ks = vop({"v": v_f3}, lambda vv: vv.tensor_tensor(
                    sbBp[:, :], kraw[:, :], sbAp[:, :], ALU.mult))
                v_e1 = vop({"v": v_ks, "sc": c_mkg},
                           lambda vv: vv.tensor_tensor(
                               sbAp[:, :], wdec[:, :], mkg[:, :], ALU.mult))
                c_e2 = cop({"v": v_e1}, lambda cc: cc.activation(
                    sbAp[:, :], sbAp[:, :], AF.Exp))
                v_kf = vop(gmax({"sc": c_e2}, send2),
                           lambda vv, par=par: vv.tensor_tensor(
                               kfs[par][:, :], sbBp[:, :], sbAp[:, :],
                               ALU.mult))
                C["v_kf"] = v_kf

                v_base = vop({"v": v_lw, "sc": c_mkg},
                             lambda vv: vv.tensor_copy(basec[:, :],
                                                       carry[:, :]))
                scans = v_base
                for m in range(NCH):
                    scans = vop({"v": scans}, lambda vv, m=m:
                                vv.tensor_tensor_scan(
                                    Lct[:, ts(m)], lwt[:, ts(m)],
                                    z128, carry[:, m:m + 1],
                                    ALU.add, ALU.add))
                v_cu = vop({"v": scans}, lambda vv: vv.tensor_copy(
                    carry[:, :], Lct[:, W - 1::W]))
                v_rel = vop({"v": v_cu}, lambda vv: vv.tensor_tensor(
                    v3(Lct), v3(Lct),
                    basec.rearrange("p (m o) -> p m o", o=1).broadcast_to(
                        (128, 8, W)), ALU.subtract))
                c_di = cop({"v": v_rel, "sc": c_e2},
                           lambda cc: cc.activation(
                               dinc[:, :], Lct[:, :], AF.Exp))
                c_dl = cop(gmax({"sc": c_di}, sr3x),
                           lambda cc: cc.activation(
                               dlfI[:, :], Lct[:, W - 1::W], AF.Exp))
                c_dv = cop({"sc": c_dl}, lambda cc: cc.activation(
                    dinv[:, :], Lct[:, :], AF.Exp, scale=-1.0))
                v_de = vop({"v": v_rel}, lambda vv: vv.tensor_tensor(
                    scrA[:, :], Lct[:, :], lwt[:, :], ALU.subtract))
                c_de = cop({"v": max(v_de, v_kf), "sc": c_dv},
                           lambda cc: cc.activation(sbAp[:, :], scrA[:, :],
                                                    AF.Exp))
                C["c_dl"] = c_dl

                v_Qb = pop(gmax({"sc": c_di, "v": C["v_r"]}, sr3),
                           lambda pp, par=par: pp.tensor_tensor(
                               QbtI[:, :], rTs[par][:, :], dinc[:, :],
                               ALU.mult))
                v_Ab = vop(gmax({"sc": c_de, "v": v_kkn}, sr3),
                           lambda vv: vv.tensor_tensor(
                               Abt[:, :], kkn[:, :], sbAp[:, :], ALU.mult))
                v_b1 = vop({"sc": c_dv, "v": v_Ab}, lambda vv:
                           vv.scalar_tensor_tensor(sbBp[:, :], alpha[:, :],
                                                   -1.0, dinv[:, :],
                                                   ALU.mult, ALU.mult))
                v_Bt = vop(gmax({"v": v_b1}, sr3), lambda vv:
                           vv.tensor_tensor(Btt[:, :], kkn[:, :],
                                            sbBp[:, :], ALU.mult))
                p_Kt = pop(gmax({"v": v_kf, "sc": c_dv, "pl": v_Qb},
                                sr3),
                           lambda pp, par=par: pp.tensor_tensor(
                               Ktt[:, :], kfs[par][:, :], dinv[:, :],
                               ALU.mult))
                dl3 = v3(dinc)[:, :, W - 1:W].broadcast_to((128, 8, W))
                v_Bh = vop({"v": v_Bt}, lambda vv: vv.tensor_tensor(
                    v3(sbAp), v3(Btt), dl3, ALU.mult))
                p_Kh = pop({"pl": p_Kt, "v": v_Bt}, lambda pp:
                           pp.tensor_tensor(
                               v3(sbBp), v3(Ktt), dl3, ALU.mult))

                pe_tl = 0
                for src, half, gwait in ((sbAp, 0, {"v": v_Bh}),
                                         (sbBp, 1, {"pl": p_Kh})):
                    ps_t, g_t = acqT()
                    for k2_, v2_ in gwait.items():
                        g_t[k2_] = max(g_t.get(k2_, 0), v2_)
                    if pe_tl:
                        g_t["pe"] = max(g_t.get("pe", 0), pe_tl)

                    def ftr(tt, src=src, ps_t=ps_t):
                        for m in range(NCH):
                            ins = tt.transpose(ps_t[:, m * W:(m + 1) * W],
                                               src[:, ts(m)], ident_bf[:])
                        return ins
                    pe_tl = top(g_t, ftr)
                    vt = vop(gmax({"pe": pe_tl}, sr3),
                             lambda vv, half=half, ps_t=ps_t:
                             vv.tensor_copy(
                                 BKhI[:, half * DM:(half + 1) * DM],
                                 ps_t[:, :1024]))
                    ptrot.release({"v": vt})
                C["pe_BK"] = pe_tl

            RNDS = [[0, 1, 2], [3, 4, 5], [6, 7]]

            def sl_ap(sl, base, w):
                return slb[sl][:, base:base + w]

            def S_solve_round(i, rnd):
                par = i % 2
                pend_i = Pend[i]
                pairs = RNDS[rnd]
                pctx = {}
                for p in pairs:
                    gp = i * NPAIR + p
                    sl = gp % NSL
                    slg = dict(slot_busy.get(sl, {}))
                    csl = slice(p * W, (p + 1) * W)
                    X = dict(p=p, sl=sl, slg=slg, csl=csl)
                    X["Af"] = [Abt[pb:pb + 64, csl] for pb in (0, 64)]
                    X["Bf"] = [Btt[pb:pb + 64, csl] for pb in (0, 64)]
                    X["Kf"] = [Ktt[pb:pb + 64, csl] for pb in (0, 64)]
                    X["Qf"] = [QbtI[pb:pb + 64, csl] for pb in (0, 64)]
                    X["STp"] = [STb[pb:pb + 64, p * 64:(p + 1) * 64]
                                for pb in (0, 64)]
                    X["Vt"] = [vtmI[:, (2 * p + hx) * 64:
                                    (2 * p + hx) * 64 + 64]
                               for hx in (0, 1)]
                    X["Bh"] = [BKhI[:, (2 * p + hx) * 64:
                                    (2 * p + hx) * 64 + 64]
                               for hx in (0, 1)]
                    X["Kh"] = [BKhI[:, DM + (2 * p + hx) * 64:
                                    DM + (2 * p + hx) * 64 + 64]
                               for hx in (0, 1)]
                    pctx[p] = X

                for p in pairs:
                    X = pctx[p]
                    sl = X["sl"]
                    ps1, g1 = acqS()
                    pe1 = top(gmax(g1, X["slg"], pend_i),
                              lambda tt, X=X, ps1=ps1: [
                                  tt.matmul(ps1[:, hx * 512:hx * 512 + 128],
                                            X["Af"][hx], X["Bf"][hx],
                                            start=True, stop=True)
                                  for hx in (0, 1)][-1])
                    vN = vop(gmax({"pe": pe1}, X["slg"]),
                             lambda vv, sl=sl, ps1=ps1: vv.tensor_tensor(
                                 b2(sl_ap(sl, C_PL0, 256), 128),
                                 b2(ps1[:, 0:1024], 128),
                                 b2(mask_sl2[:, :], 128), ALU.mult))
                    sprot.release({"v": vN})
                    X["gN"] = {"v": vN}
                    ps2, g2 = acqS()

                    def f13(tt, X=X, ps2=ps2):
                        for hx in (0, 1):
                            tt.matmul(ps2[:, hx * 512:hx * 512 + 128],
                                      X["Bf"][hx], X["Af"][hx],
                                      start=True, stop=True)
                            ins = tt.matmul(ps2[:, hx * 512 + 128:
                                                hx * 512 + 256],
                                            X["Bf"][hx], X["Qf"][hx],
                                            start=True, stop=True)
                        return ins
                    pe2 = top(gmax(g2, X["slg"], pend_i), f13)
                    v13 = vop(gmax({"pe": pe2}, X["slg"]),
                              lambda vv, sl=sl, ps2=ps2: vv.tensor_tensor(
                                  b2(sl_ap(sl, C_NT, 512), 256),
                                  b2(ps2[:, 0:1024], 256),
                                  b2(mask_t2[:, :], 256), ALU.mult))
                    sprot.release({"v": v13})
                    X["g13"] = {"v": v13}
                    ps3, g3 = acqS()

                    def f24(tt, X=X, ps3=ps3):
                        for hx in (0, 1):
                            tt.matmul(ps3[:, hx * 512:hx * 512 + 128],
                                      X["Kf"][hx], X["Af"][hx],
                                      start=True, stop=True)
                            ins = tt.matmul(ps3[:, hx * 512 + 128:
                                                hx * 512 + 256],
                                            X["Kf"][hx], X["Qf"][hx],
                                            start=True, stop=True)
                        return ins
                    pe3 = top(gmax(g3, X["slg"], pend_i), f24)
                    v24 = vop(gmax({"pe": pe3}, X["slg"]),
                              lambda vv, sl=sl, ps3=ps3: vv.tensor_tensor(
                                  b2(sl_ap(sl, C_T24, 512), 256),
                                  b2(ps3[:, 0:1024], 256),
                                  b2(mask_t2[:, :], 256), ALU.mult))
                    sprot.release({"v": v24})
                    X["g24"] = {"v": v24}

                for p in pairs:
                    X = pctx[p]
                    sl = X["sl"]
                    m0v = slb[sl][:, C_NT:C_NT + 512].rearrange(
                        "p (g c) -> p g c", c=128)[:, 0::2, :]
                    r0v = slb[sl][:, C_RA:C_RA + 256].rearrange(
                        "p (g c) -> p g c", c=128)
                    idv = ident_bf[:, :].rearrange(
                        "p (g c) -> p g c", g=1).broadcast_to((128, 2, 128))
                    vR = vop(gmax(X["g13"], X["slg"]),
                             lambda vv, m0v=m0v, r0v=r0v, idv=idv:
                             vv.tensor_tensor(r0v, m0v, idv, ALU.add))
                    X["gR"] = {"v": vR}

                for lev in range(1, 7):
                    dstc = C_SQA if lev % 2 == 1 else C_SQB
                    for p in pairs:
                        X = pctx[p]
                        sl = X["sl"]
                        if lev == 1:
                            Psrc = [sl_ap(sl, C_PL0 + hx * 128, 128)
                                    for hx in (0, 1)]
                            Msrc = [sl_ap(sl, C_NT + hx * 256, 128)
                                    for hx in (0, 1)]
                            gsrc = gmax(X["gN"], X["g13"])
                        else:
                            sc_ = C_SQA if (lev - 1) % 2 == 1 else C_SQB
                            Psrc = [sl_ap(sl, sc_ + hx * 128, 128)
                                    for hx in (0, 1)]
                            Msrc = [sl_ap(sl, sc_ + 256 + hx * 128, 128)
                                    for hx in (0, 1)]
                            gsrc = X["gPM"]
                        ps, g = acqS()

                        def fsq(tt, Psrc=Psrc, Msrc=Msrc, ps=ps):
                            for hx in (0, 1):
                                tt.matmul(ps[:, hx * 128:(hx + 1) * 128],
                                          Msrc[hx], Psrc[hx],
                                          start=True, stop=True)
                            for hx in (0, 1):
                                ins = tt.matmul(
                                    ps[:, 256 + hx * 128:384 + hx * 128],
                                    Psrc[hx], Msrc[hx],
                                    start=True, stop=True)
                            return ins
                        pe_sq = top(gmax(g, gsrc, X["slg"]), fsq)
                        wv = gmax({"pe": pe_sq}, X["slg"])
                        if (p + lev) % 2 == 0:
                            gPM = {"v": vop(wv, lambda vv, sl=sl, ps=ps,
                                            dstc=dstc: vv.tensor_copy(
                                                sl_ap(sl, dstc, 512),
                                                ps[:, 0:512]))}
                        else:
                            gPM = {"sc": cop(wv, lambda cc, sl=sl, ps=ps,
                                             dstc=dstc: cc.copy(
                                                 sl_ap(sl, dstc, 512),
                                                 ps[:, 0:512]))}
                        sprot.release(gPM)
                        X["gPM"] = gPM
                    rdst = C_RB if lev % 2 == 1 else C_RA
                    rsrc = C_RA if lev % 2 == 1 else C_RB
                    for p in pairs:
                        X = pctx[p]
                        sl = X["sl"]
                        ps, g = acqS()

                        def fpr(tt, X=X, sl=sl, ps=ps, dstc=dstc,
                                rsrc=rsrc):
                            for hx in (0, 1):
                                tt.matmul(ps[:, hx * 128:(hx + 1) * 128],
                                          ident_bf[:, :],
                                          sl_ap(sl, rsrc + hx * 128, 128),
                                          start=True, stop=False)
                                ins = tt.matmul(
                                    ps[:, hx * 128:(hx + 1) * 128],
                                    sl_ap(sl, dstc + hx * 128, 128),
                                    sl_ap(sl, rsrc + hx * 128, 128),
                                    start=False, stop=True)
                            return ins
                        pe_pr = top(gmax(g, X["gPM"], X["gR"], X["slg"]),
                                    fpr)
                        wv = gmax({"pe": pe_pr}, X["slg"])
                        if (p + lev) % 2 == 1:
                            gR = {"v": vop(wv, lambda vv, sl=sl, ps=ps,
                                           rdst=rdst: vv.tensor_copy(
                                               sl_ap(sl, rdst, 256),
                                               ps[:, 0:256]))}
                        else:
                            gR = {"sc": cop(wv, lambda cc, sl=sl, ps=ps,
                                            rdst=rdst: cc.copy(
                                                sl_ap(sl, rdst, 256),
                                                ps[:, 0:256]))}
                        sprot.release(gR)
                        X["gR"] = gR

                for p in pairs:
                    X = pctx[p]
                    sl = X["sl"]
                    ps, g = acqS()

                    def fw0(tt, X=X, sl=sl, ps=ps):
                        for hx in (0, 1):
                            tt.matmul(ps[:, hx * 512:hx * 512 + 64],
                                      X["Af"][hx], X["STp"][hx],
                                      start=True, stop=False)
                            ins = tt.matmul(
                                ps[:, hx * 512:hx * 512 + 64],
                                sl_ap(sl, C_T24 + hx * 256, 128),
                                X["Vt"][hx], start=False, stop=True)
                        return ins
                    pe_w0 = top(gmax(g, X["g24"], X["slg"], stv[p],
                                     pend_i), fw0)
                    gW0 = {"v": vop(gmax({"pe": pe_w0}, X["slg"]),
                                    lambda vv, sl=sl, ps=ps:
                                    vv.tensor_copy(
                                        b2(sl_ap(sl, C_W0, 128), 64),
                                        b2(ps[:, 0:1024], 64)))}
                    sprot.release(gW0)
                    ps2, g2 = acqS()

                    def fx(tt, sl=sl, ps2=ps2):
                        for hx in (0, 1):
                            ins = tt.matmul(
                                ps2[:, hx * 512:hx * 512 + 64],
                                sl_ap(sl, C_RA + hx * 128, 128),
                                sl_ap(sl, C_W0 + hx * 64, 64),
                                start=True, stop=True)
                        return ins
                    pe_x = top(gmax(g2, gW0, X["gR"], X["slg"]), fx)
                    gX = {"sc": cop(gmax({"pe": pe_x}, X["slg"]),
                                    lambda cc, sl=sl, ps2=ps2:
                                    cc.copy(b2(sl_ap(sl, C_XC, 128), 64),
                                            b2(ps2[:, 0:1024], 64)))}
                    sprot.release(gX)
                    X["gX"] = gX
                    X["pe_w0"] = pe_w0

                for p in pairs:
                    X = pctx[p]
                    sl = X["sl"]
                    ps, g = acqS()

                    def fy(tt, X=X, sl=sl, ps=ps):
                        for hx in (0, 1):
                            tt.matmul(ps[:, hx * 512:hx * 512 + 64],
                                      X["Qf"][hx], X["STp"][hx],
                                      start=True, stop=False)
                            tt.matmul(ps[:, hx * 512:hx * 512 + 64],
                                      sl_ap(sl, C_NT + hx * 256 + 128, 128),
                                      sl_ap(sl, C_XC + hx * 64, 64),
                                      start=False, stop=False)
                            ins = tt.matmul(
                                ps[:, hx * 512:hx * 512 + 64],
                                sl_ap(sl, C_T24 + hx * 256 + 128, 128),
                                X["Vt"][hx], start=False, stop=True)
                        return ins
                    pe_y = top(gmax(g, X["gX"], X["g13"], X["g24"],
                                    stv[X["p"]], X["slg"]), fy)
                    gY = {"v": vop(gmax({"pe": pe_y}, X["slg"],
                                        Send.get(i - 1)),
                                   lambda vv, ps=ps, p=p: vv.tensor_copy(
                                       b2(ytm[:, p * 128:(p + 1) * 128],
                                          64),
                                       b2(ps[:, 0:1024], 64)))}
                    sprot.release(gY)
                    X["gY"] = gY
                    X["pe_y"] = pe_y
                    gy_all_blk.setdefault(i, []).append(gY)

                for p in pairs:
                    X = pctx[p]
                    sl = X["sl"]
                    vdd = vop(gmax({"sc": CC[i]["c_dl"]}, X["gY"],
                                   X["slg"]),
                              lambda vv, sl=sl, p=p, par=par:
                              vv.tensor_scalar(
                                  sl_ap(sl, C_DD, 64), ident2x[:, :],
                                  dlfI[:, p:p + 1], None, ALU.mult))
                    ps, g = acqS()

                    def fs6(tt, X=X, sl=sl, ps=ps):
                        for hx, pb in ((0, 0), (1, 64)):
                            tt.matmul(ps[pb:pb + 64,
                                         hx * 512:hx * 512 + 64],
                                      slb[sl][pb:pb + 64, C_DD:C_DD + 64],
                                      X["STp"][hx], start=True, stop=False)
                            tt.matmul(ps[pb:pb + 64,
                                         hx * 512:hx * 512 + 64],
                                      X["Bh"][hx],
                                      sl_ap(sl, C_XC + hx * 64, 64),
                                      start=False, stop=False)
                            ins = tt.matmul(ps[pb:pb + 64,
                                            hx * 512:hx * 512 + 64],
                                            X["Kh"][hx], X["Vt"][hx],
                                            start=False, stop=True)
                        return ins
                    pe_st = top(gmax(g, {"v": vdd}, X["gX"],
                                     {"pe": X["pe_y"]}, pend_i, X["slg"]),
                                fs6)

                    def fgs(cc, ps=ps, p=p):
                        cc.copy(STb[0:64, p * 64:(p + 1) * 64],
                                ps[0:64, 0:64])
                        return cc.copy(STb[64:128, p * 64:(p + 1) * 64],
                                       ps[64:128, 512:576])
                    gS = {"sc": cop({"pe": pe_st}, fgs)}
                    sprot.release(gS)
                    stv[p] = gS
                    slot_busy[X["sl"]] = {"pe": pe_st}
                    last_state_blk[i] = pe_st

            def S_tail(i):
                par = i % 2
                gy_all = gmax(*gy_all_blk[i])
                sp4 = Sr4end.get(i - 1, {})
                y3 = ytm.rearrange("p (g i2) -> p g i2", g=16)
                c_ys = cop(gmax(gy_all, {"pe": last_state_blk[i]}, sp4),
                           lambda cc: cc.activation(sbAs[:, :DM], ytm[:, :],
                                                    AF.Square))
                sbA3 = sbAs[:, :DM].rearrange("p (g i2) -> p g i2", g=16)
                musum = scrS[:, 0:16]
                sqsum = scrS[:, 16:32]
                muv = scrS[:, 32:48]
                mus2 = scrS[:, 48:64]
                varv = scrS[:, 64:80]
                rstd = scrS[:, 80:96]
                vgs = vop(gmax({"sc": c_ys}, gy_all, sp4),
                          lambda vv: vv.reduce_sum(
                              musum.rearrange("p (g o) -> p g o", o=1), y3,
                              axis=mybir.AxisListType.X))
                vgs2 = vop({"v": vgs}, lambda vv: vv.reduce_sum(
                    sqsum.rearrange("p (g o) -> p g o", o=1), sbA3,
                    axis=mybir.AxisListType.X))
                vgs = vop({"v": vgs2}, lambda vv: vv.tensor_scalar_mul(
                    muv, musum, 1.0 / HS))
                vgs = vop({"v": vgs}, lambda vv: vv.tensor_mul(
                    mus2, muv, muv))
                vgs = vop({"v": vgs}, lambda vv: vv.scalar_tensor_tensor(
                    varv, sqsum, 1.0 / HS, mus2, ALU.mult, ALU.subtract))
                c_rs = cop({"v": vgs}, lambda cc: cc.activation(
                    rstd, varv, AF.Ln,
                    bias=consts[:, SC_GNEPS:SC_GNEPS + 1]))
                c_rs = cop({"sc": c_rs}, lambda cc: cc.activation(
                    rstd, rstd, AF.Exp, scale=-0.5))
                mu3 = muv.rearrange("p (g o) -> p g o", o=1).broadcast_to(
                    (128, 16, HS))
                rs3 = rstd.rearrange("p (g o) -> p g o", o=1).broadcast_to(
                    (128, 16, HS))
                vyn = vop({"sc": c_rs, "v": vgs},
                          lambda vv: vv.tensor_tensor(
                    sbBs[:, :DM].rearrange("p (g i2) -> p g i2", g=16), y3,
                    mu3, ALU.subtract))
                vyn = vop({"v": vyn}, lambda vv: vv.tensor_tensor(
                    sbBs[:, :DM].rearrange("p (g i2) -> p g i2", g=16),
                    sbBs[:, :DM].rearrange("p (g i2) -> p g i2", g=16),
                    rs3, ALU.mult))
                vyn = vop({"v": vyn}, lambda vv: vv.tensor_tensor(
                    sbBs[:, :DM], sbBs[:, :DM], lnw_tm[:, :], ALU.mult))
                vyn = vop({"v": vyn}, lambda vv: vv.tensor_tensor(
                    sbBs[:, :DM], sbBs[:, :DM], lnb_tm[:, :], ALU.add))
                prk = pop(gmax({"v": max(vyn, vgs2)}, Pend[i], sp4),
                          lambda pp, par=par: pp.tensor_tensor(
                              sbAs[:, :], rTs[par][:, :], kfs[par][:, :],
                              ALU.mult))
                prk = pop({"pl": prk}, lambda pp: pp.tensor_tensor(
                    v3(sbAs), v3(sbAs), cv3(CV_FAAAA), ALU.mult))
                ps_rk, g_rk = acqS()
                g_rk["pl"] = max(g_rk.get("pl", 0), prk)

                def frk(tt, ps_rk=ps_rk):
                    for m in range(NCH):
                        ins = tt.matmul(ps_rk[0:16, 0:W],
                                        hselt_bf[:, m * 16:(m + 1) * 16],
                                        sbAs[:, ts(m)], start=(m == 0),
                                        stop=(m == NCH - 1))
                    return ins
                pe_rk = top(g_rk, frk)
                rksb = scrS.bitcast(BF16)[0:16, 192:320]
                rktm = scrS.bitcast(BF16)[:, 320:336]
                v_rk2 = vop(gmax({"pe": pe_rk}, sp4),
                            lambda vv, ps_rk=ps_rk:
                            vv.tensor_copy(rksb[:, :], ps_rk[0:16, 0:W]))
                sprot.release({"v": v_rk2})
                ps_rt, g_rt = acqT()
                g_rt["v"] = max(g_rt.get("v", 0), v_rk2)
                pe_rt = top(g_rt, lambda tt, ps_rt=ps_rt: tt.transpose(
                    ps_rt[:, 0:16], rksb[:, :], ident_bf[0:16, 0:16]))
                v_rt = vop({"pe": pe_rt}, lambda vv, ps_rt=ps_rt:
                           vv.tensor_copy(rktm[:, :], ps_rt[:, 0:16]))
                ptrot.release({"v": v_rt})
                rk3 = rktm.rearrange("p (g o) -> p g o", o=1).broadcast_to(
                    (128, 16, HS))
                v3tm = vtmI.rearrange("p (g i2) -> p g i2", g=16)
                vrkv = vop({"v": v_rt}, lambda vv: vv.tensor_tensor(
                    sbAs[:, :DM].rearrange("p (g i2) -> p g i2", g=16),
                    v3tm, rk3, ALU.mult))
                vfin = vop({"v": vrkv}, lambda vv: vv.tensor_tensor(
                    sbBs[:, :DM], sbBs[:, :DM], sbAs[:, :DM], ALU.add))
                vyg = vop(gmax({"v": max(vfin, CC[i]["v_gt"])}),
                          lambda vv, par=par: vv.tensor_tensor(
                              sbBs[:, :DM], sbBs[:, :DM], gtms[par][:, :],
                              ALU.mult))
                ps_yf, g_yf = acqT()
                g_yf["v"] = max(g_yf.get("v", 0), vyg)

                def fyf(tt, ps_yf=ps_yf):
                    for m in range(NCH):
                        ins = tt.transpose(ps_yf[:, m * W:(m + 1) * W],
                                           sbBs[:, ts(m)], ident_bf[:])
                    return ins
                pe_yf = top(g_yf, fyf)
                v_yf = vop(gmax({"pe": pe_yf}, sp4),
                           lambda vv, ps_yf=ps_yf:
                           vv.tensor_copy(ygfm[:, :], ps_yf[:, :1024]))
                ptrot.release({"v": v_yf})

                tsl = CC[i]["tsl"]
                pe_wo = 0
                for half in (0, 1):
                    load_ws(stB, "woT", half)
                    wbuf, dwo = take_ws(stB)
                    ps_wo, g_wo = acqS()
                    g_wo["dB"] = max(g_wo.get("dB", 0), dwo)
                    g_wo["v"] = max(g_wo.get("v", 0), v_yf)

                    def fwo2(tt, ps_wo=ps_wo, wbuf=wbuf):
                        for mo in range(4):
                            for kc in range(NCH):
                                ins = tt.matmul(
                                    ps_wo[:, mo * W:(mo + 1) * W],
                                    wbuf[:, kc * 512 + mo * 128:
                                        kc * 512 + (mo + 1) * 128],
                                    ygfm[:, ts(kc)], start=(kc == 0),
                                    stop=(kc == NCH - 1))
                        return ins
                    pe_wo = top(g_wo, fwo2)
                    done_ws(stB, pe_wo)
                    if half == 1:
                        Swo1pe[i] = pe_wo
                    v_ra = 0
                    for mo in range(4):
                        m = half * 4 + mo
                        v_ra = vop({"pe": pe_wo, "v": v_ra},
                                   lambda vv, m=m, mo=mo, ps_wo=ps_wo,
                                   tsl=tsl: vv.tensor_tensor(
                                       R[m][:, tsl], R[m][:, tsl],
                                       ps_wo[:, mo * W:(mo + 1) * W],
                                       ALU.add))
                    sprot.release({"v": v_ra})
                Sr4end[i] = snap()

            maxsegs = (4 * (NBK + 1) if pers is None
                       else int(round(4 * pers)))
            nseg_done = 0
            for per in range(NBK + 1):
                for seg in range(4):
                    if nseg_done >= maxsegs:
                        break
                    nseg_done += 1
                    lastseg = (nseg_done == maxsegs)
                    if per > 0 and not (lastseg and lastskip == "s"):
                        if seg < 3:
                            S_solve_round(per - 1, seg)
                            if seg == 2:
                                Sr3pe[per - 1] = b.n("pe")
                                Sr3x[per - 1] = snap()
                        else:
                            S_tail(per - 1)
                            Send[per - 1] = snap()
                    if per < NBK and not (lastseg and lastskip == "p"):
                        [P_c1, P_c2, P_c3, P_c4][seg](per)
                if per < NBK:
                    Pend[per] = snap()

            _da, _db, _ds = b.n("dA"), b.n("dB"), b.n("dS")

            def fdrain(s, _da=_da, _db=_db, _ds=_ds):
                s.wait_ge(b.h("dA"), _da)
                s.wait_ge(b.h("dB"), _db)
                s.wait_ge(b.h("dS"), _ds)
            b.g(fdrain)
            b.barrier()

    emit_ffn(1)
    dump("ffn1", [R[m][:, :T] for m in range(NCH)])
    if stop_after not in ("ln1", "ffn1"):
        emit_attn_pipe()
        dump("attn_x2", [R[m][:, :T] for m in range(NCH)])
    if stop_after not in ("ln1", "ffn1", "attn"):
        emit_ffn(2)

    tgt = b.r("dY", 16 * NCH)

    def fstore(s, tgt=tgt):
        for m in range(NCH):
            s.dma_start(out_d[ts(m), :], R[m][:, :T]).then_inc(b.h("dY"), 16)
        s.wait_ge(b.h("dY"), tgt)
    b.y(fstore)
    b.barrier()
    return nc, d, dbg_outs


def _consts_np(T):
    NTC = T // 128
    c = {}
    c["ident_bf"] = np.eye(128, dtype=ml_dtypes.bfloat16)
    zid = np.zeros((128, 256), ml_dtypes.bfloat16)
    zid[:, 128:] = np.eye(128, dtype=ml_dtypes.bfloat16)
    c["zid_bf"] = zid
    c["ones1_bf"] = np.ones((1, 128), ml_dtypes.bfloat16)
    c["ones1_f32"] = np.ones((1, 128), np.float32)
    c["onescol_bf"] = np.ones((128, 1), ml_dtypes.bfloat16)
    c["mask_sl"] = np.tril(np.ones((128, 128), np.float32), -1)
    mt = np.zeros((128, 256), np.float32)
    mt[:, :128] = np.triu(np.ones((128, 128), np.float32), 1)
    mt[:, 128:] = np.triu(np.ones((128, 128), np.float32), 0)
    c["mask_t"] = mt
    c["mask_sl2"] = np.concatenate([c["mask_sl"], c["mask_sl"]],
                                   axis=1).astype(ml_dtypes.bfloat16)
    c["mask_t2"] = np.concatenate([mt, mt], axis=1).astype(ml_dtypes.bfloat16)
    i2 = np.zeros((128, 64), ml_dtypes.bfloat16)
    i2[0:64, :] = np.eye(64, dtype=ml_dtypes.bfloat16)
    i2[64:128, :] = np.eye(64, dtype=ml_dtypes.bfloat16)
    c["ident2x"] = i2
    hs = np.zeros((16, NCH * 128), np.float32)
    hst = np.zeros((128, NCH * 16), np.float32)
    for m in range(NCH):
        for j in range(128):
            h = 2 * m + (j >= 64)
            hs[h, m * 128 + j] = 1.0
            hst[j, m * 16 + h] = 1.0
    c["hsel"] = hs
    c["hselt_bf"] = hst.astype(ml_dtypes.bfloat16)
    sc = np.zeros((128, NSC), np.float32)
    sc[:, SC_LNEPS] = LN_EPS
    sc[:, SC_GNEPS] = GN_EPS
    sc[:, SC_KNEPS] = 1e-12
    sc[:, SC_EPS30] = 1e-30
    c["consts"] = sc
    return c


def prep_inputs(inputs, T):
    f32 = np.float32

    def A(k):
        return np.ascontiguousarray(np.asarray(inputs[k], f32))
    m = dict(_consts_np(T))
    cv = np.zeros((DM, NVC), f32)
    cv[:, CV_TMX] = A("L0_attn_time_maa_x")
    cv[:, CV_TD] = A("L0_attn_time_decay")
    cv[:, CV_FAAAA] = A("L0_attn_time_faaaa")
    cv[:, CV_AAAAA] = A("L0_attn_time_aaaaa")
    cv[:, CV_MISCA] = A("L0_attn_time_misc_a")
    cv[:, CV_MISCK] = A("L0_attn_time_misc_k")
    cv[:, CV_LNW] = A("L0_attn_ln_w")
    cv[:, CV_LNB] = A("L0_attn_ln_b")
    tm = A("L0_attn_time_maa").reshape(4, DM)
    for g in range(4):
        cv[:, CV_TMAA0 + g] = tm[g]
    m["chanvec"] = cv
    m["lnw_tm"] = np.ascontiguousarray(
        np.broadcast_to(A("L0_attn_ln_w")[None, :], (128, DM)))
    m["lnb_tm"] = np.ascontiguousarray(
        np.broadcast_to(A("L0_attn_ln_b")[None, :], (128, DM)))
    m["wg"] = A("L0_ffn_Wg")
    bf = ml_dtypes.bfloat16
    m["ws1"] = A("L0_ffn_Ws1").astype(bf)
    m["ws3"] = A("L0_ffn_Ws3").astype(bf)
    m["ws2"] = A("L0_ffn_Ws2").astype(bf)
    m["we1"] = A("L0_ffn_We1").astype(bf)
    m["we3"] = A("L0_ffn_We3").astype(bf)
    m["we2"] = A("L0_ffn_We2").astype(bf)
    for nm in ("wrT", "wkT", "wvT", "woT"):
        srck = {"wrT": "L0_attn_Wr", "wkT": "L0_attn_Wk",
                "wvT": "L0_attn_Wv", "woT": "L0_attn_Wo"}[nm]
        m[nm] = np.ascontiguousarray(A(srck).T).astype(ml_dtypes.bfloat16)
    for nm, src in [("maa_w1T", "L0_attn_maa_w1"),
                    ("decay_w1T", "L0_attn_decay_w1"),
                    ("kkk_w1T", "L0_attn_kkk_w1"),
                    ("aaa_w1T", "L0_attn_aaa_w1"),
                    ("ma_w1T", "L0_attn_ma_w1"),
                    ("mk_w1T", "L0_attn_mk_w1"),
                    ("gate_w1T", "L0_attn_gate_w1")]:
        m[nm] = np.ascontiguousarray(A(src).T)
    for nm, src in [("maa_w2", "L0_attn_maa_w2"),
                    ("decay_w2", "L0_attn_decay_w2"),
                    ("kkk_w2", "L0_attn_kkk_w2"),
                    ("aaa_w2", "L0_attn_aaa_w2"),
                    ("ma_w2", "L0_attn_ma_w2"),
                    ("mk_w2", "L0_attn_mk_w2"),
                    ("gate_w2", "L0_attn_gate_w2")]:
        m[nm] = A(src)
    return m


def kernel(**inputs):
    x = np.asarray(inputs["L0_attn_x"], np.float32)
    Bn, T, _ = x.shape
    shared = prep_inputs(inputs, T)
    nc, d, dbg_outs = build_program(T)
    in_maps = []
    for c in range(Bn):
        im = dict(shared)
        im["x_fm"] = np.ascontiguousarray(x[c].T)
        in_maps.append(im)
    res = run_bass_kernel_spmd(nc, in_maps, core_ids=list(range(Bn)))
    out = np.stack([np.ascontiguousarray(r["out_fm"].T)
                    for r in res.results])
    return out.astype(np.float32)


# version 14 (final)
# speedup vs baseline: 1.0225x; 1.0225x over previous
import numpy as np
import ml_dtypes
import concourse.bass as bass
import concourse.mybir as mybir
from concourse.bass_utils import run_bass_kernel_spmd

F32 = mybir.dt.float32
BF16 = mybir.dt.bfloat16
AF = mybir.ActivationFunctionType
ALU = mybir.AluOpType

DM = 1024
NCH = DM // 128
FF = 2048
NEXP = 8
H, HS = 16, 64
L = 128
LN_EPS = 1e-5
GN_EPS = 0.00064

CV_TMX, CV_TD, CV_FAAAA, CV_AAAAA, CV_MISCA, CV_MISCK, CV_LNW, CV_LNB = range(8)
CV_TMAA0 = 8
NVC = 12
SC_LNEPS, SC_GNEPS, SC_KNEPS, SC_EPS30, SC_ZERO = range(5)
NSC = 5


def ts(i, n=128):
    return slice(i * n, (i + 1) * n)


class PsumRot:

    def __init__(self, n):
        self.n = n
        self.uses = 0
        self.frees = []

    def acquire(self):
        slot = self.uses % self.n
        guard = (self.frees[self.uses - self.n]
                 if self.uses >= self.n else None)
        self.uses += 1
        return slot, guard

    def release(self, guards):
        self.frees.append(dict(guards))


class B:

    def __init__(self, nc):
        self.nc = nc
        self.q = {k: [] for k in ("tensor", "vector", "scalar", "gpsimd",
                                  "sync")}
        self._sems = {}

    def sem(self, name):
        if name not in self._sems:
            self._sems[name] = [self.nc.semaphore("sem_" + name).__enter__(),
                                0]
        return self._sems[name]

    def n(self, name):
        return self.sem(name)[1]

    def r(self, name, k=1):
        s = self.sem(name)
        s[1] += k
        return s[1]

    def h(self, name):
        return self.sem(name)[0]

    def t(self, fn):
        self.q["tensor"].append(fn)

    def v(self, fn):
        self.q["vector"].append(fn)

    def c(self, fn):
        self.q["scalar"].append(fn)

    def g(self, fn):
        self.q["gpsimd"].append(fn)

    def y(self, fn):
        self.q["sync"].append(fn)

    def barrier(self):
        nc = self.nc
        if not any(self.q.values()):
            return
        with nc.Block() as blk:
            for eng in ("tensor", "vector", "scalar", "gpsimd", "sync"):
                fns = self.q[eng]
                if fns:
                    getattr(blk, eng)(lambda e, fns=fns: [f(e) for f in fns])
        self.q = {k: [] for k in self.q}


def build_program(T=2048, dbg=(), stop_after=None, pers=None, lastskip=None, sph=9):
    NTC = T // 128
    NS = T // 512
    nc = bass.Bass("TRN2", target_bir_lowering=False)
    b = B(nc)
    dbg = set(dbg)
    dbg_outs = {}

    d = {}

    def din(name, shape, dt=F32):
        d[name] = nc.dram_tensor(name, list(shape), dt,
                                 kind="ExternalInput").ap()
        return d[name]

    din("x_fm", [DM, T])
    out_d = nc.dram_tensor("out_fm", [DM, T], F32, kind="ExternalOutput").ap()

    for nm, sh, dt in [
        ("ident_bf", [128, 128], BF16), ("zid_bf", [128, 256], BF16),
        ("ones1_bf", [1, 128], BF16), ("ones1_f32", [1, 128], F32),
        ("onescol_bf", [128, 1], BF16), ("mask_sl", [128, 128], F32),
        ("mask_t", [128, 256], F32), ("mask_sl2", [128, 256], BF16),
        ("mask_t2", [128, 512], BF16), ("ident2x", [128, 64], BF16),
        ("hsel", [16, NCH * 128], F32),
        ("hselt_bf", [128, NCH * 16], BF16), ("chanvec", [DM, NVC], F32),
        ("consts", [128, NSC], F32), ("lnw_tm", [128, DM], F32),
        ("lnb_tm", [128, DM], F32),
        ("wg", [DM, 8], F32), ("ws1", [DM, FF], BF16),
        ("ws3", [DM, FF], BF16),
        ("ws2", [FF, DM], BF16), ("we1", [NEXP, DM, FF], BF16),
        ("we3", [NEXP, DM, FF], BF16), ("we2", [NEXP, FF, DM], BF16),
        ("wrT", [DM, DM], BF16), ("wkT", [DM, DM], BF16),
        ("wvT", [DM, DM], BF16), ("woT", [DM, DM], BF16),
        ("maa_w1T", [DM, 128], F32), ("maa_w2", [4, 32, DM], F32),
        ("decay_w1T", [DM, 64], F32), ("decay_w2", [64, DM], F32),
        ("kkk_w1T", [DM, 16], F32), ("kkk_w2", [16, DM], F32),
        ("aaa_w1T", [DM, 16], F32), ("aaa_w2", [16, DM], F32),
        ("ma_w1T", [DM, 16], F32), ("ma_w2", [16, DM], F32),
        ("mk_w1T", [DM, 16], F32), ("mk_w2", [16, DM], F32),
        ("gate_w1T", [DM, 128], F32), ("gate_w2", [128, DM], F32),
    ]:
        din(nm, sh, dt)

    wdram = nc.dram_tensor("wdram", [NTC * 8, 128], BF16,
                           kind="Internal").ap()

    def sbuf(name, shape, dt=F32):
        return nc.alloc_sbuf_tensor(name, list(shape), dt).ap()

    R = [sbuf(f"R{m}", [128, T]) for m in range(NCH)]
    Xn = [None] * NCH
    ident_bf = sbuf("s_identbf", [128, 128], BF16)
    zid_bf = sbuf("s_zidbf", [128, 256], BF16)
    ones1_bf = sbuf("s_ones1bf", [1, 128], BF16)
    ones1_f32 = sbuf("s_ones1f", [1, 128])
    onescol_bf = sbuf("s_onescolbf", [128, 1], BF16)
    mask_sl = sbuf("s_masksl", [128, 128])
    mask_t = sbuf("s_maskt", [128, 256])
    mask_sl2 = sbuf("s_masksl2", [128, 256], BF16)
    mask_t2 = sbuf("s_maskt2", [128, 512], BF16)
    ident2x = sbuf("s_ident2x", [128, 64], BF16)
    hsel = sbuf("s_hsel", [16, NCH * 128])
    hselt_bf = sbuf("s_hselt", [128, NCH * 16], BF16)
    chanvec = sbuf("s_chanvec", [128, NCH * NVC])
    consts = sbuf("s_consts", [128, NSC])
    lnw_tm = sbuf("s_lnwtm", [128, DM])
    lnb_tm = sbuf("s_lnbtm", [128, DM])

    def cvap(m, col):
        return chanvec[:, m * NVC + col: m * NVC + col + 1]

    def dump(name, ap_list, dt=None):
        if name not in dbg:
            return
        if not isinstance(ap_list, (list, tuple)):
            ap_list = [ap_list]
        rows = sum(a.shape[0] for a in ap_list)
        w = ap_list[0].shape[1]
        dt = dt or ap_list[0].dtype
        od = nc.dram_tensor("dbg_" + name, [rows, w], dt,
                            kind="ExternalOutput").ap()
        dbg_outs[name] = od
        tgt = b.r("dY", 16 * len(ap_list))

        def f(s, ap_list=list(ap_list), tgt=tgt):
            r0 = 0
            for a in ap_list:
                s.dma_start(od[r0:r0 + a.shape[0], :], a[:]
                            ).then_inc(b.h("dY"), 16)
                r0 += a.shape[0]
            s.wait_ge(b.h("dY"), tgt)
        b.y(f)
        b.barrier()

    def f_init(s):
        for sb_ap, nm in [
            (ident_bf, "ident_bf"), (zid_bf, "zid_bf"),
            (ones1_bf, "ones1_bf"), (ones1_f32, "ones1_f32"),
            (onescol_bf, "onescol_bf"), (mask_sl, "mask_sl"),
            (mask_t, "mask_t"), (mask_sl2, "mask_sl2"),
            (mask_t2, "mask_t2"), (ident2x, "ident2x"),
            (hsel, "hsel"), (hselt_bf, "hselt_bf"),
            (consts, "consts"), (lnw_tm, "lnw_tm"), (lnb_tm, "lnb_tm"),
        ]:
            s.dma_start(sb_ap[:], d[nm][:]).then_inc(b.h("dY"), 16)
        s.dma_start(chanvec.rearrange("p (m v) -> p m v", m=NCH),
                    d["chanvec"].rearrange("(m p) v -> p m v", p=128)
                    ).then_inc(b.h("dY"), 16)
        for m in range(NCH):
            s.dma_start(R[m][:], d["x_fm"][ts(m), :]).then_inc(b.h("dY"), 16)
        s.wait_ge(b.h("dY"), 16 * (16 + NCH))
    b.r("dY", 16 * (16 + NCH))

    b.y(f_init)
    b.barrier()

    _ln_ctr = [0]

    def emit_ln(out_bf_list, sq_bf, rb_bf, apl_f32, lnrow):
        _ln_ctr[0] += 1
        _lt = _ln_ctr[0]
        with (
            nc.psum_tensor(f"ln_mean{_lt}", [1, T], F32) as mean_ps,
            nc.psum_tensor(f"ln_sq{_lt}", [1, T], F32) as sq_ps,
        ):
            dY0 = b.n("dY")
            pe_prev = 0
            for m in range(NCH):
                cn = b.r("sc")

                def fc(cc, m=m, dY0=dY0, pe_prev=pe_prev):
                    if m == 0:
                        cc.wait_ge(b.h("dY"), dY0)
                    if pe_prev:
                        cc.wait_ge(b.h("pe"), pe_prev)
                    cc.activation(sq_bf[:, :T], R[m][:, :T],
                                  AF.Square).then_inc(b.h("sc"), 1)
                b.c(fc)
                vn = b.r("v")

                def fv(vv, m=m, dY0=dY0, pe_prev=pe_prev):
                    if m == 0:
                        vv.wait_ge(b.h("dY"), dY0)
                    if pe_prev:
                        vv.wait_ge(b.h("pe"), pe_prev)
                    vv.tensor_copy(rb_bf[:, :T],
                                   R[m][:, :T]).then_inc(b.h("v"), 1)
                b.v(fv)
                pe_m = b.r("pe")

                def ft(tt, m=m, vn=vn, cn=cn):
                    tt.wait_ge(b.h("v"), vn)
                    tt.wait_ge(b.h("sc"), cn)
                    for nsI in range(NS):
                        tt.matmul(mean_ps[:, ts(nsI, 512)], onescol_bf[:],
                                  rb_bf[:, ts(nsI, 512)], start=(m == 0),
                                  stop=(m == NCH - 1))
                        ins = tt.matmul(sq_ps[:, ts(nsI, 512)], onescol_bf[:],
                                        sq_bf[:, ts(nsI, 512)],
                                        start=(m == 0), stop=(m == NCH - 1))
                    ins.then_inc(b.h("pe"), 1)
                b.t(ft)
                pe_prev = pe_m
            pe_stats = b.n("pe")

            mu = lnrow[:, 0 * T:1 * T]
            var = lnrow[:, 1 * T:2 * T]
            srow = var
            trow = mu
            v1 = b.r("v"); v2 = b.r("v"); v3 = b.r("v")

            def frow_v(vv, pe_stats=pe_stats, v1=v1, v2=v2):
                vv.wait_ge(b.h("pe"), pe_stats)
                vv.tensor_scalar_mul(mu, mean_ps[:1, :T],
                                     1.0 / DM).then_inc(b.h("v"), 1)
                vv.wait_ge(b.h("v"), v1)
                vv.tensor_mul(var, mu, mu).then_inc(b.h("v"), 1)
                vv.wait_ge(b.h("v"), v2)
                vv.scalar_tensor_tensor(var, sq_ps[:1, :T], 1.0 / DM, var,
                                        ALU.mult,
                                        ALU.subtract).then_inc(b.h("v"), 1)
            b.v(frow_v)
            c1 = b.r("sc", 2)

            def frow_c(cc, v3=v3, c1=c1):
                cc.wait_ge(b.h("v"), v3)
                cc.activation(srow, var, AF.Ln,
                              bias=consts[:1, SC_LNEPS:SC_LNEPS + 1]
                              ).then_inc(b.h("sc"), 1)
                cc.wait_ge(b.h("sc"), c1 - 1)
                cc.activation(srow, srow, AF.Exp,
                              scale=-0.5).then_inc(b.h("sc"), 1)
            b.c(frow_c)
            v5 = b.r("v")

            def frow_v2(vv, c1=c1):
                vv.wait_ge(b.h("sc"), c1)
                vv.scalar_tensor_tensor(trow, mu, -1.0, srow, ALU.mult,
                                        ALU.mult).then_inc(b.h("v"), 1)
            b.v(frow_v2)
            b.barrier()

        with (
            nc.psum_tensor(f"ln_sb{_lt}", [128, T], F32) as sb_ps,
            nc.psum_tensor(f"ln_tb{_lt}", [128, T], F32) as tb_ps,
        ):
            pe_bc = b.r("pe")

            def fbc(tt, pe_bc=pe_bc):
                for nsI in range(NS):
                    tt.matmul(sb_ps[:, ts(nsI, 512)], ones1_f32[:],
                              srow[:1, ts(nsI, 512)], start=True, stop=True)
                    ins = tt.matmul(tb_ps[:, ts(nsI, 512)], ones1_f32[:],
                                    trow[:1, ts(nsI, 512)], start=True,
                                    stop=True)
                ins.then_inc(b.h("pe"), 1)
            b.t(fbc)

            for m in range(NCH):
                vprev = b.n("v")
                va = b.r("v")
                vb = b.r("v")

                def fap(vv, m=m, va=va, pe_bc=pe_bc, vprev=vprev):
                    if m == 0:
                        vv.wait_ge(b.h("pe"), pe_bc)
                    else:
                        vv.wait_ge(b.h("v"), vprev)
                    vv.tensor_tensor(apl_f32[:, :T], R[m][:, :T],
                                     sb_ps[:, :T],
                                     ALU.mult).then_inc(b.h("v"), 1)
                    vv.wait_ge(b.h("v"), va)
                    vv.tensor_tensor(out_bf_list[m][:, :T], apl_f32[:, :T],
                                     tb_ps[:, :T],
                                     ALU.add).then_inc(b.h("v"), 1)
                b.v(fap)
            b.barrier()

    def emit_gating(callno, gtile, gscr, gm12, wtile_bf, wT_bf, wg_bf):
        with (
            nc.psum_tensor(f"g_ps{callno}", [128, 8], F32) as gps,
            nc.psum_tensor(f"g_tp{callno}", [128, 128], BF16) as gtp,
        ):
            dgw = b.r("dG", 16)

            def fwg(g, dgw=dgw):
                g.dma_start(wg_bf.rearrange("p (kc e) -> p kc e", kc=NCH),
                            d["wg"].rearrange("(kc p) e -> p kc e", p=128)
                            ).then_inc(b.h("dG"), 16)
                g.wait_ge(b.h("dG"), dgw)
            b.g(fwg)
            for tc in range(NTC):
                pen = b.r("pe")

                def ft(tt, tc=tc, pen=pen, dgw=dgw):
                    if tc == 0:
                        tt.wait_ge(b.h("dG"), dgw)
                    for kc in range(NCH):
                        ins = tt.matmul(gps[:, :8], Xn[kc][:, ts(tc)],
                                        wg_bf[:, kc * 8:(kc + 1) * 8],
                                        start=(kc == 0), stop=(kc == NCH - 1))
                    ins.then_inc(b.h("pe"), 1)
                b.t(ft)
                cn = b.r("sc")

                def fc(cc, tc=tc, pen=pen):
                    cc.wait_ge(b.h("pe"), pen)
                    cc.activation(gtile[:, tc * 8:(tc + 1) * 8], gps[:, :8],
                                  AF.Sigmoid).then_inc(b.h("sc"), 1)
                b.c(fc)

                def ftw(tt, cn=cn):
                    tt.wait_ge(b.h("sc"), cn)
                b.t(ftw)

            g3v = gtile[:, :NTC * 8].rearrange("p (a e) -> p a e", a=NTC)
            m1 = gm12[:, 0 * NTC:1 * NTC]
            m2 = gm12[:, 1 * NTC:2 * NTC]
            winv = gm12[:, 2 * NTC:3 * NTC]
            mask1 = gscr[:, 0 * NTC * 8:1 * NTC * 8]
            gmsk = gscr[:, 1 * NTC * 8:2 * NTC * 8]
            wraw = gscr[:, 2 * NTC * 8:3 * NTC * 8]
            sc_all = b.n("sc")
            vv0 = b.n("v")
            vtop = [b.r("v") for _ in range(9)]

            def ftop(vv, sc_all=sc_all, vv0=vv0):
                vv.wait_ge(b.h("sc"), sc_all)

                def r3(ap_):
                    return ap_.rearrange("p (a e) -> p a e", a=NTC)

                def r1(ap_):
                    return ap_.rearrange("p (a e) -> p a e", a=NTC, e=1)
                m1b = r1(m1).broadcast_to((128, NTC, 8))
                m2b = r1(m2).broadcast_to((128, NTC, 8))
                wib = r1(winv).broadcast_to((128, NTC, 8))
                vv.reduce_max(r1(m1), g3v,
                              axis=mybir.AxisListType.X).then_inc(b.h("v"), 1)
                vv.wait_ge(b.h("v"), vv0 + 1)
                vv.tensor_tensor(r3(mask1), g3v, m1b,
                                 ALU.is_ge).then_inc(b.h("v"), 1)
                vv.wait_ge(b.h("v"), vv0 + 2)
                vv.scalar_tensor_tensor(r3(gmsk), r3(mask1), -1e30, g3v,
                                        ALU.mult,
                                        ALU.add).then_inc(b.h("v"), 1)
                vv.wait_ge(b.h("v"), vv0 + 3)
                vv.reduce_max(r1(m2), r3(gmsk),
                              axis=mybir.AxisListType.X).then_inc(b.h("v"), 1)
                vv.wait_ge(b.h("v"), vv0 + 4)
                vv.tensor_tensor(winv, m1, m2, ALU.add).then_inc(b.h("v"), 1)
                vv.wait_ge(b.h("v"), vv0 + 5)
                vv.reciprocal(winv, winv).then_inc(b.h("v"), 1)
                vv.wait_ge(b.h("v"), vv0 + 6)
                vv.tensor_tensor(r3(wraw), g3v, m2b,
                                 ALU.is_ge).then_inc(b.h("v"), 1)
                vv.wait_ge(b.h("v"), vv0 + 7)
                vv.tensor_tensor(r3(wraw), r3(wraw), g3v,
                                 ALU.mult).then_inc(b.h("v"), 1)
                vv.wait_ge(b.h("v"), vv0 + 8)
                vv.tensor_tensor(r3(wtile_bf), r3(wraw), wib,
                                 ALU.mult).then_inc(b.h("v"), 1)
            b.v(ftop)
            v_top = vtop[-1]
            pe_tr = b.r("pe")

            def ftr(tt, v_top=v_top, pe_tr=pe_tr):
                tt.wait_ge(b.h("v"), v_top)
                tt.transpose(gtp[:NTC * 8, :], wtile_bf[:, :NTC * 8],
                             ident_bf[:]).then_inc(b.h("pe"), 1)
            b.t(ftr)
            v_wT = b.r("v")

            def fev(vv, pe_tr=pe_tr):
                vv.wait_ge(b.h("pe"), pe_tr)
                vv.tensor_copy(wT_bf[:NTC * 8, :],
                               gtp[:NTC * 8, :]).then_inc(b.h("v"), 1)
            b.v(fev)
            dg1 = b.r("dGt", 16)

            def fdma(g, v_wT=v_wT, dg1=dg1):
                g.wait_ge(b.h("v"), v_wT)
                g.dma_start(wdram[:, :],
                            wT_bf[:NTC * 8, :]).then_inc(b.h("dGt"), 16)
                g.wait_ge(b.h("dGt"), dg1)
            b.g(fdma)
            b.barrier()

    def emit_ffn(callno):
        NTQ = T // 512
        from contextlib import ExitStack
        with ExitStack() as _est:
            w_row = _est.enter_context(nc.sbuf_tensor(f"f{callno}_wrow", [1, T], BF16))
            w_bc = _est.enter_context(nc.sbuf_tensor(f"f{callno}_wbc", [128, T], BF16))
            w1a = _est.enter_context(nc.sbuf_tensor(f"f{callno}_w1a", [128, 8 * 512], BF16))
            w3a = _est.enter_context(nc.sbuf_tensor(f"f{callno}_w3a", [128, 8 * 512], BF16))
            w1b = _est.enter_context(nc.sbuf_tensor(f"f{callno}_w1b", [128, 8 * 512], BF16))
            w3b = _est.enter_context(nc.sbuf_tensor(f"f{callno}_w3b", [128, 8 * 512], BF16))
            w2s = _est.enter_context(nc.sbuf_tensor(f"f{callno}_w2", [128, 8 * DM], BF16))
            hbuf = _est.enter_context(nc.sbuf_tensor(f"f{callno}_h", [128, 8 * T], BF16))
            sibuf = _est.enter_context(nc.sbuf_tensor(f"f{callno}_si", [128, 2 * 512], BF16))
            g3buf = _est.enter_context(nc.sbuf_tensor(f"f{callno}_g3", [128, 2 * 512], BF16))
            tmbuf = _est.enter_context(nc.sbuf_tensor(f"f{callno}_tm", [128, 2 * 512], BF16))
            for m in range(NCH):
                Xn[m] = _est.enter_context(
                    nc.sbuf_tensor(f"f{callno}_Xn{m}", [128, T], BF16)).ap() \
                    if False else _est.enter_context(
                    nc.sbuf_tensor(f"f{callno}_Xn{m}x", [128, T], BF16))
            gtile = hbuf[:, 0:2 * NTC * 8].bitcast(F32)
            gscr = hbuf[:, 2 * NTC * 8:8 * NTC * 8].bitcast(F32)
            gm12 = hbuf[:, 8 * NTC * 8:8 * NTC * 8 + 6 * NTC].bitcast(F32)
            _go = 8 * NTC * 8 + 6 * NTC
            wtile_bf = hbuf[:, _go:_go + NTC * 8]
            wT_bf = hbuf[:, _go + NTC * 8:_go + NTC * 8 + 128]
            wg_bf = hbuf[:, _go + NTC * 8 + 128:_go + NTC * 8 + 128 + 64]
            sq_bf = hbuf[:, 0:T]
            rb_bf = hbuf[:, T:2 * T]
            apl_f32 = hbuf[:, 2 * T:4 * T].bitcast(F32)
            lnrow = hbuf[0:1, 4 * T:8 * T].bitcast(F32)
            emit_ln(Xn, sq_bf, rb_bf, apl_f32, lnrow)
            emit_gating(callno, gtile, gscr, gm12, wtile_bf, wT_bf, wg_bf)
            pg = _est.enter_context(nc.psum_tensor(f"f{callno}_pg",
                                                   [128, 4 * 512], F32))
            po = _est.enter_context(nc.psum_tensor(f"f{callno}_po",
                                                   [128, 2 * 512], F32))
            dgt_n = b.n("dGt")
            wrow_free_pe = 0
            w13 = [(w1a, w3a), (w1b, w3b)]
            experts = [-1] + list(range(NEXP))
            units = [(e, hf) for e in experts for hf in (0, 1)]
            wsem = ["dWa", "dWb"]

            w13_loaded = {}
            w2_loaded = {}
            grot = PsumRot(2)
            orot = PsumRot(2)
            q_done_pe = {}
            u_done_pe = {}
            wbc_evac = {}
            hready = {}
            pln_hist = []
            gq = 0

            def dma_w13(g, e, qf, slot):
                src1 = d["ws1"] if e < 0 else d["we1"][e]
                src3 = d["ws3"] if e < 0 else d["we3"][e]
                for wsb, src in ((w13[slot][0], src1), (w13[slot][1], src3)):
                    g.dma_start(
                        wsb.rearrange("p (kc f) -> p kc f", kc=8),
                        src[:, ts(qf, 512)].rearrange("(kc p) f -> p kc f",
                                                      p=128)
                    ).then_inc(b.h(wsem[slot]), 16)

            def dma_w2(g, e, hf):
                src2 = d["ws2"] if e < 0 else d["we2"][e]
                g.dma_start(
                    w2s.rearrange("p (kc dd) -> p kc dd", kc=8),
                    src2[ts(hf, 1024), :].rearrange("(kc p) dd -> p kc dd",
                                                    p=128)
                ).then_inc(b.h("dW2"), 16)

            w13_loaded[0] = b.r("dWa", 32)
            w13_loaded[1] = b.r("dWb", 32)
            w2_loaded[0] = b.r("dW2", 16)

            def fpro(g):
                dma_w13(g, units[0][0], 0, 0)
                dma_w13(g, units[0][0], 1, 1)
                dma_w2(g, units[0][0], 0)
            b.g(fpro)

            def wait_guard(tt, guard):
                if guard:
                    for snm, cnt in guard.items():
                        tt.wait_ge(b.h(snm), cnt)

            for ui, (e, hf) in enumerate(units):
                if e >= 0 and hf == 0:
                    drow = b.r("dGt", 16)

                    def frow(g, e=e, drow=drow, dgt_n=dgt_n,
                             wrow_free_pe=wrow_free_pe):
                        g.wait_ge(b.h("dGt"), dgt_n)
                        if wrow_free_pe:
                            g.wait_ge(b.h("pe"), wrow_free_pe)
                        srcw = bass.AP(wdram.tensor, e * 128,
                                       [[1, 1], [8 * 128, NTC], [1, 128]])
                        g.dma_start(w_row[:1, :].rearrange(
                            "p (a n) -> p a n", a=NTC),
                            srcw).then_inc(b.h("dGt"), 16)
                    b.g(frow)
                    for nsI in range(NS):
                        oslot, guard = orot.acquire()
                        pen = b.r("pe")

                        def fbc(tt, nsI=nsI, oslot=oslot, guard=guard,
                                drow=drow):
                            wait_guard(tt, guard)
                            tt.wait_ge(b.h("dGt"), drow)
                            tt.matmul(po[:, ts(oslot, 512)], ones1_bf[:],
                                      w_row[:1, nsI * 512:(nsI + 1) * 512],
                                      start=True,
                                      stop=True).then_inc(b.h("pe"), 1)
                        b.t(fbc)
                        vn = b.r("v")

                        def fbe(vv, nsI=nsI, oslot=oslot, pen=pen):
                            vv.wait_ge(b.h("pe"), pen)
                            vv.tensor_copy(w_bc[:, ts(nsI, 512)],
                                           po[:, ts(oslot, 512)]
                                           ).then_inc(b.h("v"), 1)
                        b.v(fbe)
                        orot.release({"v": vn})
                    wrow_free_pe = b.n("pe")
                    wbc_evac[e] = b.n("v")

                for q in (0, 1):
                    qf = hf * 2 + q
                    slot = gq % 2
                    myload = w13_loaded[gq]
                    w1sb, w3sb = w13[slot]
                    for fc in range(4):
                        for tq in range(NTQ):
                            gslot, guard = grot.acquire()
                            s1 = gslot * 2
                            s3 = s1 + 1
                            pen = b.r("pe")

                            def fg(tt, fc=fc, tq=tq, s1=s1, s3=s3,
                                   guard=guard, myload=myload, w1sb=w1sb,
                                   w3sb=w3sb, slot=slot):
                                tt.wait_ge(b.h(wsem[slot]), myload)
                                wait_guard(tt, guard)
                                for kc in range(NCH):
                                    tt.matmul(
                                        pg[:, ts(s1, 512)],
                                        w1sb[:, kc * 512 + fc * 128:
                                             kc * 512 + (fc + 1) * 128],
                                        Xn[kc][:, ts(tq, 512)],
                                        start=(kc == 0), stop=(kc == NCH - 1))
                                for kc in range(NCH):
                                    ins = tt.matmul(
                                        pg[:, ts(s3, 512)],
                                        w3sb[:, kc * 512 + fc * 128:
                                             kc * 512 + (fc + 1) * 128],
                                        Xn[kc][:, ts(tq, 512)],
                                        start=(kc == 0), stop=(kc == NCH - 1))
                                ins.then_inc(b.h("pe"), 1)
                            b.t(fg)

                            sslot = gslot
                            scr_guard = (pln_hist[-2] if len(pln_hist) >= 2
                                         else 0)
                            cn = b.r("sc")

                            def fsi(cc, s1=s1, sslot=sslot, pen=pen,
                                    scr_guard=scr_guard):
                                cc.wait_ge(b.h("pe"), pen)
                                if scr_guard:
                                    cc.wait_ge(b.h("pl"), scr_guard)
                                cc.activation(sibuf[:, ts(sslot, 512)],
                                              pg[:, ts(s1, 512)],
                                              AF.Silu).then_inc(b.h("sc"), 1)
                            b.c(fsi)
                            vg3 = b.r("v")

                            def fg3(vv, s3=s3, sslot=sslot, pen=pen,
                                    scr_guard=scr_guard):
                                vv.wait_ge(b.h("pe"), pen)
                                if scr_guard:
                                    vv.wait_ge(b.h("pl"), scr_guard)
                                vv.tensor_scalar_mul(
                                    g3buf[:, ts(sslot, 512)],
                                    pg[:, ts(s3, 512)],
                                    1.0 / 6.0).then_inc(b.h("v"), 1)
                            b.v(fg3)
                            grot.release({"v": vg3, "sc": cn})

                            hfc = q * 4 + fc
                            hslice = hbuf[:, hfc * T + tq * 512:
                                          hfc * T + (tq + 1) * 512]
                            prev_u = u_done_pe.get(ui - 1, 0)
                            pln = b.r("pl")
                            if e < 0:
                                def fmul(pp, hslice=hslice, sslot=sslot,
                                         vg3=vg3, cn=cn, prev_u=prev_u):
                                    pp.wait_ge(b.h("v"), vg3)
                                    pp.wait_ge(b.h("sc"), cn)
                                    if prev_u:
                                        pp.wait_ge(b.h("pe"), prev_u)
                                    pp.tensor_tensor(
                                        hslice, sibuf[:, ts(sslot, 512)],
                                        g3buf[:, ts(sslot, 512)],
                                        ALU.mult).then_inc(b.h("pl"), 1)
                                b.g(fmul)
                                hready[(qf, fc, tq)] = ("pl", pln)
                            else:
                                def fmul(pp, sslot=sslot, vg3=vg3, cn=cn):
                                    pp.wait_ge(b.h("v"), vg3)
                                    pp.wait_ge(b.h("sc"), cn)
                                    pp.tensor_tensor(
                                        tmbuf[:, ts(sslot, 512)],
                                        sibuf[:, ts(sslot, 512)],
                                        g3buf[:, ts(sslot, 512)],
                                        ALU.mult).then_inc(b.h("pl"), 1)
                                b.g(fmul)
                                wbcn = wbc_evac[e]
                                vw = b.r("v")

                                def fwm(vv, hslice=hslice, sslot=sslot,
                                        tq=tq, pln=pln, wbcn=wbcn,
                                        prev_u=prev_u):
                                    vv.wait_ge(b.h("pl"), pln)
                                    vv.wait_ge(b.h("v"), wbcn)
                                    if prev_u:
                                        vv.wait_ge(b.h("pe"), prev_u)
                                    vv.tensor_tensor(
                                        hslice, tmbuf[:, ts(sslot, 512)],
                                        w_bc[:, ts(tq, 512)],
                                        ALU.mult).then_inc(b.h("v"), 1)
                                b.v(fwm)
                                hready[(qf, fc, tq)] = ("v", vw)
                            pln_hist.append(pln)
                    q_done_pe[gq] = b.n("pe")
                    nxt = gq + 2
                    if nxt < len(units) * 2:
                        ne, nhf = units[nxt // 2]
                        nqf = nhf * 2 + (nxt % 2)
                        mydone = q_done_pe[gq]
                        w13_loaded[nxt] = b.r(wsem[slot], 32)

                        def fpre(g, ne=ne, nqf=nqf, slot=slot,
                                 mydone=mydone):
                            g.wait_ge(b.h("pe"), mydone)
                            dma_w13(g, ne, nqf, slot)
                        b.g(fpre)
                    gq += 1

                myw2 = w2_loaded[ui]
                for dc in range(NCH):
                    for tq in range(NTQ):
                        oslot, guard = orot.acquire()
                        hw = [hready[(hf * 2 + qq, fc, tq)]
                              for qq in (0, 1) for fc in range(4)]
                        vmax = max([n for k, n in hw if k == "v"], default=0)
                        plmax = max([n for k, n in hw if k == "pl"],
                                    default=0)
                        pen = b.r("pe")

                        def fo(tt, dc=dc, tq=tq, oslot=oslot, guard=guard,
                               myw2=myw2, vmax=vmax, plmax=plmax):
                            tt.wait_ge(b.h("dW2"), myw2)
                            if vmax:
                                tt.wait_ge(b.h("v"), vmax)
                            if plmax:
                                tt.wait_ge(b.h("pl"), plmax)
                            wait_guard(tt, guard)
                            for kc in range(8):
                                ins = tt.matmul(
                                    po[:, ts(oslot, 512)],
                                    w2s[:, kc * DM + dc * 128:
                                        kc * DM + (dc + 1) * 128],
                                    hbuf[:, kc * T + tq * 512:
                                         kc * T + (tq + 1) * 512],
                                    start=(kc == 0), stop=(kc == 7))
                            ins.then_inc(b.h("pe"), 1)
                        b.t(fo)
                        vn = b.r("v")

                        def fadd(vv, dc=dc, tq=tq, oslot=oslot, pen=pen):
                            vv.wait_ge(b.h("pe"), pen)
                            vv.tensor_tensor(R[dc][:, ts(tq, 512)],
                                             R[dc][:, ts(tq, 512)],
                                             po[:, ts(oslot, 512)],
                                             ALU.add).then_inc(b.h("v"), 1)
                        b.v(fadd)
                        orot.release({"v": vn})
                u_done_pe[ui] = b.n("pe")
                if ui + 1 < len(units):
                    ne, nhf = units[ui + 1]
                    mydone = u_done_pe[ui]
                    w2_loaded[ui + 1] = b.r("dW2", 16)

                    def fw2(g, ne=ne, nhf=nhf, mydone=mydone):
                        g.wait_ge(b.h("pe"), mydone)
                        dma_w2(g, ne, nhf)
                    b.g(fw2)
            _wa, _wb, _w2, _dgt = (b.n("dWa"), b.n("dWb"), b.n("dW2"),
                                   b.n("dGt"))

            def fdrain2(g, _wa=_wa, _wb=_wb, _w2=_w2, _dgt=_dgt):
                g.wait_ge(b.h("dWa"), _wa)
                g.wait_ge(b.h("dWb"), _wb)
                g.wait_ge(b.h("dW2"), _w2)
                g.wait_ge(b.h("dGt"), _dgt)
            b.g(fdrain2)
            b.barrier()

    def emit_attn():
        W = 128
        NBK = T // W

        def vop(waits, fn):
            n = b.r("v")

            def f(vv, waits=dict(waits), fn=fn):
                for s, c in waits.items():
                    if c:
                        vv.wait_ge(b.h(s), c)
                fn(vv).then_inc(b.h("v"), 1)
            b.v(f)
            return n

        def cop(waits, fn):
            n = b.r("sc")

            def f(cc, waits=dict(waits), fn=fn):
                for s, c in waits.items():
                    if c:
                        cc.wait_ge(b.h(s), c)
                fn(cc).then_inc(b.h("sc"), 1)
            b.c(f)
            return n

        def pop(waits, fn):
            n = b.r("pl")

            def f(pp, waits=dict(waits), fn=fn):
                for s, c in waits.items():
                    if c:
                        pp.wait_ge(b.h(s), c)
                fn(pp).then_inc(b.h("pl"), 1)
            b.g(f)
            return n

        def top(waits, fn):
            n = b.r("pe")

            def f(tt, waits=dict(waits), fn=fn):
                for s, c in waits.items():
                    if c:
                        tt.wait_ge(b.h(s), c)
                fn(tt).then_inc(b.h("pe"), 1)
            b.t(f)
            return n

        from contextlib import ExitStack
        with ExitStack() as _est:
            wsA = _est.enter_context(nc.sbuf_tensor("a_wsA", [128, 8 * 512], BF16))
            wsB = _est.enter_context(nc.sbuf_tensor("a_wsB", [128, 8 * 512], BF16))
            maa1 = _est.enter_context(nc.sbuf_tensor("a_maa1", [128, 8 * 128], BF16))
            maa2a = _est.enter_context(nc.sbuf_tensor("a_maa2a", [64, DM], BF16))
            maa2b = _est.enter_context(nc.sbuf_tensor("a_maa2b", [64, DM], BF16))
            dw1 = _est.enter_context(nc.sbuf_tensor("a_dw1", [128, 8 * 64], BF16))
            dw2 = _est.enter_context(nc.sbuf_tensor("a_dw2", [64, DM], BF16))
            bw1 = _est.enter_context(nc.sbuf_tensor("a_bw1", [128, 8 * 64], BF16))
            bw2ka = _est.enter_context(nc.sbuf_tensor("a_bw2ka", [64, DM], BF16))
            bw2mm = _est.enter_context(nc.sbuf_tensor("a_bw2mm", [64, DM], BF16))
            gw1 = _est.enter_context(nc.sbuf_tensor("a_gw1", [128, 8 * 128], BF16))
            gw2 = _est.enter_context(nc.sbuf_tensor("a_gw2", [128, DM], BF16))
            xnb = _est.enter_context(nc.sbuf_tensor("a_xnb", [128, 8 * W], BF16))
            rT = _est.enter_context(nc.sbuf_tensor("a_rT", [128, 8 * W], BF16))
            kraw = _est.enter_context(nc.sbuf_tensor("a_kraw", [128, 8 * W], BF16))
            gTt = _est.enter_context(nc.sbuf_tensor("a_gT", [128, 8 * W], BF16))
            alpha = _est.enter_context(nc.sbuf_tensor("a_alpha", [128, 8 * W], BF16))
            kkn = _est.enter_context(nc.sbuf_tensor("a_kkn", [128, 8 * W], BF16))
            kf = _est.enter_context(nc.sbuf_tensor("a_kf", [128, 8 * W], BF16))
            wdec = _est.enter_context(nc.sbuf_tensor("a_wdec", [128, 8 * W], BF16))
            dinc = _est.enter_context(nc.sbuf_tensor("a_dinc", [128, 8 * W], BF16))
            sqb2 = dinc
            dinv = _est.enter_context(nc.sbuf_tensor("a_dinv", [128, 8 * W], BF16))
            Qbt = _est.enter_context(nc.sbuf_tensor("a_Qb", [128, 8 * W], BF16))
            Abt = _est.enter_context(nc.sbuf_tensor("a_Ab", [128, 8 * W], BF16))
            Btt = _est.enter_context(nc.sbuf_tensor("a_Bt", [128, 8 * W], BF16))
            Ktt = _est.enter_context(nc.sbuf_tensor("a_Kt", [128, 8 * W], BF16))
            sbA = _est.enter_context(nc.sbuf_tensor("a_sbA", [128, 8 * W], BF16))
            losb = sbA[0:64, 0:256]
            gt1 = sbA[:, 256:384]
            dt1 = sbA[0:64, 384:512]
            kbt = sbA[0:16, 512:640]
            abt = sbA[0:48, 640:768]
            mabt = sbA[0:16, 768:896]
            mkbt = sbA[0:48, 896:1024]
            sbB = _est.enter_context(nc.sbuf_tensor("a_sbB", [128, 8 * W], BF16))
            xx = _est.enter_context(nc.sbuf_tensor("a_xx", [128, 8 * W], F32))
            lwt = _est.enter_context(nc.sbuf_tensor("a_lw", [128, 8 * W], F32))
            Lct = _est.enter_context(nc.sbuf_tensor("a_Lc", [128, 8 * W], F32))
            Bhtm = Lct.bitcast(BF16)[:, 0:DM]
            Khtm = Lct.bitcast(BF16)[:, DM:2 * DM]
            scrA = _est.enter_context(nc.sbuf_tensor("a_scrA", [128, 8 * W], F32))
            ln2r = scrA[0:1, 0:512]
            scrB = _est.enter_context(nc.sbuf_tensor("a_scrB", [128, 8 * W], F32))
            rnsb = scrB[0:16, 0:128]
            rksb = scrB.bitcast(BF16)[0:16, 256:384]
            rktm = scrB.bitcast(BF16)[:, 384:400]
            m16 = scrB[:, 256:352]
            vtm = _est.enter_context(nc.sbuf_tensor("a_vtm", [128, DM], BF16))
            Atm = _est.enter_context(nc.sbuf_tensor("a_Atm", [128, DM], BF16))
            xmr = Atm
            ytm = _est.enter_context(nc.sbuf_tensor("a_ytm", [128, DM], BF16))
            mag = ytm
            xmw = ytm

            gtm = _est.enter_context(nc.sbuf_tensor("a_gtm", [128, DM], BF16))
            mkg = gtm
            xmk = gtm

            ygfm = _est.enter_context(nc.sbuf_tensor("a_ygfm", [128, 8 * W], BF16))
            rcp2 = ygfm
            xmv = ygfm
            STf = _est.enter_context(nc.sbuf_tensor("a_STf", [128, H * HS // 2], F32))
            STb = _est.enter_context(nc.sbuf_tensor("a_STb", [128, H * HS // 2], BF16))
            NILV = 8
            NsbS = [_est.enter_context(nc.sbuf_tensor(f"a_Nsb{i}",
                    [128, 128], BF16)) for i in range(NILV)]
            nt3tS = [_est.enter_context(nc.sbuf_tensor(f"a_nt3t{i}",
                     [128, 256], BF16)) for i in range(NILV)]
            t2t4S = [_est.enter_context(nc.sbuf_tensor(f"a_t2t4{i}",
                     [128, 256], BF16)) for i in range(NILV)]
            Pb0S = [_est.enter_context(nc.sbuf_tensor(f"a_Pb0{i}",
                    [128, 128], BF16)) for i in range(NILV)]
            Pb1S = [_est.enter_context(nc.sbuf_tensor(f"a_Pb1{i}",
                    [128, 128], BF16)) for i in range(NILV)]
            PTtS = [_est.enter_context(nc.sbuf_tensor(f"a_PTt{i}",
                    [128, 6 * 128], BF16)) for i in range(NILV)]
            Xc0S = [_est.enter_context(nc.sbuf_tensor(f"a_Xc0{i}",
                    [128, 64], BF16)) for i in range(NILV)]
            Xc1S = [_est.enter_context(nc.sbuf_tensor(f"a_Xc1{i}",
                    [128, 64], BF16)) for i in range(NILV)]
            ddlS = [_est.enter_context(nc.sbuf_tensor(f"a_ddl{i}",
                    [128, 64], BF16)) for i in range(NILV)]
            slot_free_pe = {}
            carry = _est.enter_context(nc.sbuf_tensor("a_carry", [128, 8], F32))
            basec = _est.enter_context(nc.sbuf_tensor("a_base", [128, 8], F32))
            dlastf = _est.enter_context(nc.sbuf_tensor("a_dlastf", [128, 8], F32))
            bnd = _est.enter_context(nc.sbuf_tensor("a_bnd", [128, 8], BF16))
            P0 = _est.enter_context(nc.psum_tensor("a_P0", [128, 1024], F32))
            P1 = _est.enter_context(nc.psum_tensor("a_P1", [128, 1024], F32))
            P2 = _est.enter_context(nc.psum_tensor("a_P2", [128, 1024], F32))
            PTps = _est.enter_context(nc.psum_tensor("a_PT", [128, 2048], BF16))
            Pt = [P0, P1, P2]
            prot = PsumRot(3)
            ptrot = PsumRot(2)

            def acq():
                s, g = prot.acquire()
                return Pt[s], dict(g or {})

            def acqT():
                s, g = ptrot.acquire()
                return PTps[:, s * 1024:(s + 1) * 1024], dict(g or {})

            def v3(ap_):
                return ap_.rearrange("p (m w) -> p m w", m=8)

            def cv3(col):
                apx = chanvec[:, col::NVC]
                return apx.rearrange("p (m o) -> p m o", o=1).broadcast_to(
                    (128, 8, W))

            z128 = consts[:, SC_ZERO:SC_ZERO + 1].broadcast_to((128, 128))
            vz = vop({}, lambda vv: vv.memset(STf[:], 0.0))
            vz = vop({"v": vz}, lambda vv: vv.memset(STb[:], 0.0))
            vz = vop({"v": vz}, lambda vv: vv.memset(carry[:], 0.0))
            v_bn_prev = vop({"v": vz}, lambda vv: vv.memset(bnd[:], 0.0))

            dS_done = b.r("dS", 16 * 17)

            def fsm(g):
                def ld(dst, nm):
                    g.dma_start(dst, d[nm]).then_inc(b.h("dS"), 16)
                g.dma_start(maa1.rearrange("p (kc x) -> p kc x", kc=8),
                            d["maa_w1T"].rearrange("(kc p) x -> p kc x",
                                                   p=128)
                            ).then_inc(b.h("dS"), 16)
                g.dma_start(dw1.rearrange("p (kc x) -> p kc x", kc=8),
                            d["decay_w1T"].rearrange("(kc p) x -> p kc x",
                                                     p=128)
                            ).then_inc(b.h("dS"), 16)
                g.dma_start(gw1.rearrange("p (kc x) -> p kc x", kc=8),
                            d["gate_w1T"].rearrange("(kc p) x -> p kc x",
                                                    p=128)
                            ).then_inc(b.h("dS"), 16)
                for i, nm in enumerate(("kkk_w1T", "aaa_w1T", "ma_w1T",
                                        "mk_w1T")):
                    g.dma_start(
                        bw1.rearrange("p (kc x) -> p kc x", kc=8
                                      )[:, :, i * 16:(i + 1) * 16],
                        d[nm].rearrange("(kc p) x -> p kc x", p=128)
                    ).then_inc(b.h("dS"), 16)
                ld(gw2[:], "gate_w2")
                ld(dw2[:], "decay_w2")
                ld(bw2ka[0:16, :], "kkk_w2")
                ld(bw2ka[32:48, :], "aaa_w2")
                ld(bw2mm[0:16, :], "ma_w2")
                ld(bw2mm[32:48, :], "mk_w2")
                g.dma_start(maa2a[0:32, :], d["maa_w2"][0]
                            ).then_inc(b.h("dS"), 16)
                g.dma_start(maa2a[32:64, :], d["maa_w2"][1]
                            ).then_inc(b.h("dS"), 16)
                g.dma_start(maa2b[0:32, :], d["maa_w2"][2]
                            ).then_inc(b.h("dS"), 16)
                g.dma_start(maa2b[32:64, :], d["maa_w2"][3]
                            ).then_inc(b.h("dS"), 16)
            b.g(fsm)

            dA_cur = b.r("dA", 16)
            dB_cur = b.r("dB", 16)

            def fw0(g):
                for slot_sb, nm in ((wsA, "wrT"), (wsB, "wkT")):
                    sem = "dA" if slot_sb is wsA else "dB"
                    g.dma_start(
                        slot_sb.rearrange("p (kc x) -> p kc x", kc=8),
                        d[nm][:, 0:512].rearrange("(kc p) x -> p kc x",
                                                  p=128)
                    ).then_inc(b.h(sem), 16)
            b.g(fw0)

            ln2_pe_prev = 0
            stv = {h: {} for h in range(H)}

            for blk in range(NBK):
                t0 = blk * W
                tsl = slice(t0, t0 + W)

                ps_st, g_st = acq()
                pe_prev = ln2_pe_prev
                for m in range(NCH):
                    cn = cop({"dS": dS_done, "pe": pe_prev},
                             lambda cc, m=m: cc.activation(
                                 sqb2[:, ts(m)], R[m][:, tsl], AF.Square))
                    vn = vop({"pe": pe_prev},
                             lambda vv, m=m: vv.tensor_copy(
                                 rcp2[:, ts(m)], R[m][:, tsl]))
                    waits = {"sc": cn, "v": vn}
                    if m == 0:
                        for kk_, vv_ in g_st.items():
                            waits[kk_] = max(waits.get(kk_, 0), vv_)

                    def fst(tt, m=m):
                        tt.matmul(ps_st[:1, 0:W], onescol_bf[:],
                                  rcp2[:, ts(m)], start=(m == 0),
                                  stop=(m == NCH - 1))
                        return tt.matmul(ps_st[:1, 512:512 + W],
                                         onescol_bf[:], sqb2[:, ts(m)],
                                         start=(m == 0), stop=(m == NCH - 1))
                    pe_prev = top(waits, fst)
                pe_stats = pe_prev
                ln2_pe_prev = pe_stats
                mu2 = ln2r[:, 0 * W:1 * W]
                var2 = ln2r[:, 1 * W:2 * W]
                s2 = ln2r[:, 2 * W:3 * W]
                t2r = ln2r[:, 3 * W:4 * W]
                vn = vop({"pe": pe_stats}, lambda vv: vv.tensor_scalar_mul(
                    mu2, ps_st[:1, 0:W], 1.0 / DM))
                vn = vop({"v": vn}, lambda vv: vv.tensor_mul(var2, mu2, mu2))
                vn = vop({"v": vn}, lambda vv: vv.scalar_tensor_tensor(
                    var2, ps_st[:1, 512:512 + W], 1.0 / DM, var2, ALU.mult,
                    ALU.subtract))
                cn = cop({"v": vn}, lambda cc: cc.activation(
                    s2, var2, AF.Sqrt,
                    bias=consts[:1, SC_LNEPS:SC_LNEPS + 1]))
                vn = vop({"sc": cn}, lambda vv: vv.reciprocal(s2, s2))
                vrows = vop({"v": vn}, lambda vv: vv.scalar_tensor_tensor(
                    t2r, mu2, -1.0, s2, ALU.mult, ALU.mult))
                prot.release({"v": vrows})

                ps_bc, g_bc = acq()
                g_bc["v"] = max(g_bc.get("v", 0), vrows)

                def fbc(tt):
                    tt.matmul(ps_bc[:, 0:W], ones1_f32[:], s2[:1, :],
                              start=True, stop=True)
                    return tt.matmul(ps_bc[:, 512:512 + W], ones1_f32[:],
                                     t2r[:1, :], start=True, stop=True)
                pe_bc = top(g_bc, fbc)

                vap = 0
                for m in range(NCH):
                    vap = vop({"pe": pe_bc, "v": v_bn_prev},
                              lambda vv, m=m: vv.tensor_tensor(
                                  scrA[:, ts(m)], R[m][:, tsl],
                                  ps_bc[:, 0:W], ALU.mult))
                    vap = vop({"v": vap},
                              lambda vv, m=m: vv.tensor_tensor(
                                  xnb[:, ts(m)], scrA[:, ts(m)],
                                  ps_bc[:, 512:512 + W], ALU.add))
                v_ln2 = vap
                prot.release({"v": v_ln2})
                vxx = vop({"v": v_ln2}, lambda vv: vv.tensor_tensor(
                    v3(xx)[:, :, 1:W], v3(xnb)[:, :, 0:W - 1],
                    v3(xnb)[:, :, 1:W], ALU.subtract))
                vxx = vop({"v": vxx}, lambda vv: vv.tensor_tensor(
                    v3(xx)[:, :, 0:1],
                    bnd.rearrange("p (m o) -> p m o", o=1),
                    v3(xnb)[:, :, 0:1], ALU.subtract))
                v_bn_prev = vop({"v": vxx}, lambda vv: vv.tensor_copy(
                    bnd[:], xnb[:, W - 1::W]))

                vmx = vop({"v": vxx}, lambda vv: vv.tensor_tensor(
                    v3(scrB), v3(xx), cv3(CV_TMX), ALU.mult))
                pmx = pop({"v": vmx}, lambda pp: pp.tensor_tensor(
                    xmv[:, :], scrB[:, :], xnb[:, :], ALU.add))

                ps_lo, g_lo = acq()
                g_lo["pl"] = max(g_lo.get("pl", 0), pmx)
                g_lo["dS"] = dS_done

                def flo(tt):
                    for half in (0, 1):
                        for kc in range(NCH):
                            ins = tt.matmul(
                                ps_lo[0:64, half * 512:half * 512 + W],
                                maa1[:, kc * 128 + half * 64:
                                     kc * 128 + half * 64 + 64],
                                xmv[:, ts(kc)], start=(kc == 0),
                                stop=(kc == NCH - 1))
                    return ins
                pe_lo = top(g_lo, flo)
                cn1 = cop({"pe": pe_lo}, lambda cc: cc.activation(
                    losb[:, 0:128], ps_lo[0:64, 0:W], AF.Tanh))
                cn2 = cop({"sc": cn1}, lambda cc: cc.activation(
                    losb[:, 128:256], ps_lo[0:64, 512:512 + W], AF.Tanh))
                prot.release({"sc": cn2})

                xm_outs = [xmr, xmw, xmk, xmv]
                xm_ready = {}
                v_lo2_prev = vmx
                prev_pool = pmx
                for gi in range(4):
                    ps_g, g_g = acq()
                    g_g["sc"] = max(g_g.get("sc", 0), cn2)
                    if gi == 3:
                        g_g["pe"] = max(g_g.get("pe", 0), pe_lo)
                    m2ap = maa2a if gi < 2 else maa2b
                    pbase = 32 * (gi % 2)
                    lslice = losb[pbase:pbase + 32,
                                  (gi // 2) * 128:(gi // 2) * 128 + W]

                    def flo2(tt, m2ap=m2ap, pbase=pbase, lslice=lslice,
                             ps_g=ps_g):
                        for m in range(NCH):
                            ins = tt.matmul(ps_g[:, m * W:(m + 1) * W],
                                            m2ap[pbase:pbase + 32, ts(m)],
                                            lslice, start=True, stop=True)
                        return ins
                    pe_g = top(g_g, flo2)
                    vs = vop({"pe": pe_g, "v": v_lo2_prev,
                              "pl": prev_pool},
                             lambda vv, ps_g=ps_g, gi=gi: vv.tensor_tensor(
                                 v3(scrA), v3(ps_g), cv3(CV_TMAA0 + gi),
                                 ALU.add))
                    vs = vop({"v": vs}, lambda vv: vv.tensor_tensor(
                        scrA[:, :], scrA[:, :], xx[:, :], ALU.mult))
                    pw = {"v": vs}
                    if gi == 3:
                        pw["pl"] = pmx
                        pw["pe"] = pe_g
                    pn = pop(pw, lambda pp, gi=gi:
                             pp.tensor_tensor(xm_outs[gi][:, :], scrA[:, :],
                                              xnb[:, :], ALU.add))
                    xm_ready[gi] = pn
                    prev_pool = pn
                    prot.release({"v": vs})
                    v_lo2_prev = vs

                def load_after(pe_n, slot_sb, nm, half):
                    def f(g, pe_n=pe_n, slot_sb=slot_sb, nm=nm, half=half):
                        if pe_n:
                            g.wait_ge(b.h("pe"), pe_n)
                        sem = "dA" if slot_sb is wsA else "dB"
                        g.dma_start(
                            slot_sb.rearrange("p (kc x) -> p kc x", kc=8),
                            d[nm][:, half * 512:(half + 1) * 512].rearrange(
                                "(kc p) x -> p kc x", p=128)
                        ).then_inc(b.h(sem), 16)
                    b.g(f)

                def proj_half(slot_sb, src_xm, dst, half, wsem, wn, xm_n):
                    ps_p, g_p = acq()
                    g_p[wsem] = max(g_p.get(wsem, 0), wn)
                    g_p["pl"] = max(g_p.get("pl", 0), xm_n)

                    def fp(tt, slot_sb=slot_sb, src_xm=src_xm, ps_p=ps_p,
                           half=half):
                        for mo in range(4):
                            for kc in range(NCH):
                                ins = tt.matmul(
                                    ps_p[:, mo * W:(mo + 1) * W],
                                    slot_sb[:, kc * 512 + mo * 128:
                                            kc * 512 + (mo + 1) * 128],
                                    src_xm[:, ts(kc)], start=(kc == 0),
                                    stop=(kc == NCH - 1))
                        return ins
                    pe_p = top(g_p, fp)
                    vn = vop({"pe": pe_p}, lambda vv, dst=dst, ps_p=ps_p,
                             half=half: vv.tensor_copy(
                                 dst[:, half * 512:(half + 1) * 512],
                                 ps_p[:, 0:512]))
                    prot.release({"v": vn})
                    return pe_p, vn

                pe_r0, v_r0 = proj_half(wsA, xmr, rT, 0, "dA", dA_cur,
                                        xm_ready[0])
                dA_cur = b.r("dA", 16)
                load_after(pe_r0, wsA, "wrT", 1)
                pe_r1, v_r1 = proj_half(wsA, xmr, rT, 1, "dA", dA_cur,
                                        xm_ready[0])
                dA_cur = b.r("dA", 16)
                load_after(pe_r1, wsA, "wvT", 0)
                pe_r, v_r = pe_r1, max(v_r0, v_r1)

                pe_k0, v_k0 = proj_half(wsB, xmk, kraw, 0, "dB", dB_cur,
                                        xm_ready[2])
                dB_cur = b.r("dB", 16)
                load_after(pe_k0, wsB, "wkT", 1)
                pe_k1, v_k1 = proj_half(wsB, xmk, kraw, 1, "dB", dB_cur,
                                        xm_ready[2])
                dB_cur = b.r("dB", 16)
                load_after(pe_k1, wsB, "woT", 0)
                pe_k, v_k = pe_k1, max(v_k0, v_k1)
                dB_wo0 = dB_cur

                v_v = 0
                pe_v = 0
                for nsI in (0, 1):
                    ps_v, g_v = acq()
                    g_v["dA"] = max(g_v.get("dA", 0), dA_cur)
                    g_v["pl"] = max(g_v.get("pl", 0), xm_ready[3])

                    def fv_(tt, ps_v=ps_v, nsI=nsI):
                        for kc in range(NCH):
                            ins = tt.matmul(
                                ps_v[:, 0:512], xmv[:, ts(kc)],
                                wsA[:, kc * 512:(kc + 1) * 512],
                                start=(kc == 0), stop=(kc == NCH - 1))
                        return ins
                    pe_v = top(g_v, fv_)
                    v_v = vop({"pe": pe_v, "v": v_v},
                              lambda vv, ps_v=ps_v, nsI=nsI:
                              vv.tensor_copy(vtm[:, nsI * 512:
                                                 (nsI + 1) * 512],
                                             ps_v[:, 0:512]))
                    prot.release({"v": v_v})
                    dA_cur = b.r("dA", 16)
                    if nsI == 0:
                        load_after(pe_v, wsA, "wvT", 1)
                    else:
                        load_after(pe_v, wsA, "wrT", 0)

                ps_g1, g_g1 = acq()
                g_g1["dS"] = dS_done
                g_g1["pl"] = max(g_g1.get("pl", 0), xm_ready[0])

                def fg1(tt, ps_g1=ps_g1):
                    for kc in range(NCH):
                        ins = tt.matmul(ps_g1[:, 0:W],
                                        gw1[:, kc * 128:(kc + 1) * 128],
                                        xmr[:, ts(kc)], start=(kc == 0),
                                        stop=(kc == NCH - 1))
                    return ins
                pe_g1 = top(g_g1, fg1)
                c_g1 = cop({"pe": pe_g1}, lambda cc, ps_g1=ps_g1:
                           cc.activation(gt1[:, :], ps_g1[:, 0:W], AF.Tanh))
                prot.release({"sc": c_g1})
                ps_g2, g_g2 = acq()
                g_g2["sc"] = max(g_g2.get("sc", 0), c_g1)

                def fg2(tt, ps_g2=ps_g2):
                    for m in range(NCH):
                        ins = tt.matmul(ps_g2[:, m * W:(m + 1) * W],
                                        gw2[:, ts(m)], gt1[:, :],
                                        start=True, stop=True)
                    return ins
                pe_g2 = top(g_g2, fg2)
                v_g2 = vop({"pe": pe_g2}, lambda vv, ps_g2=ps_g2:
                           vv.tensor_copy(gTt[:, :], ps_g2[:, :]))
                prot.release({"v": v_g2})

                ps_d1, g_d1 = acq()
                g_d1["pl"] = max(g_d1.get("pl", 0), xm_ready[1])
                g_d1["dS"] = dS_done

                def fd1(tt, ps_d1=ps_d1):
                    for kc in range(NCH):
                        ins = tt.matmul(ps_d1[0:64, 0:W],
                                        dw1[:, kc * 64:(kc + 1) * 64],
                                        xmw[:, ts(kc)], start=(kc == 0),
                                        stop=(kc == NCH - 1))
                    return ins
                pe_d1 = top(g_d1, fd1)
                c_d1 = cop({"pe": pe_d1}, lambda cc, ps_d1=ps_d1:
                           cc.activation(dt1[:, :], ps_d1[0:64, 0:W],
                                         AF.Tanh))
                prot.release({"sc": c_d1})
                ps_d2, g_d2 = acq()
                g_d2["sc"] = max(g_d2.get("sc", 0), c_d1)

                def fd2(tt, ps_d2=ps_d2):
                    for m in range(NCH):
                        ins = tt.matmul(ps_d2[:, m * W:(m + 1) * W],
                                        dw2[:, ts(m)], dt1[:, :],
                                        start=True, stop=True)
                    return ins
                pe_d2 = top(g_d2, fd2)
                vn_u = vop({"pe": pe_d2, "pl": xm_ready[3]},
                           lambda vv, ps_d2=ps_d2: vv.tensor_tensor(
                               v3(scrA), v3(ps_d2), cv3(CV_TD), ALU.add))
                prot.release({"v": vn_u})
                v_m1 = vop({"v": vn_u, "pl": pmx}, lambda vv:
                           vv.tensor_scalar(scrB[:, :], scrA[:, :], -1.0,
                                            0.0, ALU.mult, ALU.max))
                v_au = vop({"v": v_m1}, lambda vv:
                           vv.scalar_tensor_tensor(scrA[:, :], scrB[:, :],
                                                   2.0, scrA[:, :],
                                                   ALU.mult, ALU.add))
                c_ex = cop({"v": v_au}, lambda cc: cc.activation(
                    scrA[:, :], scrA[:, :], AF.Exp, scale=-1.0))
                c_ln = cop({"sc": c_ex}, lambda cc: cc.activation(
                    scrA[:, :], scrA[:, :], AF.Ln, bias=1.0))
                v_sp = vop({"sc": c_ln}, lambda vv:
                           vv.tensor_tensor(scrA[:, :], scrA[:, :],
                                            scrB[:, :], ALU.add))
                v_wd = vop({"v": v_sp}, lambda vv: vv.tensor_scalar(
                    wdec[:, :], scrA[:, :], -1.0, -0.5, ALU.mult, ALU.add))
                c_lw = cop({"v": v_wd}, lambda cc: cc.activation(
                    scrB[:, :], wdec[:, :], AF.Exp))
                v_lw = vop({"sc": c_lw}, lambda vv: vv.tensor_scalar_mul(
                    lwt[:, :], scrB[:, :], -1.0))

                ps_kb, g_kb = acq()
                g_kb["pl"] = max(g_kb.get("pl", 0), xm_ready[3])
                g_kb["dS"] = dS_done

                def fkb(tt, ps_kb=ps_kb):
                    for kc in range(NCH):
                        tt.matmul(ps_kb[0:16, 0:W],
                                  bw1[:, kc * 64:kc * 64 + 16],
                                  xmk[:, ts(kc)], start=(kc == 0),
                                  stop=(kc == NCH - 1))
                    for kc in range(NCH):
                        tt.matmul(ps_kb[32:48, 0:W],
                                  bw1[:, kc * 64 + 16:kc * 64 + 32],
                                  xmw[:, ts(kc)], start=(kc == 0),
                                  stop=(kc == NCH - 1))
                    for kc in range(NCH):
                        tt.matmul(ps_kb[0:16, 512:512 + W],
                                  bw1[:, kc * 64 + 32:kc * 64 + 48],
                                  xmw[:, ts(kc)], start=(kc == 0),
                                  stop=(kc == NCH - 1))
                    for kc in range(NCH):
                        ins = tt.matmul(ps_kb[32:48, 512:512 + W],
                                        bw1[:, kc * 64 + 48:kc * 64 + 64],
                                        xmk[:, ts(kc)], start=(kc == 0),
                                        stop=(kc == NCH - 1))
                    return ins
                pe_kb = top(g_kb, fkb)
                c_kb = cop({"pe": pe_kb}, lambda cc, ps_kb=ps_kb:
                           cc.activation(kbt[:, :], ps_kb[0:16, 0:W],
                                         AF.Tanh))
                v_ab2 = vop({"pe": pe_kb}, lambda vv, ps_kb=ps_kb:
                            vv.tensor_copy(abt[32:48, :], ps_kb[32:48, 0:W]))
                v_ma = vop({"v": v_ab2}, lambda vv, ps_kb=ps_kb:
                           vv.tensor_copy(mabt[:, :],
                                          ps_kb[0:16, 512:512 + W]))
                v_mk = vop({"v": v_ma}, lambda vv, ps_kb=ps_kb:
                           vv.tensor_copy(mkbt[32:48, :],
                                          ps_kb[32:48, 512:512 + W]))
                c_mk = c_kb
                prot.release({"v": v_mk, "sc": c_kb})

                ps_kk, g_kk = acq()
                g_kk["sc"] = max(g_kk.get("sc", 0), c_kb)
                g_kk["v"] = max(g_kk.get("v", 0), v_mk)

                def fkk(tt, ps_kk=ps_kk):
                    for m in range(NCH):
                        ins = tt.matmul(ps_kk[:, m * W:(m + 1) * W],
                                        bw2ka[0:16, ts(m)], kbt[:, :],
                                        start=True, stop=True)
                    return ins
                pe_kk = top(g_kk, fkk)
                v_kk = vop({"pe": pe_kk, "v": v_k}, lambda vv, ps_kk=ps_kk:
                           vv.tensor_tensor(kkn[:, :], ps_kk[:, :],
                                            kraw[:, :], ALU.add))
                prot.release({"v": v_kk})
                c_sq = cop({"v": v_kk, "pe": pe_stats},
                           lambda cc: cc.activation(
                               sqb2[:, :], kkn[:, :], AF.Square))
                ps_nr, g_nr = acq()
                g_nr["sc"] = max(g_nr.get("sc", 0), c_sq)

                def fnr(tt, ps_nr=ps_nr):
                    for m in range(NCH):
                        ins = tt.matmul(ps_nr[0:16, 0:W],
                                        hselt_bf[:, m * 16:(m + 1) * 16],
                                        sqb2[:, ts(m)], start=(m == 0),
                                        stop=(m == NCH - 1))
                    return ins
                pe_nr = top(g_nr, fnr)
                c_nr = cop({"pe": pe_nr}, lambda cc, ps_nr=ps_nr:
                           cc.activation(rnsb[:, :], ps_nr[0:16, 0:W],
                                         AF.Ln,
                                         bias=consts[0:16,
                                                     SC_EPS30:SC_EPS30 + 1]))
                c_nr = cop({"sc": c_nr}, lambda cc: cc.activation(
                    rnsb[:, :], rnsb[:, :], AF.Exp, scale=-0.5))
                prot.release({"sc": c_nr})
                v_nr = vop({"sc": c_nr}, lambda vv: vv.tensor_scalar_min(
                    rnsb[:, :], rnsb[:, :], 1e12))
                ps_nb, g_nb = acq()
                g_nb["v"] = max(g_nb.get("v", 0), v_nr)

                def fnb(tt, ps_nb=ps_nb):
                    for m in range(NCH):
                        ins = tt.matmul(ps_nb[:, m * W:(m + 1) * W],
                                        hsel[:, m * 128:(m + 1) * 128],
                                        rnsb[:, :], start=True, stop=True)
                    return ins
                pe_nb = top(g_nb, fnb)
                v_kkn = vop({"pe": pe_nb}, lambda vv, ps_nb=ps_nb:
                            vv.tensor_tensor(kkn[:, :], kkn[:, :],
                                             ps_nb[:, :], ALU.mult))
                prot.release({"v": v_kkn})

                def expand_gate(lhs_ap, rhs_sb, cvcol, dst, wait_sc,
                                wait_v=0):
                    ps_e, g_e = acq()
                    g_e["sc"] = max(g_e.get("sc", 0), wait_sc)
                    if wait_v:
                        g_e["v"] = max(g_e.get("v", 0), wait_v)

                    def fe(tt, lhs_ap=lhs_ap, rhs_sb=rhs_sb, ps_e=ps_e):
                        for m in range(NCH):
                            ins = tt.matmul(ps_e[:, m * W:(m + 1) * W],
                                            lhs_ap[ts(m)] if False else
                                            lhs_ap[:, ts(m)], rhs_sb[:, :],
                                            start=True, stop=True)
                        return ins
                    pe_e = top(g_e, fe)
                    vn = vop({"pe": pe_e, "sc": c_lw},
                             lambda vv, ps_e=ps_e, cvcol=cvcol:
                             vv.tensor_tensor(v3(scrB), v3(ps_e),
                                              cv3(cvcol), ALU.add))
                    cs = cop({"v": vn}, lambda cc, dst=dst: cc.activation(
                        dst[:, :], scrB[:, :], AF.Sigmoid))
                    prot.release({"v": vn})
                    return cs

                c_al = expand_gate(bw2ka[32:48, :], abt[32:48, :],
                                   CV_AAAAA, alpha, c_kb, v_mk)
                c_mag = expand_gate(bw2mm[0:16, :], mabt, CV_MISCA, mag,
                                    c_al, v_mk)
                c_mkg = expand_gate(bw2mm[32:48, :], mkbt[32:48, :],
                                     CV_MISCK, mkg, c_mag, v_mk)

                v_f1 = vop({"sc": c_mag}, lambda vv: vv.tensor_scalar(
                    sbA[:, :], mag[:, :], -1.0, 1.0, ALU.mult, ALU.add))
                v_f2 = vop({"v": v_f1, "sc": c_al},
                           lambda vv: vv.tensor_tensor(
                               sbB[:, :], alpha[:, :], sbA[:, :], ALU.mult))
                v_f3 = vop({"v": v_f2}, lambda vv: vv.tensor_tensor(
                    sbA[:, :], mag[:, :], sbB[:, :], ALU.add))
                v_ks = vop({"v": v_f3}, lambda vv: vv.tensor_tensor(
                    sbB[:, :], kraw[:, :], sbA[:, :], ALU.mult))
                v_e1 = vop({"v": v_ks, "sc": c_mkg},
                           lambda vv: vv.tensor_tensor(
                               sbA[:, :], wdec[:, :], mkg[:, :], ALU.mult))
                c_e2 = cop({"v": v_e1}, lambda cc: cc.activation(
                    sbA[:, :], sbA[:, :], AF.Exp))
                v_kf = vop({"sc": c_e2}, lambda vv: vv.tensor_tensor(
                    kf[:, :], sbB[:, :], sbA[:, :], ALU.mult))

                v_base = vop({"v": v_lw}, lambda vv: vv.tensor_copy(
                    basec[:, :], carry[:, :]))
                scans = v_base
                for m in range(NCH):
                    scans = vop({"v": scans}, lambda vv, m=m:
                                vv.tensor_tensor_scan(
                                    Lct[:, ts(m)], lwt[:, ts(m)],
                                    z128, carry[:, m:m + 1],
                                    ALU.add, ALU.add))
                v_cu = vop({"v": scans}, lambda vv: vv.tensor_copy(
                    carry[:, :], Lct[:, W - 1::W]))
                v_rel = vop({"v": v_cu}, lambda vv: vv.tensor_tensor(
                    v3(Lct), v3(Lct),
                    basec.rearrange("p (m o) -> p m o", o=1).broadcast_to(
                        (128, 8, W)), ALU.subtract))
                c_di = cop({"v": v_rel, "sc": c_e2},
                           lambda cc: cc.activation(
                               dinc[:, :], Lct[:, :], AF.Exp))
                c_dl = cop({"sc": c_di}, lambda cc: cc.activation(
                    dlastf[:, :], Lct[:, W - 1::W], AF.Exp))
                c_dv = cop({"sc": c_di}, lambda cc: cc.activation(
                    dinv[:, :], Lct[:, :], AF.Exp, scale=-1.0))
                v_de = vop({"v": v_rel}, lambda vv: vv.tensor_tensor(
                    scrA[:, :], Lct[:, :], lwt[:, :], ALU.subtract))
                c_de = cop({"v": v_de, "sc": c_dv, "v2": 0}, lambda cc:
                           cc.activation(sbA[:, :], scrA[:, :], AF.Exp))

                v_Qb = vop({"sc": c_di, "v": v_r}, lambda vv:
                           vv.tensor_tensor(Qbt[:, :], rT[:, :], dinc[:, :],
                                            ALU.mult))
                v_Ab = vop({"sc": c_de, "v": v_kkn}, lambda vv:
                           vv.tensor_tensor(Abt[:, :], kkn[:, :], sbA[:, :],
                                            ALU.mult))
                v_b1 = vop({"sc": c_dv, "v": v_Ab}, lambda vv:
                           vv.scalar_tensor_tensor(sbB[:, :], alpha[:, :],
                                                   -1.0, dinv[:, :],
                                                   ALU.mult, ALU.mult))
                v_Bt = vop({"v": v_b1}, lambda vv: vv.tensor_tensor(
                    Btt[:, :], kkn[:, :], sbB[:, :], ALU.mult))
                v_Kt = vop({"v": v_kf, "sc": c_dv}, lambda vv:
                           vv.tensor_tensor(Ktt[:, :], kf[:, :], dinv[:, :],
                                            ALU.mult))
                dl3 = v3(dinc)[:, :, W - 1:W].broadcast_to((128, 8, W))
                v_Bh = vop({"v": v_Bt}, lambda vv: vv.tensor_tensor(
                    v3(sbA), v3(Btt), dl3, ALU.mult))
                v_Kh = vop({"v": v_Kt}, lambda vv: vv.tensor_tensor(
                    v3(sbB), v3(Ktt), dl3, ALU.mult))

                tmv = {}
                prev_solve_pe = 0
                for nmsrc, src, dst, vwait in (("Ab", Abt, Atm, v_Ab),
                                               ("Bh", sbA, Bhtm, v_Bh),
                                               ("Kh", sbB, Khtm, v_Kh)):
                    ps_t, g_t = acqT()

                    def ftr(tt, src=src, ps_t=ps_t):
                        for m in range(NCH):
                            ins = tt.transpose(ps_t[:, m * W:(m + 1) * W],
                                               src[:, ts(m)], ident_bf[:])
                        return ins
                    wt = dict(g_t)
                    wt["v"] = max(wt.get("v", 0), vwait)
                    pe_t = top(wt, ftr)
                    vt = vop({"pe": pe_t}, lambda vv, dst=dst, ps_t=ps_t:
                             vv.tensor_copy(dst[:, :], ps_t[:, :1024]))
                    ptrot.release({"v": vt})
                    tmv[nmsrc] = vt

                vmats = max(v_Qb, v_Ab, v_Bt, v_Kt)
                vtms = max(tmv["Ab"], tmv["Bh"], tmv["Kh"], v_v)
                y_evs = []
                last_state_pe = 0

                def gmax(*ds):
                    out = {}
                    for dd in ds:
                        if not dd:
                            continue
                        for k2, v2 in dd.items():
                            out[k2] = max(out.get(k2, 0), v2)
                    return out

                def ecopy(h, waits, dst, src):
                    if h % 2 == 0:
                        return {"v": vop(waits, lambda vv, dst=dst, src=src:
                                         vv.tensor_copy(dst, src))}
                    return {"sc": cop(waits, lambda cc, dst=dst, src=src:
                                      cc.copy(dst, src))}

                SH = {}
                for h in range(H):
                    mm_ = h // 2
                    pb = 64 * (h % 2)
                    csl = slice(mm_ * W, (mm_ + 1) * W)
                    SH[h] = dict(
                        i=h % NILV, pb=pb, mm_=mm_,
                        Af=Abt[pb:pb + 64, csl], Bf=Btt[pb:pb + 64, csl],
                        Kf=Ktt[pb:pb + 64, csl], Qf=Qbt[pb:pb + 64, csl],
                        Vts=vtm[:, h * 64:h * 64 + 64],
                        Bhs=Bhtm[:, h * 64:h * 64 + 64],
                        Khs=Khtm[:, h * 64:h * 64 + 64],
                        STbs=STb[pb:pb + 64, mm_ * 64:mm_ * 64 + 64],
                        STfs=STf[pb:pb + 64, mm_ * 64:mm_ * 64 + 64],
                    )

                for hg in range(0, H, NILV):
                    hh = list(range(hg, min(hg + NILV, H)))
                    for h in hh:
                        s = SH[h]
                        i = s["i"]
                        sg = {"pe": slot_free_pe.get(i, 0)}
                        ps1, gu1 = acq()
                        pe1 = top(gmax(gu1, {"v": vmats}, sg),
                                  lambda tt, ps1=ps1, s=s: tt.matmul(
                                      ps1[:, 0:128], s["Af"], s["Bf"],
                                      start=True, stop=True))
                        vN = vop(gmax({"pe": pe1}, sg),
                                 lambda vv, ps1=ps1, i=i: vv.tensor_tensor(
                                     NsbS[i][:, :], ps1[:, 0:128],
                                     mask_sl[:, :], ALU.mult))
                        prot.release({"v": vN})
                        s["gN"] = {"v": vN}
                    for h in hh:
                        s = SH[h]
                        i = s["i"]
                        sg = {"pe": slot_free_pe.get(i, 0)}
                        ps2_, gu2 = acq()

                        def f13(tt, ps2_=ps2_, s=s):
                            tt.matmul(ps2_[:, 0:128], s["Bf"], s["Af"],
                                      start=True, stop=True)
                            return tt.matmul(ps2_[:, 128:256], s["Bf"],
                                             s["Qf"], start=True, stop=True)
                        pe2 = top(gmax(gu2, {"v": vmats}, sg), f13)
                        v13 = vop(gmax({"pe": pe2}, sg),
                                  lambda vv, ps2_=ps2_, i=i:
                                  vv.tensor_tensor(nt3tS[i][:, :],
                                                   ps2_[:, 0:256],
                                                   mask_t[:, :], ALU.mult))
                        prot.release({"v": v13})
                        s["g13"] = {"v": v13}
                    for h in hh:
                        s = SH[h]
                        i = s["i"]
                        sg = {"pe": slot_free_pe.get(i, 0)}
                        ps3, gu3 = acq()

                        def f24(tt, ps3=ps3, s=s):
                            tt.matmul(ps3[:, 0:128], s["Kf"], s["Af"],
                                      start=True, stop=True)
                            return tt.matmul(ps3[:, 128:256], s["Kf"],
                                             s["Qf"], start=True, stop=True)
                        pe3 = top(gmax(gu3, {"v": vmats}, sg), f24)
                        v24 = vop(gmax({"pe": pe3}, sg),
                                  lambda vv, ps3=ps3, i=i:
                                  vv.tensor_tensor(t2t4S[i][:, :],
                                                   ps3[:, 0:256],
                                                   mask_t[:, :], ALU.mult))
                        prot.release({"v": v24})
                        s["g24"] = {"v": v24}
                        s["P"] = NsbS[i][:, :]
                        s["PT"] = nt3tS[i][:, 0:128]
                        s["gP"] = s["gN"]
                        s["gPT"] = s["g13"]
                    for lev in range(1, 7):
                        for h in hh:
                            s = SH[h]
                            i = s["i"]
                            psq, gsq = acq()

                            def fsq(tt, psq=psq, PTc=s["PT"], Pc=s["P"]):
                                tt.matmul(psq[:, 0:128], PTc, Pc,
                                          start=True, stop=True)
                                return tt.matmul(psq[:, 512:640], Pc, PTc,
                                                 start=True, stop=True)
                            pesq = top(gmax(gsq, s["gP"], s["gPT"]), fsq)
                            Pnew = (Pb0S[i] if lev % 2 == 1 else Pb1S[i])
                            gPn = ecopy(h, {"pe": pesq}, Pnew[:, :],
                                        psq[:, 0:128])
                            PTnew = PTtS[i][:, (lev - 1) * 128:lev * 128]
                            gPTn = ecopy(h, gmax({"pe": pesq}, gPn),
                                         PTnew, psq[:, 512:640])
                            prot.release(gmax(gPn, gPTn))
                            s["P"] = Pnew[:, :]
                            s["PT"] = PTnew
                            s["gP"] = gPn
                            s["gPT"] = gPTn
                    for h in hh:
                        s = SH[h]
                        i = s["i"]
                        ps4, gu4 = acq()

                        def fw0(tt, s=s, ps4=ps4, i=i):
                            tt.matmul(ps4[:, 0:64], s["Af"], s["STbs"],
                                      start=True, stop=False)
                            return tt.matmul(ps4[:, 0:64],
                                             t2t4S[i][:, 0:128], s["Vts"],
                                             start=False, stop=True)
                        pew0 = top(gmax(gu4, s["g24"], {"v": vtms},
                                        stv[h]), fw0)
                        gX = ecopy(h, {"pe": pew0}, Xc0S[i][:, :],
                                   ps4[:, 0:64])
                        prot.release(gX)
                        s["Xcur"], s["Xnxt"] = Xc0S[i], Xc1S[i]
                        s["gX"] = gX
                    for k in range(6, -1, -1):
                        for h in hh:
                            s = SH[h]
                            i = s["i"]
                            PTk = (PTtS[i][:, (k - 1) * 128:k * 128]
                                   if k >= 1 else nt3tS[i][:, 0:128])
                            psc, gc = acq()

                            def fch(tt, psc=psc, PTk=PTk, Xc=s["Xcur"]):
                                tt.matmul(psc[:, 0:64], ident_bf[:, :],
                                          Xc[:, :], start=True, stop=False)
                                return tt.matmul(psc[:, 0:64], PTk,
                                                 Xc[:, :],
                                                 start=False, stop=True)
                            pech = top(gmax(gc, s["gX"], s["gPT"],
                                            s["g13"]), fch)
                            gX = ecopy(h, {"pe": pech}, s["Xnxt"][:, :],
                                       psc[:, 0:64])
                            prot.release(gX)
                            s["Xcur"], s["Xnxt"] = s["Xnxt"], s["Xcur"]
                            s["gX"] = gX
                    for h in hh:
                        s = SH[h]
                        i = s["i"]
                        ps5, gu5 = acq()

                        def fy(tt, ps5=ps5, s=s, i=i, Xc=s["Xcur"]):
                            tt.matmul(ps5[:, 0:64], s["Qf"], s["STbs"],
                                      start=True, stop=False)
                            tt.matmul(ps5[:, 0:64], nt3tS[i][:, 128:256],
                                      Xc[:, :], start=False, stop=False)
                            return tt.matmul(ps5[:, 0:64],
                                             t2t4S[i][:, 128:256], s["Vts"],
                                             start=False, stop=True)
                        pey = top(gmax(gu5, s["gX"], s["g24"], s["g13"],
                                       stv[h]), fy)
                        gY = ecopy(h, {"pe": pey},
                                   ytm[:, h * 64:h * 64 + 64],
                                   ps5[:, 0:64])
                        prot.release(gY)
                        y_evs.append(gY)
                        s["gY"] = gY
                        s["peY"] = pey
                    for h in hh:
                        s = SH[h]
                        i = s["i"]
                        pb = s["pb"]
                        mm_ = s["mm_"]
                        vdd = vop(gmax({"sc": c_dl}, s["gY"]),
                                  lambda vv, pb=pb, mm_=mm_, i=i:
                                  vv.tensor_scalar(
                                      ddlS[i][pb:pb + 64, :],
                                      ident_bf[pb:pb + 64, pb:pb + 64],
                                      dlastf[pb:pb + 64, mm_:mm_ + 1],
                                      None, ALU.mult))
                        ps6, gu6 = acq()

                        def fs6(tt, ps6=ps6, s=s, pb=pb, i=i,
                                Xc=s["Xcur"]):
                            tt.matmul(ps6[pb:pb + 64, 0:64],
                                      ddlS[i][pb:pb + 64, :], s["STbs"],
                                      start=True, stop=False)
                            tt.matmul(ps6[pb:pb + 64, 0:64], s["Bhs"],
                                      Xc[:, :], start=False, stop=False)
                            return tt.matmul(ps6[pb:pb + 64, 0:64],
                                             s["Khs"], s["Vts"],
                                             start=False, stop=True)
                        pes6 = top(gmax(gu6, {"v": vdd}, s["gX"],
                                        {"v": vtms}, {"pe": s["peY"]}), fs6)
                        g1_ = ecopy(h, {"pe": pes6}, s["STfs"],
                                    ps6[pb:pb + 64, 0:64])
                        g2_ = ecopy(h, gmax({"pe": pes6}, g1_), s["STbs"],
                                    ps6[pb:pb + 64, 0:64])
                        prot.release(gmax(g1_, g2_))
                        stv[h] = gmax(g1_, g2_)
                        slot_free_pe[i] = pes6
                        last_state_pe = pes6

                gy_all = gmax(*y_evs)
                y3 = ytm.rearrange("p (g i) -> p g i", g=16)
                c_ys = cop(gmax(gy_all, {"pe": last_state_pe}),
                           lambda cc: cc.activation(sbA[:, :DM], ytm[:, :],
                                                    AF.Square))
                sbA3 = sbA[:, :DM].rearrange("p (g i) -> p g i", g=16)
                musum = m16[:, 0:16]
                sqsum = m16[:, 16:32]
                muv = m16[:, 32:48]
                mus2 = m16[:, 48:64]
                varv = m16[:, 64:80]
                rstd = m16[:, 80:96]
                vgs = vop(gmax({"sc": c_ys}, gy_all),
                          lambda vv: vv.reduce_sum(
                              musum.rearrange("p (g o) -> p g o", o=1), y3,
                              axis=mybir.AxisListType.X))
                vgs = vop({"v": vgs}, lambda vv: vv.reduce_sum(
                    sqsum.rearrange("p (g o) -> p g o", o=1), sbA3,
                    axis=mybir.AxisListType.X))
                vgs = vop({"v": vgs}, lambda vv: vv.tensor_scalar_mul(
                    muv, musum, 1.0 / HS))
                vgs = vop({"v": vgs}, lambda vv: vv.tensor_mul(
                    mus2, muv, muv))
                vgs = vop({"v": vgs}, lambda vv: vv.scalar_tensor_tensor(
                    varv, sqsum, 1.0 / HS, mus2, ALU.mult, ALU.subtract))
                c_rs = cop({"v": vgs}, lambda cc: cc.activation(
                    rstd, varv, AF.Ln,
                    bias=consts[:, SC_GNEPS:SC_GNEPS + 1]))
                c_rs = cop({"sc": c_rs}, lambda cc: cc.activation(
                    rstd, rstd, AF.Exp, scale=-0.5))
                v_rs = c_rs
                v_rs = vop({"sc": c_rs}, lambda vv: vv.tensor_copy(
                    rstd, rstd)) if False else 0
                mu3 = muv.rearrange("p (g o) -> p g o", o=1).broadcast_to(
                    (128, 16, HS))
                rs3 = rstd.rearrange("p (g o) -> p g o", o=1).broadcast_to(
                    (128, 16, HS))
                vyn = vop({"sc": c_rs, "v": vgs},
                          lambda vv: vv.tensor_tensor(
                    sbB[:, :DM].rearrange("p (g i) -> p g i", g=16), y3,
                    mu3, ALU.subtract))
                vyn = vop({"v": vyn}, lambda vv: vv.tensor_tensor(
                    sbB[:, :DM].rearrange("p (g i) -> p g i", g=16),
                    sbB[:, :DM].rearrange("p (g i) -> p g i", g=16), rs3,
                    ALU.mult))
                vyn = vop({"v": vyn}, lambda vv: vv.tensor_tensor(
                    sbB[:, :DM], sbB[:, :DM], lnw_tm[:, :], ALU.mult))
                vyn = vop({"v": vyn}, lambda vv: vv.tensor_tensor(
                    sbB[:, :DM], sbB[:, :DM], lnb_tm[:, :], ALU.add))
                vrk = vop({"v": max(v_r, v_kf, vyn)},
                          lambda vv: vv.tensor_tensor(
                              sbA[:, :], rT[:, :], kf[:, :], ALU.mult))
                vrk = vop({"v": vrk}, lambda vv: vv.tensor_tensor(
                    v3(sbA), v3(sbA), cv3(CV_FAAAA), ALU.mult))
                ps_rk, g_rk = acq()
                g_rk["v"] = max(g_rk.get("v", 0), vrk)

                def frk(tt, ps_rk=ps_rk):
                    for m in range(NCH):
                        ins = tt.matmul(ps_rk[0:16, 0:W],
                                        hselt_bf[:, m * 16:(m + 1) * 16],
                                        sbA[:, ts(m)], start=(m == 0),
                                        stop=(m == NCH - 1))
                    return ins
                pe_rk = top(g_rk, frk)
                v_rk2 = vop({"pe": pe_rk}, lambda vv, ps_rk=ps_rk:
                            vv.tensor_copy(rksb[:, :], ps_rk[0:16, 0:W]))
                prot.release({"v": v_rk2})
                ps_rt, g_rt = acqT()
                g_rt["v"] = max(g_rt.get("v", 0), v_rk2)
                pe_rt = top(g_rt, lambda tt, ps_rt=ps_rt: tt.transpose(
                    ps_rt[:, 0:16], rksb[:, :], ident_bf[0:16, 0:16]))
                v_rt = vop({"pe": pe_rt}, lambda vv, ps_rt=ps_rt:
                           vv.tensor_copy(rktm[:, :], ps_rt[:, 0:16]))
                ptrot.release({"v": v_rt})
                rk3 = rktm.rearrange("p (g o) -> p g o", o=1).broadcast_to(
                    (128, 16, HS))
                v3tm = vtm.rearrange("p (g i) -> p g i", g=16)
                vrkv = vop({"v": max(v_rt, v_v)}, lambda vv:
                           vv.tensor_tensor(
                               sbA[:, :DM].rearrange("p (g i) -> p g i",
                                                     g=16), v3tm, rk3,
                               ALU.mult))
                vfin = vop({"v": vrkv}, lambda vv: vv.tensor_tensor(
                    sbB[:, :DM], sbB[:, :DM], sbA[:, :DM], ALU.add))
                ps_gt, g_gt = acqT()
                g_gt["v"] = max(g_gt.get("v", 0), v_g2)

                def fgt(tt, ps_gt=ps_gt):
                    for m in range(NCH):
                        ins = tt.transpose(ps_gt[:, m * W:(m + 1) * W],
                                           gTt[:, ts(m)], ident_bf[:])
                    return ins
                pe_gt = top(g_gt, fgt)
                v_gt = vop({"pe": pe_gt}, lambda vv, ps_gt=ps_gt:
                           vv.tensor_copy(gtm[:, :], ps_gt[:, :1024]))
                ptrot.release({"v": v_gt})
                vyg = vop({"v": max(vfin, v_gt)}, lambda vv:
                          vv.tensor_tensor(sbB[:, :DM], sbB[:, :DM],
                                           gtm[:, :], ALU.mult))
                ps_yf, g_yf = acqT()
                g_yf["v"] = max(g_yf.get("v", 0), vyg)

                def fyf(tt, ps_yf=ps_yf):
                    for m in range(NCH):
                        ins = tt.transpose(ps_yf[:, m * W:(m + 1) * W],
                                           sbB[:, ts(m)], ident_bf[:])
                    return ins
                pe_yf = top(g_yf, fyf)
                v_yf = vop({"pe": pe_yf}, lambda vv, ps_yf=ps_yf:
                           vv.tensor_copy(ygfm[:, :], ps_yf[:, :1024]))
                ptrot.release({"v": v_yf})

                wo_halves = [dB_wo0, None]
                for half in (0, 1):
                    if half == 1:
                        dB_cur = b.r("dB", 16)
                        load_after(pe_wo, wsB, "woT", 1)
                        wo_halves[1] = dB_cur
                    ps_wo, g_wo = acq()
                    g_wo["dB"] = max(g_wo.get("dB", 0),
                                     wo_halves[half] if half == 0
                                     else dB_cur)
                    g_wo["v"] = max(g_wo.get("v", 0), v_yf)

                    def fwo2(tt, ps_wo=ps_wo, half=half):
                        for mo in range(4):
                            for kc in range(NCH):
                                ins = tt.matmul(
                                    ps_wo[:, mo * W:(mo + 1) * W],
                                    wsB[:, kc * 512 + mo * 128:
                                        kc * 512 + (mo + 1) * 128],
                                    ygfm[:, ts(kc)], start=(kc == 0),
                                    stop=(kc == NCH - 1))
                        return ins
                    pe_wo = top(g_wo, fwo2)
                    v_ra = 0
                    for mo in range(4):
                        m = half * 4 + mo
                        v_ra = vop({"pe": pe_wo, "v": v_ra, "sc": c_ys},
                                   lambda vv, m=m, mo=mo, ps_wo=ps_wo:
                                   vv.tensor_tensor(
                                       R[m][:, tsl], R[m][:, tsl],
                                       ps_wo[:, mo * W:(mo + 1) * W],
                                       ALU.add))
                    prot.release({"v": v_ra})
                dB_cur = b.r("dB", 16)
                load_after(pe_wo, wsB, "wkT", 0)
                b.barrier()
            _da, _db, _ds = b.n("dA"), b.n("dB"), b.n("dS")

            def fdrain(g, _da=_da, _db=_db, _ds=_ds):
                g.wait_ge(b.h("dA"), _da)
                g.wait_ge(b.h("dB"), _db)
                g.wait_ge(b.h("dS"), _ds)
            b.g(fdrain)
            b.barrier()


    def emit_attn_pipe():
        W = 128
        NBK = T // W
        NPAIR = 8
        NSL = 3

        def gmax(*ds):
            out = {}
            for dd in ds:
                if not dd:
                    continue
                for k2, v2 in dd.items():
                    out[k2] = max(out.get(k2, 0), v2)
            return out

        def vop(waits, fn):
            n = b.r("v")

            def f(vv, waits=dict(waits), fn=fn):
                for s, c in waits.items():
                    if c:
                        vv.wait_ge(b.h(s), c)
                fn(vv).then_inc(b.h("v"), 1)
            b.v(f)
            return n

        def cop(waits, fn):
            n = b.r("sc")

            def f(cc, waits=dict(waits), fn=fn):
                for s, c in waits.items():
                    if c:
                        cc.wait_ge(b.h(s), c)
                fn(cc).then_inc(b.h("sc"), 1)
            b.c(f)
            return n

        def pop(waits, fn):
            n = b.r("pl")

            def f(pp, waits=dict(waits), fn=fn):
                for s, c in waits.items():
                    if c:
                        pp.wait_ge(b.h(s), c)
                fn(pp).then_inc(b.h("pl"), 1)
            b.g(f)
            return n

        def top(waits, fn):
            n = b.r("pe")

            def f(tt, waits=dict(waits), fn=fn):
                for s, c in waits.items():
                    if c:
                        tt.wait_ge(b.h(s), c)
                fn(tt).then_inc(b.h("pe"), 1)
            b.t(f)
            return n

        def snap():
            return {"pe": b.n("pe"), "v": b.n("v"), "sc": b.n("sc"),
                    "pl": b.n("pl")}

        from contextlib import ExitStack
        with ExitStack() as _est:
            def sb(name, shape, dt=BF16):
                return _est.enter_context(nc.sbuf_tensor(name, list(shape),
                                                         dt))

            wsA = sb("q_wsA", [128, 8 * 512])
            wsB = sb("q_wsB", [128, 8 * 512])
            maa1 = sb("q_maa1", [128, 8 * 128])
            maa2a = sb("q_maa2a", [64, DM])
            maa2b = sb("q_maa2b", [64, DM])
            dw1 = sb("q_dw1", [128, 8 * 64])
            dw2 = sb("q_dw2", [64, DM])
            bw1 = sb("q_bw1", [128, 8 * 64])
            bw2ka = sb("q_bw2ka", [64, DM])
            bw2mm = sb("q_bw2mm", [64, DM])
            gw1 = sb("q_gw1", [128, 8 * 128])
            gw2 = sb("q_gw2", [128, DM])

            xnb = sb("q_xnb", [128, DM])
            xmr = sb("q_xmr", [128, DM])
            xmw = sb("q_xmw", [128, DM])
            xmk = sb("q_xmk", [128, DM])
            xmv = sb("q_xmv", [128, DM])
            kraw = sb("q_kraw", [128, DM])
            kkn = sb("q_kkn", [128, DM])
            alpha = sb("q_alpha", [128, DM])
            wdec = sb("q_wdec", [128, DM])
            dinc = sb("q_dinc", [128, DM])
            dinv = sb("q_dinv", [128, DM])
            mag = sb("q_mag", [128, DM])
            sbAp = sb("q_sbA", [128, DM])
            sbBp = sb("q_sbB", [128, DM])
            smb = sb("q_smb", [128, DM])
            xxb = _est.enter_context(nc.sbuf_tensor("q_xx", [128, DM], F32))
            scrA = _est.enter_context(nc.sbuf_tensor("q_scrA", [128, DM],
                                                     F32))
            scrB = _est.enter_context(nc.sbuf_tensor("q_scrB", [128, DM],
                                                     F32))
            xx = xxb
            lwt = xxb
            Lct = scrB
            losb = smb[0:64, 0:256]
            gt1 = smb[:, 256:384]
            dt1 = smb[0:64, 384:512]
            kbt = smb[0:16, 512:640]
            abt = smb[32:48, 640:768]
            mabt = smb[0:16, 768:896]
            mkbt = smb[32:48, 896:1024]
            gTt = sbAp
            mkg = dinc

            rTs = [sb(f"q_rT{s}", [128, DM]) for s in (0, 1)]
            kfs = [sb(f"q_kf{s}", [128, DM]) for s in (0, 1)]
            gtms = [sb(f"q_gt{s}", [128, DM]) for s in (0, 1)]
            QbtI = sb("q_Qb", [128, DM])
            vtmI = sb("q_vt", [128, DM])
            BKhI = sb("q_bk", [128, 2 * DM])
            dlfI = _est.enter_context(nc.sbuf_tensor("q_dl", [128, 8], F32))
            Abt = sb("q_Ab", [128, DM])
            Btt = sb("q_Bt", [128, DM])
            Ktt = sb("q_Kt", [128, DM])

            ytm = sb("q_ytm", [128, DM])
            sbAs = sb("q_sbAs", [128, DM])
            sbBs = sb("q_sbBs", [128, DM])
            ygfm = sbAs
            STb = sb("q_STb", [128, H * HS // 2])
            scrS = _est.enter_context(nc.sbuf_tensor("q_scrS", [128, 176],
                                                     F32))
            slb = [sb(f"q_slb{s}", [128, 2816]) for s in range(NSL)]
            carry = _est.enter_context(nc.sbuf_tensor("q_carry", [128, 8],
                                                      F32))
            basec = _est.enter_context(nc.sbuf_tensor("q_base", [128, 8],
                                                      F32))
            bnd = sb("q_bnd", [128, 8])

            ppT = [_est.enter_context(nc.psum_tensor(f"q_pp{j}", [128, 1024],
                                                     F32)) for j in (0, 1)]
            spT = [_est.enter_context(nc.psum_tensor(f"q_sp{j}", [128, 1024],
                                                     F32)) for j in range(2)]
            pprot = PsumRot(2)
            sprot = PsumRot(2)

            def acqP():
                s, g = pprot.acquire()
                return ppT[s], dict(g or {})

            def acqS():
                s, g = sprot.acquire()
                return spT[s], dict(g or {})

            def acqT():
                s, g = pprot.acquire()
                return ppT[s][:, 0:512].bitcast(BF16), dict(g or {})

            ptrot = pprot

            def b2(ap_, w):
                return ap_.rearrange("p (b c) -> p b c", b=2)[:, :, 0:w]

            def v3(ap_):
                return ap_.rearrange("p (m w) -> p m w", m=8)

            def cv3(col):
                apx = chanvec[:, col::NVC]
                return apx.rearrange("p (m o) -> p m o", o=1).broadcast_to(
                    (128, 8, W))

            z128 = consts[:, SC_ZERO:SC_ZERO + 1].broadcast_to((128, 128))

            C_NT, C_T24, C_PL0, C_SQA, C_SQB, C_RA, C_RB = \
                0, 512, 1024, 1280, 1792, 2304, 2560
            C_W0, C_XC, C_DD = C_PL0, C_PL0 + 128, C_SQA

            vz = vop({}, lambda vv: vv.memset(STb[:], 0.0))
            vz = vop({"v": vz}, lambda vv: vv.memset(carry[:], 0.0))
            v_init = vop({"v": vz}, lambda vv: vv.memset(bnd[:], 0.0))

            dS_done = b.r("dS", 16 * 17)

            def fsm(s):
                def ld(dst, nm):
                    s.dma_start(dst, d[nm]).then_inc(b.h("dS"), 16)
                s.dma_start(maa1.rearrange("p (kc x) -> p kc x", kc=8),
                            d["maa_w1T"].rearrange("(kc p) x -> p kc x",
                                                   p=128)
                            ).then_inc(b.h("dS"), 16)
                s.dma_start(dw1.rearrange("p (kc x) -> p kc x", kc=8),
                            d["decay_w1T"].rearrange("(kc p) x -> p kc x",
                                                     p=128)
                            ).then_inc(b.h("dS"), 16)
                s.dma_start(gw1.rearrange("p (kc x) -> p kc x", kc=8),
                            d["gate_w1T"].rearrange("(kc p) x -> p kc x",
                                                    p=128)
                            ).then_inc(b.h("dS"), 16)
                for i2_, nm in enumerate(("kkk_w1T", "aaa_w1T", "ma_w1T",
                                          "mk_w1T")):
                    s.dma_start(
                        bw1.rearrange("p (kc x) -> p kc x", kc=8
                                      )[:, :, i2_ * 16:(i2_ + 1) * 16],
                        d[nm].rearrange("(kc p) x -> p kc x", p=128)
                    ).then_inc(b.h("dS"), 16)
                ld(gw2[:], "gate_w2")
                ld(dw2[:], "decay_w2")
                ld(bw2ka[0:16, :], "kkk_w2")
                ld(bw2ka[32:48, :], "aaa_w2")
                ld(bw2mm[0:16, :], "ma_w2")
                ld(bw2mm[32:48, :], "mk_w2")
                s.dma_start(maa2a[0:32, :], d["maa_w2"][0]
                            ).then_inc(b.h("dS"), 16)
                s.dma_start(maa2a[32:64, :], d["maa_w2"][1]
                            ).then_inc(b.h("dS"), 16)
                s.dma_start(maa2b[0:32, :], d["maa_w2"][2]
                            ).then_inc(b.h("dS"), 16)
                s.dma_start(maa2b[32:64, :], d["maa_w2"][3]
                            ).then_inc(b.h("dS"), 16)
            b.g(fsm)

            stA = {"bufs": [wsA], "sem": "dA", "q": [], "cons": [],
                   "nl": 0, "eng": b.y}
            stB = {"bufs": [wsB], "sem": "dB", "q": [], "cons": [],
                   "nl": 0, "eng": b.g}

            def load_ws(st, nm, half):
                j = st["nl"]
                st["nl"] += 1
                nb = len(st["bufs"])
                buf = st["bufs"][j % nb]
                guard = st["cons"][j - nb] if j >= nb else 0
                cnt = b.r(st["sem"], 16)

                def f(s, nm=nm, half=half, buf=buf, guard=guard,
                      sem=st["sem"]):
                    if guard:
                        s.wait_ge(b.h("pe"), guard)
                    s.dma_start(
                        buf.rearrange("p (kc x) -> p kc x", kc=8),
                        d[nm][:, half * 512:(half + 1) * 512].rearrange(
                            "(kc p) x -> p kc x", p=128)
                    ).then_inc(b.h(sem), 16)
                st["eng"](f)
                st["q"].append((buf, cnt))

            def take_ws(st):
                return st["q"].pop(0)

            def done_ws(st, pe):
                st["cons"].append(pe)

            load_ws(stA, "wrT", 0)
            load_ws(stB, "wkT", 0)

            CC = {}
            Pend = {}
            Send = {}
            Sr3pe = {}
            Sr3x = {}
            Sr4end = {}
            Swo1pe = {}
            stv = {p: {"v": v_init} for p in range(NPAIR)}
            slot_busy = {}
            gy_all_blk = {}
            last_state_blk = {}

            def P_c1(i):
                C = {}
                CC[i] = C
                t0 = i * W
                tsl = slice(t0, t0 + W)
                C["tsl"] = tsl
                par = i % 2
                C["par"] = par
                pprev = Pend.get(i - 1, {})
                C["pprev"] = pprev
                if i > 0:
                    load_ws(stB, "wkT", 0)

                ps_st, g_st = acqP()
                pe_prev = 0
                for m in range(NCH):
                    w0_ = gmax({"dS": dS_done}, pprev if m == 0 else None,
                               g_st if m == 0 else None)
                    cn = cop(w0_, lambda cc, m=m: cc.activation(
                        sbAp[:, ts(m)], R[m][:, tsl], AF.Square))
                    vn = vop(w0_, lambda vv, m=m: vv.tensor_copy(
                        sbBp[:, ts(m)], R[m][:, tsl]))

                    def fst(tt, m=m, ps_st=ps_st):
                        tt.matmul(ps_st[:1, 0:W], onescol_bf[:],
                                  sbBp[:, ts(m)], start=(m == 0),
                                  stop=(m == NCH - 1))
                        return tt.matmul(ps_st[:1, 512:512 + W],
                                         onescol_bf[:], sbAp[:, ts(m)],
                                         start=(m == 0), stop=(m == NCH - 1))
                    pe_prev = top({"sc": cn, "v": vn}, fst)
                pe_stats = pe_prev
                C["pe_stats"] = pe_stats
                mu2 = scrA[0:1, 0:W]
                var2 = scrA[0:1, W:2 * W]
                s2 = scrA[0:1, 2 * W:3 * W]
                t2r = scrA[0:1, 3 * W:4 * W]
                vn = vop({"pe": pe_stats}, lambda vv: vv.tensor_scalar_mul(
                    mu2, ps_st[:1, 0:W], 1.0 / DM))
                vn = vop({"v": vn}, lambda vv: vv.tensor_mul(var2, mu2, mu2))
                vn = vop({"v": vn}, lambda vv: vv.scalar_tensor_tensor(
                    var2, ps_st[:1, 512:512 + W], 1.0 / DM, var2, ALU.mult,
                    ALU.subtract))
                cn = cop({"v": vn}, lambda cc: cc.activation(
                    s2, var2, AF.Sqrt,
                    bias=consts[:1, SC_LNEPS:SC_LNEPS + 1]))
                vn = vop({"sc": cn}, lambda vv: vv.reciprocal(s2, s2))
                vrows = vop({"v": vn}, lambda vv: vv.scalar_tensor_tensor(
                    t2r, mu2, -1.0, s2, ALU.mult, ALU.mult))
                pprot.release({"v": vrows})

                ps_bc, g_bc = acqP()
                g_bc["v"] = max(g_bc.get("v", 0), vrows)

                def fbc(tt, ps_bc=ps_bc):
                    tt.matmul(ps_bc[:, 0:W], ones1_f32[:], s2[:1, :],
                              start=True, stop=True)
                    return tt.matmul(ps_bc[:, 512:512 + W], ones1_f32[:],
                                     t2r[:1, :], start=True, stop=True)
                pe_bc = top(g_bc, fbc)

                vap = 0
                for m in range(NCH):
                    w1_ = {"pe": pe_bc, "v": vap}
                    vap = vop(w1_, lambda vv, m=m, ps_bc=ps_bc:
                              vv.tensor_tensor(scrB[:, ts(m)], R[m][:, tsl],
                                               ps_bc[:, 0:W], ALU.mult))
                    vap = vop({"v": vap}, lambda vv, m=m, ps_bc=ps_bc:
                              vv.tensor_tensor(xnb[:, ts(m)], scrB[:, ts(m)],
                                               ps_bc[:, 512:512 + W],
                                               ALU.add))
                v_ln2 = vap
                pprot.release({"v": v_ln2})
                vxx = vop({"v": v_ln2}, lambda vv: vv.tensor_tensor(
                    v3(xx)[:, :, 1:W], v3(xnb)[:, :, 0:W - 1],
                    v3(xnb)[:, :, 1:W], ALU.subtract))
                vxx = vop({"v": vxx}, lambda vv: vv.tensor_tensor(
                    v3(xx)[:, :, 0:1],
                    bnd.rearrange("p (m o) -> p m o", o=1),
                    v3(xnb)[:, :, 0:1], ALU.subtract))
                C["v_bn"] = vop({"v": vxx}, lambda vv: vv.tensor_copy(
                    bnd[:], xnb[:, W - 1::W]))
                C["vxx"] = vxx
                C["v_ln2"] = v_ln2

            def P_c2(i):
                C = CC[i]
                par = C["par"]
                vxx = C["vxx"]
                vmx = vop({"v": vxx}, lambda vv: vv.tensor_tensor(
                    v3(scrB), v3(xx), cv3(CV_TMX), ALU.mult))
                pmx = pop(gmax({"v": vmx}, C["pprev"]),
                          lambda pp: pp.tensor_tensor(
                              xmv[:, :], scrB[:, :], xnb[:, :], ALU.add))
                ps_lo, g_lo = acqP()
                g_lo["pl"] = max(g_lo.get("pl", 0), pmx)
                g_lo["dS"] = dS_done

                def flo(tt, ps_lo=ps_lo):
                    for half in (0, 1):
                        for kc in range(NCH):
                            ins = tt.matmul(
                                ps_lo[0:64, half * 512:half * 512 + W],
                                maa1[:, kc * 128 + half * 64:
                                     kc * 128 + half * 64 + 64],
                                xmv[:, ts(kc)], start=(kc == 0),
                                stop=(kc == NCH - 1))
                    return ins
                pe_lo = top(g_lo, flo)
                cn1 = cop({"pe": pe_lo}, lambda cc, ps_lo=ps_lo:
                          cc.activation(losb[:, 0:128], ps_lo[0:64, 0:W],
                                        AF.Tanh))
                cn2 = cop({"sc": cn1}, lambda cc, ps_lo=ps_lo:
                          cc.activation(losb[:, 128:256],
                                        ps_lo[0:64, 512:512 + W], AF.Tanh))
                pprot.release({"sc": cn2})

                xm_outs = [xmr, xmw, xmk, xmv]
                xm_ready = {}
                v_lo2_prev = vmx
                prev_pool = pmx
                for gi in range(4):
                    ps_g, g_g = acqP()
                    g_g["sc"] = max(g_g.get("sc", 0), cn2)
                    if gi == 3:
                        g_g["pe"] = max(g_g.get("pe", 0), pe_lo)
                    m2ap = maa2a if gi < 2 else maa2b
                    pbase = 32 * (gi % 2)
                    lslice = losb[pbase:pbase + 32,
                                  (gi // 2) * 128:(gi // 2) * 128 + W]

                    def flo2(tt, m2ap=m2ap, pbase=pbase, lslice=lslice,
                             ps_g=ps_g):
                        for m in range(NCH):
                            ins = tt.matmul(ps_g[:, m * W:(m + 1) * W],
                                            m2ap[pbase:pbase + 32, ts(m)],
                                            lslice, start=True, stop=True)
                        return ins
                    pe_g = top(g_g, flo2)
                    vs = vop({"pe": pe_g, "v": v_lo2_prev, "pl": prev_pool},
                             lambda vv, ps_g=ps_g, gi=gi: vv.tensor_tensor(
                                 v3(scrA), v3(ps_g), cv3(CV_TMAA0 + gi),
                                 ALU.add))
                    vs = vop({"v": vs}, lambda vv: vv.tensor_tensor(
                        scrA[:, :], scrA[:, :], xx[:, :], ALU.mult))
                    pw = {"v": vs}
                    if gi == 3:
                        pw["pl"] = pmx
                        pw["pe"] = pe_g
                    pn = pop(pw, lambda pp, gi=gi:
                             pp.tensor_tensor(xm_outs[gi][:, :], scrA[:, :],
                                              xnb[:, :], ALU.add))
                    xm_ready[gi] = pn
                    prev_pool = pn
                    pprot.release({"v": vs})
                    v_lo2_prev = vs
                C["xm_ready"] = xm_ready
                C["v_lo2"] = v_lo2_prev
                ps_d1, g_d1 = acqP()
                g_d1["pl"] = max(g_d1.get("pl", 0), xm_ready[1])
                g_d1["dS"] = dS_done

                def fd1(tt, ps_d1=ps_d1):
                    for kc in range(NCH):
                        ins = tt.matmul(ps_d1[0:64, 0:W],
                                        dw1[:, kc * 64:(kc + 1) * 64],
                                        xmw[:, ts(kc)], start=(kc == 0),
                                        stop=(kc == NCH - 1))
                    return ins
                pe_d1 = top(g_d1, fd1)
                c_d1 = cop({"pe": pe_d1}, lambda cc, ps_d1=ps_d1:
                           cc.activation(dt1[:, :], ps_d1[0:64, 0:W],
                                         AF.Tanh))
                pprot.release({"sc": c_d1})
                ps_d2, g_d2 = acqP()
                g_d2["sc"] = max(g_d2.get("sc", 0), c_d1)

                def fd2(tt, ps_d2=ps_d2):
                    for m in range(NCH):
                        ins = tt.matmul(ps_d2[:, m * W:(m + 1) * W],
                                        dw2[:, ts(m)], dt1[:, :],
                                        start=True, stop=True)
                    return ins
                pe_d2 = top(g_d2, fd2)
                vn_u = vop({"pe": pe_d2, "pl": xm_ready[3]},
                           lambda vv, ps_d2=ps_d2: vv.tensor_tensor(
                               v3(scrA), v3(ps_d2), cv3(CV_TD), ALU.add))
                pprot.release({"v": vn_u})
                v_m1 = vop({"v": vn_u}, lambda vv:
                           vv.tensor_scalar(scrB[:, :], scrA[:, :], -1.0,
                                            0.0, ALU.mult, ALU.max))
                v_au = vop({"v": v_m1}, lambda vv:
                           vv.scalar_tensor_tensor(scrA[:, :], scrB[:, :],
                                                   2.0, scrA[:, :],
                                                   ALU.mult, ALU.add))
                c_ex = cop({"v": v_au}, lambda cc: cc.activation(
                    scrA[:, :], scrA[:, :], AF.Exp, scale=-1.0))
                c_ln = cop({"sc": c_ex}, lambda cc: cc.activation(
                    scrA[:, :], scrA[:, :], AF.Ln, bias=1.0))
                v_sp = vop({"sc": c_ln}, lambda vv:
                           vv.tensor_tensor(scrA[:, :], scrA[:, :],
                                            scrB[:, :], ALU.add))
                v_wd = vop({"v": v_sp}, lambda vv: vv.tensor_scalar(
                    wdec[:, :], scrA[:, :], -1.0, -0.5, ALU.mult, ALU.add))
                c_lw = cop({"v": v_wd}, lambda cc: cc.activation(
                    scrB[:, :], wdec[:, :], AF.Exp))
                v_lw = vop({"sc": c_lw, "v": C["v_lo2"],
                            "pl": xm_ready[3]},
                           lambda vv: vv.tensor_scalar_mul(
                               lwt[:, :], scrB[:, :], -1.0))
                C["v_lw"] = v_lw
                C["c_lw"] = c_lw

                ps_g1, g_g1 = acqP()
                g_g1["dS"] = dS_done
                g_g1["pl"] = max(g_g1.get("pl", 0), xm_ready[0])

                def fg1(tt, ps_g1=ps_g1):
                    for kc in range(NCH):
                        ins = tt.matmul(ps_g1[:, 0:W],
                                        gw1[:, kc * 128:(kc + 1) * 128],
                                        xmr[:, ts(kc)], start=(kc == 0),
                                        stop=(kc == NCH - 1))
                    return ins
                pe_g1 = top(g_g1, fg1)
                c_g1 = cop({"pe": pe_g1}, lambda cc, ps_g1=ps_g1:
                           cc.activation(gt1[:, :], ps_g1[:, 0:W], AF.Tanh))
                pprot.release({"sc": c_g1})
                ps_g2, g_g2 = acqP()
                g_g2["sc"] = max(g_g2.get("sc", 0), c_g1)

                def fg2(tt, ps_g2=ps_g2):
                    for m in range(NCH):
                        ins = tt.matmul(ps_g2[:, m * W:(m + 1) * W],
                                        gw2[:, ts(m)], gt1[:, :],
                                        start=True, stop=True)
                    return ins
                pe_g2 = top(g_g2, fg2)
                v_g2 = vop({"pe": pe_g2}, lambda vv, ps_g2=ps_g2:
                           vv.tensor_copy(gTt[:, :], ps_g2[:, :]))
                pprot.release({"v": v_g2})
                ps_gt, g_gt = acqT()
                g_gt["v"] = max(g_gt.get("v", 0), v_g2)

                def fgt(tt, ps_gt=ps_gt):
                    for m in range(NCH):
                        ins = tt.transpose(ps_gt[:, m * W:(m + 1) * W],
                                           gTt[:, ts(m)], ident_bf[:])
                    return ins
                pe_gt = top(g_gt, fgt)
                C["pe_gt"] = pe_gt
                v_gt = vop(gmax({"pe": pe_gt}, Send.get(i - 2)),
                           lambda vv, ps_gt=ps_gt, par=par:
                           vv.tensor_copy(gtms[par][:, :], ps_gt[:, :1024]))
                ptrot.release({"v": v_gt})
                C["v_gt"] = v_gt

            def P_c3(i):
                C = CC[i]
                par = C["par"]
                xm_ready = C["xm_ready"]
                send2 = Send.get(i - 2)

                def proj_half(st, src_xm, dst, half, xm_n):
                    buf, cnt = take_ws(st)
                    ps_p, g_p = acqP()
                    g_p[st["sem"]] = max(g_p.get(st["sem"], 0), cnt)
                    g_p["pl"] = max(g_p.get("pl", 0), xm_n)

                    def fp(tt, buf=buf, src_xm=src_xm, ps_p=ps_p):
                        for mo in range(4):
                            for kc in range(NCH):
                                ins = tt.matmul(
                                    ps_p[:, mo * W:(mo + 1) * W],
                                    buf[:, kc * 512 + mo * 128:
                                        kc * 512 + (mo + 1) * 128],
                                    src_xm[:, ts(kc)], start=(kc == 0),
                                    stop=(kc == NCH - 1))
                        return ins
                    pe_p = top(g_p, fp)
                    done_ws(st, pe_p)
                    vn = vop(gmax({"pe": pe_p}, send2),
                             lambda vv, dst=dst, ps_p=ps_p, half=half:
                             vv.tensor_copy(
                                 dst[:, half * 512:(half + 1) * 512],
                                 ps_p[:, 0:512]))
                    pprot.release({"v": vn})
                    return pe_p, vn

                pe_r0, v_r0 = proj_half(stA, xmr, rTs[par], 0, xm_ready[0])
                load_ws(stA, "wrT", 1)
                pe_r1, v_r1 = proj_half(stA, xmr, rTs[par], 1, xm_ready[0])
                load_ws(stA, "wvT", 0)
                C["v_r"] = max(v_r0, v_r1)
                pe_k0, v_k0 = proj_half(stB, xmk, kraw, 0, xm_ready[2])
                load_ws(stB, "wkT", 1)
                pe_k1, v_k1 = proj_half(stB, xmk, kraw, 1, xm_ready[2])
                C["pe_k1"] = pe_k1
                C["v_k"] = max(v_k0, v_k1)
                v_v = 0
                for nsI in (0, 1):
                    vbuf, vcnt = take_ws(stA)
                    ps_v, g_v = acqP()
                    g_v["dA"] = max(g_v.get("dA", 0), vcnt)
                    g_v["pl"] = max(g_v.get("pl", 0), xm_ready[3])

                    def fv_(tt, ps_v=ps_v, vbuf=vbuf):
                        for kc in range(NCH):
                            ins = tt.matmul(
                                ps_v[:, 0:512], xmv[:, ts(kc)],
                                vbuf[:, kc * 512:(kc + 1) * 512],
                                start=(kc == 0), stop=(kc == NCH - 1))
                        return ins
                    pe_v = top(g_v, fv_)
                    done_ws(stA, pe_v)
                    v_v = vop(gmax({"pe": pe_v, "v": v_v},
                                   {"pe": Sr3pe.get(i - 1, 0)}),
                              lambda vv, ps_v=ps_v, nsI=nsI, par=par:
                              vv.tensor_copy(vtmI[:, nsI * 512:
                                                  (nsI + 1) * 512],
                                             ps_v[:, 0:512]))
                    pprot.release({"v": v_v})
                    if nsI == 0:
                        load_ws(stA, "wvT", 1)
                    else:
                        load_ws(stA, "wrT", 0)
                C["v_v"] = v_v


            def P_c4(i):
                C = CC[i]
                par = C["par"]
                xm_ready = C["xm_ready"]
                send2 = Send.get(i - 2)
                sr3 = {"pe": Sr3pe.get(i - 1, 0)}
                sr3x = Sr3x.get(i - 1)
                v_lw = C["v_lw"]

                ps_kb, g_kb = acqP()
                g_kb["pl"] = max(g_kb.get("pl", 0), xm_ready[3])
                g_kb["dS"] = dS_done

                def fkb(tt, ps_kb=ps_kb):
                    for kc in range(NCH):
                        tt.matmul(ps_kb[0:16, 0:W],
                                  bw1[:, kc * 64:kc * 64 + 16],
                                  xmk[:, ts(kc)], start=(kc == 0),
                                  stop=(kc == NCH - 1))
                    for kc in range(NCH):
                        tt.matmul(ps_kb[32:48, 0:W],
                                  bw1[:, kc * 64 + 16:kc * 64 + 32],
                                  xmw[:, ts(kc)], start=(kc == 0),
                                  stop=(kc == NCH - 1))
                    for kc in range(NCH):
                        tt.matmul(ps_kb[0:16, 512:512 + W],
                                  bw1[:, kc * 64 + 32:kc * 64 + 48],
                                  xmw[:, ts(kc)], start=(kc == 0),
                                  stop=(kc == NCH - 1))
                    for kc in range(NCH):
                        ins = tt.matmul(ps_kb[32:48, 512:512 + W],
                                        bw1[:, kc * 64 + 48:kc * 64 + 64],
                                        xmk[:, ts(kc)], start=(kc == 0),
                                        stop=(kc == NCH - 1))
                    return ins
                pe_kb = top(g_kb, fkb)
                c_kb = cop({"pe": pe_kb}, lambda cc, ps_kb=ps_kb:
                           cc.activation(kbt[:, :], ps_kb[0:16, 0:W],
                                         AF.Tanh))
                v_ab2 = vop({"pe": pe_kb}, lambda vv, ps_kb=ps_kb:
                            vv.tensor_copy(abt[:, :], ps_kb[32:48, 0:W]))
                v_ma = vop({"v": v_ab2}, lambda vv, ps_kb=ps_kb:
                           vv.tensor_copy(mabt[:, :],
                                          ps_kb[0:16, 512:512 + W]))
                v_mk = vop({"v": v_ma}, lambda vv, ps_kb=ps_kb:
                           vv.tensor_copy(mkbt[:, :],
                                          ps_kb[32:48, 512:512 + W]))
                pprot.release({"v": v_mk, "sc": c_kb})

                ps_kk, g_kk = acqP()
                g_kk["sc"] = max(g_kk.get("sc", 0), c_kb)
                g_kk["v"] = max(g_kk.get("v", 0), v_mk)

                def fkk(tt, ps_kk=ps_kk):
                    for m in range(NCH):
                        ins = tt.matmul(ps_kk[:, m * W:(m + 1) * W],
                                        bw2ka[0:16, ts(m)], kbt[:, :],
                                        start=True, stop=True)
                    return ins
                pe_kk = top(g_kk, fkk)
                v_kk = vop({"pe": pe_kk, "v": C["v_k"]},
                           lambda vv, ps_kk=ps_kk:
                           vv.tensor_tensor(kkn[:, :], ps_kk[:, :],
                                            kraw[:, :], ALU.add))
                pprot.release({"v": v_kk})
                c_sq = cop({"v": v_kk, "pe": C["pe_stats"]},
                           lambda cc: cc.activation(
                               sbBp[:, :], kkn[:, :], AF.Square))
                ps_nr, g_nr = acqP()
                g_nr["sc"] = max(g_nr.get("sc", 0), c_sq)

                def fnr(tt, ps_nr=ps_nr):
                    for m in range(NCH):
                        ins = tt.matmul(ps_nr[0:16, 0:W],
                                        hselt_bf[:, m * 16:(m + 1) * 16],
                                        sbBp[:, ts(m)], start=(m == 0),
                                        stop=(m == NCH - 1))
                    return ins
                pe_nr = top(g_nr, fnr)
                rnsb = scrB[0:16, 0:128]
                c_nr = cop({"pe": pe_nr, "v": v_lw},
                           lambda cc, ps_nr=ps_nr:
                           cc.activation(rnsb[:, :], ps_nr[0:16, 0:W],
                                         AF.Ln,
                                         bias=consts[0:16,
                                                     SC_EPS30:SC_EPS30 + 1]))
                c_nr = cop({"sc": c_nr}, lambda cc: cc.activation(
                    rnsb[:, :], rnsb[:, :], AF.Exp, scale=-0.5))
                pprot.release({"sc": c_nr})
                v_nr = vop({"sc": c_nr}, lambda vv: vv.tensor_scalar_min(
                    rnsb[:, :], rnsb[:, :], 1e12))
                ps_nb, g_nb = acqP()
                g_nb["v"] = max(g_nb.get("v", 0), v_nr)

                def fnb(tt, ps_nb=ps_nb):
                    for m in range(NCH):
                        ins = tt.matmul(ps_nb[:, m * W:(m + 1) * W],
                                        hsel[:, m * 128:(m + 1) * 128],
                                        rnsb[:, :], start=True, stop=True)
                    return ins
                pe_nb = top(g_nb, fnb)
                v_kkn = vop({"pe": pe_nb}, lambda vv, ps_nb=ps_nb:
                            vv.tensor_tensor(kkn[:, :], kkn[:, :],
                                             ps_nb[:, :], ALU.mult))
                pprot.release({"v": v_kkn})

                def expand_gate(lhs_ap, rhs_sb, cvcol, dst, wait_sc,
                                wait_v=0):
                    ps_e, g_e = acqP()
                    g_e["sc"] = max(g_e.get("sc", 0), wait_sc)
                    if wait_v:
                        g_e["v"] = max(g_e.get("v", 0), wait_v)

                    def fe(tt, lhs_ap=lhs_ap, rhs_sb=rhs_sb, ps_e=ps_e):
                        for m in range(NCH):
                            ins = tt.matmul(ps_e[:, m * W:(m + 1) * W],
                                            lhs_ap[:, ts(m)], rhs_sb[:, :],
                                            start=True, stop=True)
                        return ins
                    pe_e = top(g_e, fe)
                    vn = vop({"pe": pe_e, "sc": C["c_lw"]},
                             lambda vv, ps_e=ps_e, cvcol=cvcol:
                             vv.tensor_tensor(v3(scrB), v3(ps_e),
                                              cv3(cvcol), ALU.add))
                    cs = cop({"v": vn}, lambda cc, dst=dst: cc.activation(
                        dst[:, :], scrB[:, :], AF.Sigmoid))
                    pprot.release({"v": vn})
                    return cs

                c_al = expand_gate(bw2ka[32:48, :], abt[:, :],
                                   CV_AAAAA, alpha, c_kb, v_mk)
                c_mag = expand_gate(bw2mm[0:16, :], mabt[:, :], CV_MISCA,
                                    mag, c_al, v_mk)
                c_mkg = expand_gate(bw2mm[32:48, :], mkbt[:, :], CV_MISCK,
                                    mkg, c_mag, v_mk)

                v_f1 = vop({"sc": c_mag, "pe": C["pe_gt"]},
                           lambda vv: vv.tensor_scalar(
                               sbAp[:, :], mag[:, :], -1.0, 1.0, ALU.mult,
                               ALU.add))
                v_f2 = vop({"v": v_f1, "sc": c_al, "pe": pe_nr},
                           lambda vv: vv.tensor_tensor(
                               sbBp[:, :], alpha[:, :], sbAp[:, :],
                               ALU.mult))
                v_f3 = vop({"v": v_f2}, lambda vv: vv.tensor_tensor(
                    sbAp[:, :], mag[:, :], sbBp[:, :], ALU.add))
                v_ks = vop({"v": v_f3}, lambda vv: vv.tensor_tensor(
                    sbBp[:, :], kraw[:, :], sbAp[:, :], ALU.mult))
                v_e1 = vop({"v": v_ks, "sc": c_mkg},
                           lambda vv: vv.tensor_tensor(
                               sbAp[:, :], wdec[:, :], mkg[:, :], ALU.mult))
                c_e2 = cop({"v": v_e1}, lambda cc: cc.activation(
                    sbAp[:, :], sbAp[:, :], AF.Exp))
                v_kf = vop(gmax({"sc": c_e2}, send2),
                           lambda vv, par=par: vv.tensor_tensor(
                               kfs[par][:, :], sbBp[:, :], sbAp[:, :],
                               ALU.mult))
                C["v_kf"] = v_kf

                v_base = vop({"v": v_lw, "sc": c_mkg},
                             lambda vv: vv.tensor_copy(basec[:, :],
                                                       carry[:, :]))
                scans = v_base
                for m in range(NCH):
                    scans = vop({"v": scans}, lambda vv, m=m:
                                vv.tensor_tensor_scan(
                                    Lct[:, ts(m)], lwt[:, ts(m)],
                                    z128, carry[:, m:m + 1],
                                    ALU.add, ALU.add))
                v_cu = vop({"v": scans}, lambda vv: vv.tensor_copy(
                    carry[:, :], Lct[:, W - 1::W]))
                v_rel = vop({"v": v_cu}, lambda vv: vv.tensor_tensor(
                    v3(Lct), v3(Lct),
                    basec.rearrange("p (m o) -> p m o", o=1).broadcast_to(
                        (128, 8, W)), ALU.subtract))
                c_di = cop({"v": v_rel, "sc": c_e2},
                           lambda cc: cc.activation(
                               dinc[:, :], Lct[:, :], AF.Exp))
                c_dl = cop(gmax({"sc": c_di}, sr3x),
                           lambda cc: cc.activation(
                               dlfI[:, :], Lct[:, W - 1::W], AF.Exp))
                c_dv = cop({"sc": c_dl}, lambda cc: cc.activation(
                    dinv[:, :], Lct[:, :], AF.Exp, scale=-1.0))
                v_de = vop({"v": v_rel}, lambda vv: vv.tensor_tensor(
                    scrA[:, :], Lct[:, :], lwt[:, :], ALU.subtract))
                c_de = cop({"v": max(v_de, v_kf), "sc": c_dv},
                           lambda cc: cc.activation(sbAp[:, :], scrA[:, :],
                                                    AF.Exp))
                C["c_dl"] = c_dl

                v_Qb = pop(gmax({"sc": c_di, "v": C["v_r"]}, sr3),
                           lambda pp, par=par: pp.tensor_tensor(
                               QbtI[:, :], rTs[par][:, :], dinc[:, :],
                               ALU.mult))
                v_Ab = vop(gmax({"sc": c_de, "v": v_kkn}, sr3),
                           lambda vv: vv.tensor_tensor(
                               Abt[:, :], kkn[:, :], sbAp[:, :], ALU.mult))
                v_b1 = vop({"sc": c_dv, "v": v_Ab}, lambda vv:
                           vv.scalar_tensor_tensor(sbBp[:, :], alpha[:, :],
                                                   -1.0, dinv[:, :],
                                                   ALU.mult, ALU.mult))
                v_Bt = vop(gmax({"v": v_b1}, sr3), lambda vv:
                           vv.tensor_tensor(Btt[:, :], kkn[:, :],
                                            sbBp[:, :], ALU.mult))
                p_Kt = pop(gmax({"v": v_kf, "sc": c_dv, "pl": v_Qb},
                                sr3),
                           lambda pp, par=par: pp.tensor_tensor(
                               Ktt[:, :], kfs[par][:, :], dinv[:, :],
                               ALU.mult))
                dl3 = v3(dinc)[:, :, W - 1:W].broadcast_to((128, 8, W))
                v_Bh = vop({"v": v_Bt}, lambda vv: vv.tensor_tensor(
                    v3(sbAp), v3(Btt), dl3, ALU.mult))
                p_Kh = pop({"pl": p_Kt, "v": v_Bt}, lambda pp:
                           pp.tensor_tensor(
                               v3(sbBp), v3(Ktt), dl3, ALU.mult))

                pe_tl = 0
                for src, half, gwait in ((sbAp, 0, {"v": v_Bh}),
                                         (sbBp, 1, {"pl": p_Kh})):
                    ps_t, g_t = acqT()
                    for k2_, v2_ in gwait.items():
                        g_t[k2_] = max(g_t.get(k2_, 0), v2_)
                    if pe_tl:
                        g_t["pe"] = max(g_t.get("pe", 0), pe_tl)

                    def ftr(tt, src=src, ps_t=ps_t):
                        for m in range(NCH):
                            ins = tt.transpose(ps_t[:, m * W:(m + 1) * W],
                                               src[:, ts(m)], ident_bf[:])
                        return ins
                    pe_tl = top(g_t, ftr)
                    vt = vop(gmax({"pe": pe_tl}, sr3),
                             lambda vv, half=half, ps_t=ps_t:
                             vv.tensor_copy(
                                 BKhI[:, half * DM:(half + 1) * DM],
                                 ps_t[:, :1024]))
                    ptrot.release({"v": vt})
                C["pe_BK"] = pe_tl

            RNDS = [[0, 1, 2], [3, 4, 5], [6, 7]]

            def sl_ap(sl, base, w):
                return slb[sl][:, base:base + w]

            def S_solve_round(i, rnd):
                par = i % 2
                pend_i = Pend[i]
                pairs = RNDS[rnd]
                pctx = {}
                for p in pairs:
                    gp = i * NPAIR + p
                    sl = gp % NSL
                    slg = dict(slot_busy.get(sl, {}))
                    csl = slice(p * W, (p + 1) * W)
                    X = dict(p=p, sl=sl, slg=slg, csl=csl)
                    X["Af"] = [Abt[pb:pb + 64, csl] for pb in (0, 64)]
                    X["Bf"] = [Btt[pb:pb + 64, csl] for pb in (0, 64)]
                    X["Kf"] = [Ktt[pb:pb + 64, csl] for pb in (0, 64)]
                    X["Qf"] = [QbtI[pb:pb + 64, csl] for pb in (0, 64)]
                    X["STp"] = [STb[pb:pb + 64, p * 64:(p + 1) * 64]
                                for pb in (0, 64)]
                    X["Vt"] = [vtmI[:, (2 * p + hx) * 64:
                                    (2 * p + hx) * 64 + 64]
                               for hx in (0, 1)]
                    X["Bh"] = [BKhI[:, (2 * p + hx) * 64:
                                    (2 * p + hx) * 64 + 64]
                               for hx in (0, 1)]
                    X["Kh"] = [BKhI[:, DM + (2 * p + hx) * 64:
                                    DM + (2 * p + hx) * 64 + 64]
                               for hx in (0, 1)]
                    pctx[p] = X

                for p in pairs:
                    X = pctx[p]
                    sl = X["sl"]
                    ps1, g1 = acqS()
                    pe1 = top(gmax(g1, X["slg"], pend_i),
                              lambda tt, X=X, ps1=ps1: [
                                  tt.matmul(ps1[:, hx * 512:hx * 512 + 128],
                                            X["Af"][hx], X["Bf"][hx],
                                            start=True, stop=True)
                                  for hx in (0, 1)][-1])
                    vN = vop(gmax({"pe": pe1}, X["slg"]),
                             lambda vv, sl=sl, ps1=ps1: vv.tensor_tensor(
                                 b2(sl_ap(sl, C_PL0, 256), 128),
                                 b2(ps1[:, 0:1024], 128),
                                 b2(mask_sl2[:, :], 128), ALU.mult))
                    sprot.release({"v": vN})
                    X["gN"] = {"v": vN}
                    ps2, g2 = acqS()

                    def f13(tt, X=X, ps2=ps2):
                        for hx in (0, 1):
                            tt.matmul(ps2[:, hx * 512:hx * 512 + 128],
                                      X["Bf"][hx], X["Af"][hx],
                                      start=True, stop=True)
                            ins = tt.matmul(ps2[:, hx * 512 + 128:
                                                hx * 512 + 256],
                                            X["Bf"][hx], X["Qf"][hx],
                                            start=True, stop=True)
                        return ins
                    pe2 = top(gmax(g2, X["slg"], pend_i), f13)
                    v13 = vop(gmax({"pe": pe2}, X["slg"]),
                              lambda vv, sl=sl, ps2=ps2: vv.tensor_tensor(
                                  b2(sl_ap(sl, C_NT, 512), 256),
                                  b2(ps2[:, 0:1024], 256),
                                  b2(mask_t2[:, :], 256), ALU.mult))
                    sprot.release({"v": v13})
                    X["g13"] = {"v": v13}
                    ps3, g3 = acqS()

                    def f24(tt, X=X, ps3=ps3):
                        for hx in (0, 1):
                            tt.matmul(ps3[:, hx * 512:hx * 512 + 128],
                                      X["Kf"][hx], X["Af"][hx],
                                      start=True, stop=True)
                            ins = tt.matmul(ps3[:, hx * 512 + 128:
                                                hx * 512 + 256],
                                            X["Kf"][hx], X["Qf"][hx],
                                            start=True, stop=True)
                        return ins
                    pe3 = top(gmax(g3, X["slg"], pend_i), f24)
                    v24 = vop(gmax({"pe": pe3}, X["slg"]),
                              lambda vv, sl=sl, ps3=ps3: vv.tensor_tensor(
                                  b2(sl_ap(sl, C_T24, 512), 256),
                                  b2(ps3[:, 0:1024], 256),
                                  b2(mask_t2[:, :], 256), ALU.mult))
                    sprot.release({"v": v24})
                    X["g24"] = {"v": v24}

                for p in pairs:
                    X = pctx[p]
                    sl = X["sl"]
                    m0v = slb[sl][:, C_NT:C_NT + 512].rearrange(
                        "p (g c) -> p g c", c=128)[:, 0::2, :]
                    r0v = slb[sl][:, C_RA:C_RA + 256].rearrange(
                        "p (g c) -> p g c", c=128)
                    idv = ident_bf[:, :].rearrange(
                        "p (g c) -> p g c", g=1).broadcast_to((128, 2, 128))
                    vR = vop(gmax(X["g13"], X["slg"]),
                             lambda vv, m0v=m0v, r0v=r0v, idv=idv:
                             vv.tensor_tensor(r0v, m0v, idv, ALU.add))
                    X["gR"] = {"v": vR}

                for lev in range(1, 7):
                    dstc = C_SQA if lev % 2 == 1 else C_SQB
                    for p in pairs:
                        X = pctx[p]
                        sl = X["sl"]
                        if lev == 1:
                            Psrc = [sl_ap(sl, C_PL0 + hx * 128, 128)
                                    for hx in (0, 1)]
                            Msrc = [sl_ap(sl, C_NT + hx * 256, 128)
                                    for hx in (0, 1)]
                            gsrc = gmax(X["gN"], X["g13"])
                        else:
                            sc_ = C_SQA if (lev - 1) % 2 == 1 else C_SQB
                            Psrc = [sl_ap(sl, sc_ + hx * 128, 128)
                                    for hx in (0, 1)]
                            Msrc = [sl_ap(sl, sc_ + 256 + hx * 128, 128)
                                    for hx in (0, 1)]
                            gsrc = X["gPM"]
                        ps, g = acqS()

                        def fsq(tt, Psrc=Psrc, Msrc=Msrc, ps=ps):
                            for hx in (0, 1):
                                tt.matmul(ps[:, hx * 128:(hx + 1) * 128],
                                          Msrc[hx], Psrc[hx],
                                          start=True, stop=True)
                            for hx in (0, 1):
                                ins = tt.matmul(
                                    ps[:, 256 + hx * 128:384 + hx * 128],
                                    Psrc[hx], Msrc[hx],
                                    start=True, stop=True)
                            return ins
                        pe_sq = top(gmax(g, gsrc, X["slg"]), fsq)
                        wv = gmax({"pe": pe_sq}, X["slg"])
                        if (p + lev) % 2 == 0:
                            gPM = {"v": vop(wv, lambda vv, sl=sl, ps=ps,
                                            dstc=dstc: vv.tensor_copy(
                                                sl_ap(sl, dstc, 512),
                                                ps[:, 0:512]))}
                        else:
                            gPM = {"sc": cop(wv, lambda cc, sl=sl, ps=ps,
                                             dstc=dstc: cc.copy(
                                                 sl_ap(sl, dstc, 512),
                                                 ps[:, 0:512]))}
                        sprot.release(gPM)
                        X["gPM"] = gPM
                    rdst = C_RB if lev % 2 == 1 else C_RA
                    rsrc = C_RA if lev % 2 == 1 else C_RB
                    for p in pairs:
                        X = pctx[p]
                        sl = X["sl"]
                        ps, g = acqS()

                        def fpr(tt, X=X, sl=sl, ps=ps, dstc=dstc,
                                rsrc=rsrc):
                            for hx in (0, 1):
                                tt.matmul(ps[:, hx * 128:(hx + 1) * 128],
                                          ident_bf[:, :],
                                          sl_ap(sl, rsrc + hx * 128, 128),
                                          start=True, stop=False)
                                ins = tt.matmul(
                                    ps[:, hx * 128:(hx + 1) * 128],
                                    sl_ap(sl, dstc + hx * 128, 128),
                                    sl_ap(sl, rsrc + hx * 128, 128),
                                    start=False, stop=True)
                            return ins
                        pe_pr = top(gmax(g, X["gPM"], X["gR"], X["slg"]),
                                    fpr)
                        wv = gmax({"pe": pe_pr}, X["slg"])
                        if (p + lev) % 2 == 1:
                            gR = {"v": vop(wv, lambda vv, sl=sl, ps=ps,
                                           rdst=rdst: vv.tensor_copy(
                                               sl_ap(sl, rdst, 256),
                                               ps[:, 0:256]))}
                        else:
                            gR = {"sc": cop(wv, lambda cc, sl=sl, ps=ps,
                                            rdst=rdst: cc.copy(
                                                sl_ap(sl, rdst, 256),
                                                ps[:, 0:256]))}
                        sprot.release(gR)
                        X["gR"] = gR

                for p in pairs:
                    X = pctx[p]
                    sl = X["sl"]
                    ps, g = acqS()

                    def fw0(tt, X=X, sl=sl, ps=ps):
                        for hx in (0, 1):
                            tt.matmul(ps[:, hx * 512:hx * 512 + 64],
                                      X["Af"][hx], X["STp"][hx],
                                      start=True, stop=False)
                            ins = tt.matmul(
                                ps[:, hx * 512:hx * 512 + 64],
                                sl_ap(sl, C_T24 + hx * 256, 128),
                                X["Vt"][hx], start=False, stop=True)
                        return ins
                    pe_w0 = top(gmax(g, X["g24"], X["slg"], stv[p],
                                     pend_i), fw0)
                    gW0 = {"v": vop(gmax({"pe": pe_w0}, X["slg"]),
                                    lambda vv, sl=sl, ps=ps:
                                    vv.tensor_copy(
                                        b2(sl_ap(sl, C_W0, 128), 64),
                                        b2(ps[:, 0:1024], 64)))}
                    sprot.release(gW0)
                    ps2, g2 = acqS()

                    def fx(tt, sl=sl, ps2=ps2):
                        for hx in (0, 1):
                            ins = tt.matmul(
                                ps2[:, hx * 512:hx * 512 + 64],
                                sl_ap(sl, C_RA + hx * 128, 128),
                                sl_ap(sl, C_W0 + hx * 64, 64),
                                start=True, stop=True)
                        return ins
                    pe_x = top(gmax(g2, gW0, X["gR"], X["slg"]), fx)
                    gX = {"sc": cop(gmax({"pe": pe_x}, X["slg"]),
                                    lambda cc, sl=sl, ps2=ps2:
                                    cc.copy(b2(sl_ap(sl, C_XC, 128), 64),
                                            b2(ps2[:, 0:1024], 64)))}
                    sprot.release(gX)
                    X["gX"] = gX
                    X["pe_w0"] = pe_w0

                for p in pairs:
                    X = pctx[p]
                    sl = X["sl"]
                    ps, g = acqS()

                    def fy(tt, X=X, sl=sl, ps=ps):
                        for hx in (0, 1):
                            tt.matmul(ps[:, hx * 512:hx * 512 + 64],
                                      X["Qf"][hx], X["STp"][hx],
                                      start=True, stop=False)
                            tt.matmul(ps[:, hx * 512:hx * 512 + 64],
                                      sl_ap(sl, C_NT + hx * 256 + 128, 128),
                                      sl_ap(sl, C_XC + hx * 64, 64),
                                      start=False, stop=False)
                            ins = tt.matmul(
                                ps[:, hx * 512:hx * 512 + 64],
                                sl_ap(sl, C_T24 + hx * 256 + 128, 128),
                                X["Vt"][hx], start=False, stop=True)
                        return ins
                    pe_y = top(gmax(g, X["gX"], X["g13"], X["g24"],
                                    stv[X["p"]], X["slg"]), fy)
                    gY = {"v": vop(gmax({"pe": pe_y}, X["slg"],
                                        Send.get(i - 1)),
                                   lambda vv, ps=ps, p=p: vv.tensor_copy(
                                       b2(ytm[:, p * 128:(p + 1) * 128],
                                          64),
                                       b2(ps[:, 0:1024], 64)))}
                    sprot.release(gY)
                    X["gY"] = gY
                    X["pe_y"] = pe_y
                    gy_all_blk.setdefault(i, []).append(gY)

                for p in pairs:
                    X = pctx[p]
                    sl = X["sl"]
                    vdd = vop(gmax({"sc": CC[i]["c_dl"]}, X["gY"],
                                   X["slg"]),
                              lambda vv, sl=sl, p=p, par=par:
                              vv.tensor_scalar(
                                  sl_ap(sl, C_DD, 64), ident2x[:, :],
                                  dlfI[:, p:p + 1], None, ALU.mult))
                    ps, g = acqS()

                    def fs6(tt, X=X, sl=sl, ps=ps):
                        for hx, pb in ((0, 0), (1, 64)):
                            tt.matmul(ps[pb:pb + 64,
                                         hx * 512:hx * 512 + 64],
                                      slb[sl][pb:pb + 64, C_DD:C_DD + 64],
                                      X["STp"][hx], start=True, stop=False)
                            tt.matmul(ps[pb:pb + 64,
                                         hx * 512:hx * 512 + 64],
                                      X["Bh"][hx],
                                      sl_ap(sl, C_XC + hx * 64, 64),
                                      start=False, stop=False)
                            ins = tt.matmul(ps[pb:pb + 64,
                                            hx * 512:hx * 512 + 64],
                                            X["Kh"][hx], X["Vt"][hx],
                                            start=False, stop=True)
                        return ins
                    pe_st = top(gmax(g, {"v": vdd}, X["gX"],
                                     {"pe": X["pe_y"]}, pend_i, X["slg"]),
                                fs6)

                    def fgs(cc, ps=ps, p=p):
                        cc.copy(STb[0:64, p * 64:(p + 1) * 64],
                                ps[0:64, 0:64])
                        return cc.copy(STb[64:128, p * 64:(p + 1) * 64],
                                       ps[64:128, 512:576])
                    gS = {"sc": cop({"pe": pe_st}, fgs)}
                    sprot.release(gS)
                    stv[p] = gS
                    slot_busy[X["sl"]] = {"pe": pe_st}
                    last_state_blk[i] = pe_st

            def S_tail(i):
                par = i % 2
                gy_all = gmax(*gy_all_blk[i])
                sp4 = Sr4end.get(i - 1, {})
                y3 = ytm.rearrange("p (g i2) -> p g i2", g=16)
                c_ys = cop(gmax(gy_all, {"pe": last_state_blk[i]}, sp4),
                           lambda cc: cc.activation(sbAs[:, :DM], ytm[:, :],
                                                    AF.Square))
                sbA3 = sbAs[:, :DM].rearrange("p (g i2) -> p g i2", g=16)
                musum = scrS[:, 0:16]
                sqsum = scrS[:, 16:32]
                muv = scrS[:, 32:48]
                mus2 = scrS[:, 48:64]
                varv = scrS[:, 64:80]
                rstd = scrS[:, 80:96]
                vgs = vop(gmax({"sc": c_ys}, gy_all, sp4),
                          lambda vv: vv.reduce_sum(
                              musum.rearrange("p (g o) -> p g o", o=1), y3,
                              axis=mybir.AxisListType.X))
                vgs2 = vop({"v": vgs}, lambda vv: vv.reduce_sum(
                    sqsum.rearrange("p (g o) -> p g o", o=1), sbA3,
                    axis=mybir.AxisListType.X))
                vgs = vop({"v": vgs2}, lambda vv: vv.tensor_scalar_mul(
                    muv, musum, 1.0 / HS))
                vgs = vop({"v": vgs}, lambda vv: vv.tensor_mul(
                    mus2, muv, muv))
                vgs = vop({"v": vgs}, lambda vv: vv.scalar_tensor_tensor(
                    varv, sqsum, 1.0 / HS, mus2, ALU.mult, ALU.subtract))
                c_rs = cop({"v": vgs}, lambda cc: cc.activation(
                    rstd, varv, AF.Ln,
                    bias=consts[:, SC_GNEPS:SC_GNEPS + 1]))
                c_rs = cop({"sc": c_rs}, lambda cc: cc.activation(
                    rstd, rstd, AF.Exp, scale=-0.5))
                mu3 = muv.rearrange("p (g o) -> p g o", o=1).broadcast_to(
                    (128, 16, HS))
                rs3 = rstd.rearrange("p (g o) -> p g o", o=1).broadcast_to(
                    (128, 16, HS))
                vyn = vop({"sc": c_rs, "v": vgs},
                          lambda vv: vv.tensor_tensor(
                    sbBs[:, :DM].rearrange("p (g i2) -> p g i2", g=16), y3,
                    mu3, ALU.subtract))
                vyn = vop({"v": vyn}, lambda vv: vv.tensor_tensor(
                    sbBs[:, :DM].rearrange("p (g i2) -> p g i2", g=16),
                    sbBs[:, :DM].rearrange("p (g i2) -> p g i2", g=16),
                    rs3, ALU.mult))
                vyn = vop({"v": vyn}, lambda vv: vv.tensor_tensor(
                    sbBs[:, :DM], sbBs[:, :DM], lnw_tm[:, :], ALU.mult))
                vyn = vop({"v": vyn}, lambda vv: vv.tensor_tensor(
                    sbBs[:, :DM], sbBs[:, :DM], lnb_tm[:, :], ALU.add))
                prk = pop(gmax({"v": vgs2}, Pend[i], sp4),
                          lambda pp, par=par: pp.tensor_tensor(
                              sbAs[:, :], rTs[par][:, :], kfs[par][:, :],
                              ALU.mult))
                prk = pop({"pl": prk}, lambda pp: pp.tensor_tensor(
                    v3(sbAs), v3(sbAs), cv3(CV_FAAAA), ALU.mult))
                ps_rk, g_rk = acqS()
                g_rk["pl"] = max(g_rk.get("pl", 0), prk)

                def frk(tt, ps_rk=ps_rk):
                    for m in range(NCH):
                        ins = tt.matmul(ps_rk[0:16, 0:W],
                                        hselt_bf[:, m * 16:(m + 1) * 16],
                                        sbAs[:, ts(m)], start=(m == 0),
                                        stop=(m == NCH - 1))
                    return ins
                pe_rk = top(g_rk, frk)
                rksb = scrS.bitcast(BF16)[0:16, 192:320]
                rktm = scrS.bitcast(BF16)[:, 320:336]
                v_rk2 = vop(gmax({"pe": pe_rk}, sp4),
                            lambda vv, ps_rk=ps_rk:
                            vv.tensor_copy(rksb[:, :], ps_rk[0:16, 0:W]))
                sprot.release({"v": v_rk2})
                ps_rt, g_rt = acqT()
                g_rt["v"] = max(g_rt.get("v", 0), v_rk2)
                pe_rt = top(g_rt, lambda tt, ps_rt=ps_rt: tt.transpose(
                    ps_rt[:, 0:16], rksb[:, :], ident_bf[0:16, 0:16]))
                v_rt = vop({"pe": pe_rt}, lambda vv, ps_rt=ps_rt:
                           vv.tensor_copy(rktm[:, :], ps_rt[:, 0:16]))
                ptrot.release({"v": v_rt})
                rk3 = rktm.rearrange("p (g o) -> p g o", o=1).broadcast_to(
                    (128, 16, HS))
                v3tm = vtmI.rearrange("p (g i2) -> p g i2", g=16)
                vrkv = vop({"v": v_rt}, lambda vv: vv.tensor_tensor(
                    sbAs[:, :DM].rearrange("p (g i2) -> p g i2", g=16),
                    v3tm, rk3, ALU.mult))
                vfin = vop({"v": vrkv}, lambda vv: vv.tensor_tensor(
                    sbBs[:, :DM], sbBs[:, :DM], sbAs[:, :DM], ALU.add))
                vyg = vop(gmax({"v": max(vfin, CC[i]["v_gt"])}),
                          lambda vv, par=par: vv.tensor_tensor(
                              sbBs[:, :DM], sbBs[:, :DM], gtms[par][:, :],
                              ALU.mult))
                ps_yf, g_yf = acqT()
                g_yf["v"] = max(g_yf.get("v", 0), vyg)

                def fyf(tt, ps_yf=ps_yf):
                    for m in range(NCH):
                        ins = tt.transpose(ps_yf[:, m * W:(m + 1) * W],
                                           sbBs[:, ts(m)], ident_bf[:])
                    return ins
                pe_yf = top(g_yf, fyf)
                v_yf = vop(gmax({"pe": pe_yf}, sp4),
                           lambda vv, ps_yf=ps_yf:
                           vv.tensor_copy(ygfm[:, :], ps_yf[:, :1024]))
                ptrot.release({"v": v_yf})

                tsl = CC[i]["tsl"]
                pe_wo = 0
                for half in (0, 1):
                    load_ws(stB, "woT", half)
                    wbuf, dwo = take_ws(stB)
                    ps_wo, g_wo = acqS()
                    g_wo["dB"] = max(g_wo.get("dB", 0), dwo)
                    g_wo["v"] = max(g_wo.get("v", 0), v_yf)

                    def fwo2(tt, ps_wo=ps_wo, wbuf=wbuf):
                        for mo in range(4):
                            for kc in range(NCH):
                                ins = tt.matmul(
                                    ps_wo[:, mo * W:(mo + 1) * W],
                                    wbuf[:, kc * 512 + mo * 128:
                                        kc * 512 + (mo + 1) * 128],
                                    ygfm[:, ts(kc)], start=(kc == 0),
                                    stop=(kc == NCH - 1))
                        return ins
                    pe_wo = top(g_wo, fwo2)
                    done_ws(stB, pe_wo)
                    if half == 1:
                        Swo1pe[i] = pe_wo
                    v_ra = 0
                    for mo in range(4):
                        m = half * 4 + mo
                        v_ra = vop({"pe": pe_wo, "v": v_ra},
                                   lambda vv, m=m, mo=mo, ps_wo=ps_wo,
                                   tsl=tsl: vv.tensor_tensor(
                                       R[m][:, tsl], R[m][:, tsl],
                                       ps_wo[:, mo * W:(mo + 1) * W],
                                       ALU.add))
                    sprot.release({"v": v_ra})
                Sr4end[i] = snap()

            maxsegs = (4 * (NBK + 1) if pers is None
                       else int(round(4 * pers)))
            nseg_done = 0
            for per in range(NBK + 1):
                for seg in range(4):
                    if nseg_done >= maxsegs:
                        break
                    nseg_done += 1
                    lastseg = (nseg_done == maxsegs)
                    if per > 0 and not (lastseg and lastskip == "s"):
                        if seg < 3:
                            S_solve_round(per - 1, seg)
                            if seg == 2:
                                Sr3pe[per - 1] = b.n("pe")
                                Sr3x[per - 1] = snap()
                        else:
                            S_tail(per - 1)
                            Send[per - 1] = snap()
                    if per < NBK and not (lastseg and lastskip == "p"):
                        [P_c1, P_c2, P_c3, P_c4][seg](per)
                if per < NBK:
                    Pend[per] = snap()

            _da, _db, _ds = b.n("dA"), b.n("dB"), b.n("dS")

            def fdrain(s, _da=_da, _db=_db, _ds=_ds):
                s.wait_ge(b.h("dA"), _da)
                s.wait_ge(b.h("dB"), _db)
                s.wait_ge(b.h("dS"), _ds)
            b.g(fdrain)
            b.barrier()

    emit_ffn(1)
    dump("ffn1", [R[m][:, :T] for m in range(NCH)])
    if stop_after not in ("ln1", "ffn1"):
        emit_attn_pipe()
        dump("attn_x2", [R[m][:, :T] for m in range(NCH)])
    if stop_after not in ("ln1", "ffn1", "attn"):
        emit_ffn(2)

    tgt = b.r("dY", 16 * NCH)

    def fstore(s, tgt=tgt):
        for m in range(NCH):
            s.dma_start(out_d[ts(m), :], R[m][:, :T]).then_inc(b.h("dY"), 16)
        s.wait_ge(b.h("dY"), tgt)
    b.y(fstore)
    b.barrier()
    return nc, d, dbg_outs


def _consts_np(T):
    NTC = T // 128
    c = {}
    c["ident_bf"] = np.eye(128, dtype=ml_dtypes.bfloat16)
    zid = np.zeros((128, 256), ml_dtypes.bfloat16)
    zid[:, 128:] = np.eye(128, dtype=ml_dtypes.bfloat16)
    c["zid_bf"] = zid
    c["ones1_bf"] = np.ones((1, 128), ml_dtypes.bfloat16)
    c["ones1_f32"] = np.ones((1, 128), np.float32)
    c["onescol_bf"] = np.ones((128, 1), ml_dtypes.bfloat16)
    c["mask_sl"] = np.tril(np.ones((128, 128), np.float32), -1)
    mt = np.zeros((128, 256), np.float32)
    mt[:, :128] = np.triu(np.ones((128, 128), np.float32), 1)
    mt[:, 128:] = np.triu(np.ones((128, 128), np.float32), 0)
    c["mask_t"] = mt
    c["mask_sl2"] = np.concatenate([c["mask_sl"], c["mask_sl"]],
                                   axis=1).astype(ml_dtypes.bfloat16)
    c["mask_t2"] = np.concatenate([mt, mt], axis=1).astype(ml_dtypes.bfloat16)
    i2 = np.zeros((128, 64), ml_dtypes.bfloat16)
    i2[0:64, :] = np.eye(64, dtype=ml_dtypes.bfloat16)
    i2[64:128, :] = np.eye(64, dtype=ml_dtypes.bfloat16)
    c["ident2x"] = i2
    hs = np.zeros((16, NCH * 128), np.float32)
    hst = np.zeros((128, NCH * 16), np.float32)
    for m in range(NCH):
        for j in range(128):
            h = 2 * m + (j >= 64)
            hs[h, m * 128 + j] = 1.0
            hst[j, m * 16 + h] = 1.0
    c["hsel"] = hs
    c["hselt_bf"] = hst.astype(ml_dtypes.bfloat16)
    sc = np.zeros((128, NSC), np.float32)
    sc[:, SC_LNEPS] = LN_EPS
    sc[:, SC_GNEPS] = GN_EPS
    sc[:, SC_KNEPS] = 1e-12
    sc[:, SC_EPS30] = 1e-30
    c["consts"] = sc
    return c


def prep_inputs(inputs, T):
    f32 = np.float32

    def A(k):
        return np.ascontiguousarray(np.asarray(inputs[k], f32))
    m = dict(_consts_np(T))
    cv = np.zeros((DM, NVC), f32)
    cv[:, CV_TMX] = A("L0_attn_time_maa_x")
    cv[:, CV_TD] = A("L0_attn_time_decay")
    cv[:, CV_FAAAA] = A("L0_attn_time_faaaa")
    cv[:, CV_AAAAA] = A("L0_attn_time_aaaaa")
    cv[:, CV_MISCA] = A("L0_attn_time_misc_a")
    cv[:, CV_MISCK] = A("L0_attn_time_misc_k")
    cv[:, CV_LNW] = A("L0_attn_ln_w")
    cv[:, CV_LNB] = A("L0_attn_ln_b")
    tm = A("L0_attn_time_maa").reshape(4, DM)
    for g in range(4):
        cv[:, CV_TMAA0 + g] = tm[g]
    m["chanvec"] = cv
    m["lnw_tm"] = np.ascontiguousarray(
        np.broadcast_to(A("L0_attn_ln_w")[None, :], (128, DM)))
    m["lnb_tm"] = np.ascontiguousarray(
        np.broadcast_to(A("L0_attn_ln_b")[None, :], (128, DM)))
    m["wg"] = A("L0_ffn_Wg")
    bf = ml_dtypes.bfloat16
    m["ws1"] = A("L0_ffn_Ws1").astype(bf)
    m["ws3"] = A("L0_ffn_Ws3").astype(bf)
    m["ws2"] = A("L0_ffn_Ws2").astype(bf)
    m["we1"] = A("L0_ffn_We1").astype(bf)
    m["we3"] = A("L0_ffn_We3").astype(bf)
    m["we2"] = A("L0_ffn_We2").astype(bf)
    for nm in ("wrT", "wkT", "wvT", "woT"):
        srck = {"wrT": "L0_attn_Wr", "wkT": "L0_attn_Wk",
                "wvT": "L0_attn_Wv", "woT": "L0_attn_Wo"}[nm]
        m[nm] = np.ascontiguousarray(A(srck).T).astype(ml_dtypes.bfloat16)
    for nm, src in [("maa_w1T", "L0_attn_maa_w1"),
                    ("decay_w1T", "L0_attn_decay_w1"),
                    ("kkk_w1T", "L0_attn_kkk_w1"),
                    ("aaa_w1T", "L0_attn_aaa_w1"),
                    ("ma_w1T", "L0_attn_ma_w1"),
                    ("mk_w1T", "L0_attn_mk_w1"),
                    ("gate_w1T", "L0_attn_gate_w1")]:
        m[nm] = np.ascontiguousarray(A(src).T)
    for nm, src in [("maa_w2", "L0_attn_maa_w2"),
                    ("decay_w2", "L0_attn_decay_w2"),
                    ("kkk_w2", "L0_attn_kkk_w2"),
                    ("aaa_w2", "L0_attn_aaa_w2"),
                    ("ma_w2", "L0_attn_ma_w2"),
                    ("mk_w2", "L0_attn_mk_w2"),
                    ("gate_w2", "L0_attn_gate_w2")]:
        m[nm] = A(src)
    return m


def kernel(**inputs):
    x = np.asarray(inputs["L0_attn_x"], np.float32)
    Bn, T, _ = x.shape
    shared = prep_inputs(inputs, T)
    nc, d, dbg_outs = build_program(T)
    in_maps = []
    for c in range(Bn):
        im = dict(shared)
        im["x_fm"] = np.ascontiguousarray(x[c].T)
        in_maps.append(im)
    res = run_bass_kernel_spmd(nc, in_maps, core_ids=list(range(Bn)))
    out = np.stack([np.ascontiguousarray(r["out_fm"].T)
                    for r in res.results])
    return out.astype(np.float32)
